# version 31
# baseline (speedup 1.0000x reference)
import numpy as np
import ml_dtypes
import concourse.bass as bass
import concourse.mybir as mybir
from concourse.bass_utils import run_bass_kernel_spmd

F32 = mybir.dt.float32
BF16 = mybir.dt.bfloat16
AF = mybir.ActivationFunctionType
ALU = mybir.AluOpType
AX = mybir.AxisListType
BFnp = ml_dtypes.bfloat16

MAGIC = float(np.float32(1.5 * 2**23))
DBG_QUANT_ALL = False
DBG_TRUNC = None
DBG_RACE = False
DBG_NOMASK = False
H = 1024
INNER = 4096
CH = 2048
NT = 18
NV = 16
TPC = 2048
NBLK = 8
GA = 6
ZR = 4


class Sem:
    def __init__(self, nc, name):
        self.h = nc.alloc_semaphore(name)
        self.n = 0


class Prog:
    ENGS = ("sync", "gpsimd", "tensor", "vector", "scalar")

    def __init__(self, nc):
        self.nc = nc
        self.ops = {e: [] for e in self.ENGS}
        self.sems = {}

    def sem(self, name):
        if name not in self.sems:
            self.sems[name] = Sem(self.nc, name)
        return self.sems[name]

    def op(self, eng, fn, waits=(), inc=None, dma=False):
        amt = 16 if dma else 1
        incs = []
        self.gidx = getattr(self, 'gidx', 0) + 1
        ws = [(s, v) for w in waits if w is not None for (s, v) in [w] if v > 0]
        for s, v in ws:
            assert v <= s.n, (f"wait on {s.h.name} for {v} but only {s.n} "
                              f"produced so far (eng {eng})")
        if inc is not None:
            for it in (inc if isinstance(inc, list) else [inc]):
                if isinstance(it, tuple):
                    s, a = it
                else:
                    s, a = it, amt
                s.n += a
                incs.append((s, a, s.n))
        self.ops[eng].append((fn, ws, incs, self.gidx))
        return incs[0][2] if incs else None

    def carrier(self, eng_name, eng):
        if eng_name in ("vector", "scalar", "gpsimd"):
            ent = self.carriers[eng_name]
            buf, k, cap = ent
            assert k < cap, f"carrier slots exhausted for {eng_name}"
            ent[1] = k + 1
            p, c = (k % 4) * 32, k // 4
            if eng_name == "scalar":
                return eng.memzero(buf[p:p + 1, c:c + 1])
            return eng.memset(buf[p:p + 1, c:c + 1], 0.0)
        return eng.nop()

    def emit(self, eng_name, eng):
        ops = self.ops[eng_name]
        if DBG_TRUNC is not None:
            if isinstance(DBG_TRUNC, float):
                ops = ops[:max(1, int(len(ops) * DBG_TRUNC))]
            else:
                ops = [o for o in ops if o[3] <= DBG_TRUNC]
        for fn, ws, incs, _gi in ops:
            for s, v in ws:
                eng.wait_ge(s.h, v)
            ins = fn(eng)
            if incs:
                ins.then_inc(incs[0][0].h, incs[0][1])
                for s, a, _ in incs[1:]:
                    self.carrier(eng_name, eng).then_inc(s.h, a)


def _dft_consts():
    N, K, OUT = 512, 256, 256
    t = np.arange(N)
    C = np.zeros((N, N), np.float64)
    for fr in range(N):
        if fr <= 256:
            C[:, fr] = np.cos(2 * np.pi * fr * t / N)
        else:
            C[:, fr] = np.sin(2 * np.pi * (fr - 256) * t / N)
    tau = np.arange(OUT) + (N - OUT)
    iD = np.zeros((N, OUT), np.float64)
    for fr in range(N):
        if fr <= 256:
            c = 1.0 if fr in (0, 256) else 2.0
            iD[fr] = c / N * np.cos(2 * np.pi * fr * tau / N)
        else:
            iD[fr] = 2.0 / N * np.sin(2 * np.pi * (fr - 256) * tau / N)
    j = np.arange(K)
    jr = (K - 1 - j)[:, None]
    Wc = np.cos(2 * np.pi * np.arange(257)[None, :] * jr / N)
    Ws = np.sin(2 * np.pi * np.arange(256)[None, :] * jr / N)
    U0 = Wc[:, 0:128].copy()
    U1 = Wc[:, 128:256].copy()
    V0 = np.zeros((K, 128)); V0[:, 1:] = Ws[:, 1:128]
    V1 = Ws[:, 128:256].copy()
    Nt = np.zeros((K, 128)); Nt[:, 0] = Wc[:, 256]; Nt[:, 1:] = Wc[:, 1:128]
    D2 = np.stack([U0, U1, V0, V1, Nt], axis=1)
    Cd = np.ascontiguousarray(
        C.reshape(4, 128, 4, 128).transpose(1, 0, 2, 3)).astype(BFnp)
    iDd = np.ascontiguousarray(
        iD.reshape(4, 128, 2, 128).transpose(1, 0, 2, 3)).astype(BFnp)
    D2d = np.ascontiguousarray(
        D2.reshape(2, 128, 5, 128).transpose(1, 0, 2, 3)).astype(BFnp)
    return Cd, iDd, D2d


def build_program(debug_taps=False):
    nc = bass.Bass("TRN2", target_bir_lowering=False,
                   dynamic_dma_scratch_size=16384,
                   detect_race_conditions=DBG_RACE)

    def din(name, shp, dt=F32):
        return nc.dram_tensor(name, list(shp), dt, kind="ExternalInput").ap()

    xh = din("xh", [NT * 128, H])
    zmask = din("zmask", [128, 1])
    wgT = din("wgT", [H, INNER])
    wuT = din("wuT", [H, INNER])
    wdT = din("wdT", [INNER, H])
    w_in = din("w_in", [CH, H])
    w_out = din("w_out", [H, CH])
    wa = din("wa", [H, 256])
    wb = din("wb", [H, 256])
    b_in = din("b_in", [1, CH])
    b_a = din("b_a", [1, H])
    b_b = din("b_b", [1, H])
    b_out = din("b_out", [1, H])
    wscl = din("wscl", [3, 2, 128, 2048])
    Cd = din("Cd", [128, 4, 4, 128], BF16)
    iDd = din("iDd", [128, 4, 2, 128], BF16)
    D2d = din("D2d", [128, 2, 5, 128], BF16)
    identd = din("identd", [128, 128], BF16)

    out = nc.dram_tensor("out", [TPC, H], F32, kind="ExternalOutput").ap()
    taps_out = {}
    if debug_taps:
        for nm, shp in (("dbg_xmid", [NT * 128, H]), ("dbg_z", [NT * 128, CH]),
                        ("dbg_x3", [NV * 128, H]),):
            taps_out[nm] = nc.dram_tensor(nm, shp, F32,
                                          kind="ExternalOutput").ap()

    scr_g = nc.dram_tensor("scr_g", [8, 128, 8, 512], BF16, kind="Internal").ap()
    scr_u = nc.dram_tensor("scr_u", [8, 128, 8, 512], BF16, kind="Internal").ap()
    scr_d = nc.dram_tensor("scr_d", [2, 4, 128, 8, 512], BF16,
                           kind="Internal").ap()
    cc_in = nc.dram_tensor("cc_in", [128, 3], F32, kind="Internal").ap()
    cc_out = nc.dram_tensor("cc_out", [128, 3], F32, kind="Internal",
                            addr_space="Shared").ap()

    P = Prog(nc)
    S = P.sem

    from contextlib import ExitStack
    with ExitStack() as top:
        def sb(ctx, name, shp, dt):
            return ctx.enter_context(nc.sbuf_tensor(name, shp, dt))

        xs = sb(top, "xs", [128, NT, H], BF16)
        dumv = sb(top, "dumv", [128, 256], F32)
        dums = sb(top, "dums", [128, 128], F32)
        dumg = sb(top, "dumg", [128, 128], F32)
        P.carriers = {"vector": [dumv, 0, 1024], "scalar": [dums, 0, 512],
                      "gpsimd": [dumg, 0, 512]}
        ident = sb(top, "ident", [128, 128], BF16)
        ones1 = sb(top, "ones1", [128, 128], BF16)
        ones128f = sb(top, "ones128f", [128, 128], F32)
        magicrow = sb(top, "magicrow", [128, 1], F32)
        hp = sb(top, "hp", [128, 1], F32)
        hn = sb(top, "hn", [128, 1], F32)
        zmk = sb(top, "zmk", [128, 1], F32)
        wsum = sb(top, "wsum", [128, 3], F32)
        acol = sb(top, "acol", [128, 3, 1], F32)
        nwcol = sb(top, "nwcol", [128, 3], F32)
        swv = sb(top, "swv", [128, 3], F32)
        gww = sb(top, "gww", [128, 3], F32)
        cols = {}
        for nm in ("s1", "s2", "mx", "mn", "m", "ve", "rq", "r", "mab", "sp",
                   "nb", "q", "cg", "cu", "mt", "f", "cd", "t0", "t1"):
            cols[nm] = sb(top, "col_" + nm, [128, NT], F32)
        vmax = sb(top, "vmax", [128, GA, 8], F32)
        ppool = top.enter_context(nc.psum_tensor("ppool", [128, 7, 512], F32))
        ptr = top.enter_context(nc.psum_tensor("ptr", [128, 8, 128], BF16))

        psl = [S(f"psl{i}") for i in range(7)]
        puse = [0] * 7
        ptrsl = S("pslT")
        ptruse = [0]

        def pacq(i):
            w = (psl[i], puse[i]) if puse[i] else None
            puse[i] += 1
            return w

        def ptracq():
            w = (ptrsl, ptruse[0]) if ptruse[0] else None
            ptruse[0] += 1
            return w

        sx = S("sx")
        sinit = S("sinit")
        smem = S("smem")
        szmk = S("szmk")
        sident = S("sident")
        P.op("gpsimd", lambda g: g.memset(ones1[:, :], 1.0))
        P.op("gpsimd", lambda g: g.memset(ones128f[:, :], 1.0))
        P.op("gpsimd", lambda g: g.memset(magicrow[:, :], MAGIC))
        P.op("gpsimd", lambda g: g.memset(hp[:, :], 0.5))
        P.op("gpsimd", lambda g: g.memset(hn[:, :], -0.5), inc=smem)
        v_mem = smem.n
        P.op("gpsimd", lambda g: g.dma_start(zmk[:, :], zmask), inc=szmk, dma=True)
        P.op("gpsimd", lambda g: g.dma_start(ident[:, :], identd), inc=sident,
             dma=True)
        v_init = sident.n
        xh_t = xh.rearrange("(t p) h -> t p h", p=128)
        for t in range(NT):
            P.op("gpsimd", lambda g, t=t: g.dma_start(xs[:, t, :], xh_t[t]),
                 inc=sx, dma=True)

        with ExitStack() as actx:
            wfp = sb(actx, "wfp", [128, 2, 4, 512], F32)
            wsg = sb(actx, "wsg", [128, 2, 4, 512], BF16)
            bufs = dict(
                uT=sb(actx, "uT", [128, GA, 8, 128], BF16),
                u_f=sb(actx, "u_f", [128, 1, H], F32),
                u_tm=sb(actx, "u_tm", [128, GA, H], BF16),
                vbuf=sb(actx, "vbuf", [128, GA, INNER], BF16),
                t2=sb(actx, "t2", [128, 2, 512], F32),
                vq=sb(actx, "vq", [128, 1, INNER], BF16),
                wgup=sb(actx, "wgup", [128, 2, 2, 8, 512], BF16),
                wdb=sb(actx, "wdb", [128, 2, 8, 512], BF16),
                silu=sb(actx, "silu", [128, 2, 512], F32),
                outf=None,
            )
            bufs["vqT"] = nc.alloc_sbuf_tensor_at(
                "vqT", [128, GA, 32, 128], BF16,
                offset=nc.lookup_mloc(bufs["vbuf"]).addr)

            swfp = [S("swf0"), S("swf1")]
            sabs = S("sabs")
            sacc = S("sacc")
            sscale = S("sscale")
            s_ccl = S("s_ccl")
            s_ccd = S("s_ccd")
            kglob = 0
            for wi in range(3):
                for k in range(2):
                    par = kglob % 2
                    w = (sabs, kglob - 1) if kglob >= 2 else None
                    P.op("sync", lambda e, wi=wi, k=k, par=par: e.dma_start(
                        wfp[:, par, :, :].rearrange("p c o -> p (c o)"),
                        wscl[wi, k]),
                        waits=[w], inc=swfp[par], dma=True)
                    kglob += 1
                    P.op("scalar", lambda e, wi=wi, par=par: e.activation(
                        wsg[:, 0, :, :], wfp[:, par, :, :], AF.Abs,
                        accum_out=acol[:, wi, :]),
                        waits=[(swfp[par], swfp[par].n), (sabs, sabs.n),
                               (sacc, sacc.n)],
                        inc=sabs)
                    if k == 0:
                        P.op("vector", lambda e, wi=wi: e.tensor_copy(
                            wsum[:, wi:wi + 1], acol[:, wi, :]),
                            waits=[(sabs, kglob), (sacc, sacc.n)], inc=sacc)
                    else:
                        P.op("vector", lambda e, wi=wi: e.tensor_add(
                            wsum[:, wi:wi + 1], wsum[:, wi:wi + 1],
                            acol[:, wi, :]),
                            waits=[(sabs, kglob), (sacc, sacc.n)], inc=sacc)
            NSC = kglob
            P.op("gpsimd", lambda e: e.dma_start(cc_in, wsum[:, :]),
                 waits=[(sacc, NSC)], inc=s_ccl, dma=True)
            P.op("gpsimd", lambda e: e.collective_compute(
                "AllReduce", ALU.add,
                replica_groups=[[0, 1, 2, 3, 4, 5, 6, 7]],
                ins=[cc_in], outs=[cc_out]),
                waits=[(s_ccl, 16)], inc=s_ccd)
            P.op("gpsimd", lambda e: e.dma_start(wsum[:, :], cc_out),
                 waits=[(s_ccd, 1)], inc=s_ccl, dma=True)
            sred = S("sred")
            wsl = pacq(6)
            P.op("tensor", lambda e: e.matmul(ppool[:, 6, 0:3], ones128f[:, :],
                                              wsum[:, :], start=True, stop=True),
                 waits=[(s_ccl, 32), wsl, (smem, v_mem)], inc=sred)
            P.op("vector", lambda e: e.tensor_copy(nwcol[:, :], ppool[:, 6, 0:3]),
                 waits=[(sred, 1)], inc=[psl[6], sscale])
            P.op("vector", lambda e: e.tensor_scalar(
                nwcol[:, :], nwcol[:, :], 1.0 / (H * INNER), 1e-5,
                ALU.mult, ALU.max), waits=[(sscale, sscale.n)], inc=sscale)
            P.op("vector", lambda e: e.reciprocal(swv[:, :], nwcol[:, :]),
                 waits=[(sscale, sscale.n)], inc=sscale)
            P.op("vector", lambda e: e.tensor_scalar(
                gww[:, 0:2], nwcol[:, 0:2], 1.0 / 254.0, None, ALU.mult),
                waits=[(sscale, sscale.n)], inc=sscale)
            P.op("vector", lambda e: e.tensor_scalar(
                gww[:, 2:3], nwcol[:, 2:3], 0.25 / 127.0, None, ALU.mult),
                waits=[(sscale, sscale.n)], inc=sscale)
            v_scale = sscale.n

            sqfp = [S("sqf0"), S("sqf1")]
            ssgn = S("ssgn")
            sqdve = S("sqdve")
            sscrw = S("sscrw")
            qk = [0]

            sq_hist = []

            def quant_chunk(src_view, dst_sb, dst_scr, wi, extra_w=None):
                par = qk[0] % 2
                w = (ssgn, 2 * (qk[0] - 1)) if qk[0] >= 2 else (sabs, 6)
                P.op("sync", lambda e: e.dma_start(wfp[:, par, :, :], src_view),
                     waits=[w], inc=sqfp[par], dma=True)
                qk[0] += 1
                vqf = sqfp[par].n
                wsg_w = (sqdve, sq_hist[-2]) if len(sq_hist) >= 2 else None
                P.op("scalar", lambda e, par=par, wi=wi: e.activation(
                    wsg[:, par, :, :], wfp[:, par, :, :], AF.Sign,
                    bias=hn[:, :], scale=swv[:, wi:wi + 1]),
                    waits=[(sqfp[par], vqf), (sscale, v_scale), (smem, v_mem),
                           wsg_w, (sabs, 6)],
                    inc=ssgn)
                P.op("scalar", lambda e, par=par, wi=wi, dst_sb=dst_sb:
                     e.activation(dst_sb, wfp[:, par, :, :], AF.Sign,
                                  bias=hp[:, :], scale=swv[:, wi:wi + 1]),
                     waits=[extra_w, (sscrw, sscrw.n)], inc=ssgn)
                vs = ssgn.n
                P.op("vector", lambda e, dst_sb=dst_sb, par=par:
                     e.tensor_add(dst_sb, dst_sb, wsg[:, par, :, :]),
                     waits=[(ssgn, vs)], inc=sqdve)
                vq_ = sqdve.n
                sq_hist.append(vq_)
                P.op("sync", lambda e, dst_scr=dst_scr, dst_sb=dst_sb:
                     e.dma_start(dst_scr, dst_sb),
                     waits=[(sqdve, vq_)], inc=sscrw, dma=True)
                return vq_

            s_sq = S("s_sq")
            s_sqs = S("s_sqs")
            s_row = S("s_row")
            s_sqrt = S("s_sqrt")
            s_ufw = S("s_ufw")
            s_utm = S("s_utm")
            s_utp = S("s_utp")
            s_uT = S("s_uT")
            s_wqlp = [S("s_wql0"), S("s_wql1")]
            s_wulp = [S("s_wul0"), S("s_wul1")]
            s_wdhp = [S("s_wdh0"), S("s_wdh1")]
            s_gup = S("s_gup")
            s_silu = S("s_silu")
            s_vtt = S("s_vtt")
            s_vw = S("s_vw")
            s_row2 = S("s_row2")
            s_t2 = S("s_t2")
            s_vq8 = S("s_vq8")
            s_vqtp = [S("s_vqt0"), S("s_vqt1")]
            s_wdp = [S("s_wd0"), S("s_wd1")]
            s_dwn = S("s_dwn")
            s_out = S("s_out")

            wgT_q = wgT.rearrange("(a p) (c o) -> p a c o", p=128, o=512)
            wuT_q = wuT.rearrange("(a p) (c o) -> p a c o", p=128, o=512)
            wdT_q = wdT.rearrange("(a p) h -> p a h", p=128)

            s_dmg = S("s_dmg")
            uT_rd = {i: None for i in range(GA)}
            vq_hrd = {0: None, 1: None}
            gup_hist = []
            out_store = {}
            s_outp = [S(f"s_outp{i}") for i in range(6)]
            dmg_hist = []
            out_hist = []

            def ffn_app(app, tiles, xwait, bufs):
                uT, u_f, u_tm = bufs["uT"], bufs["u_f"], bufs["u_tm"]
                vbuf, t2, vq = bufs["vbuf"], bufs["t2"], bufs["vq"]
                wgup, wdb, silub = bufs["wgup"], bufs["wdb"], bufs["silu"]
                vqT, outf = bufs["vqT"], bufs["outf"]
                groups = [tiles[i:i + GA] for i in range(0, len(tiles), GA)]

                def rv(fn, waits=(), inc=None):
                    ws = list(waits) + [(s_row, s_row.n)]
                    ics = [s_row] + ([] if inc is None else
                                     (inc if isinstance(inc, list)
                                      else [inc]))
                    P.op("vector", fn, waits=ws, inc=ics)

                def stats_tile(t):
                    wu = (s_utm, s_utm.n) if s_utm.n else None
                    P.op("scalar", lambda e, t=t: e.activation(
                        u_f[:, 0, :], xs[:, t, :], AF.Square,
                        accum_out=cols["s2"][:, t:t + 1]),
                        waits=list(xwait(t)) + [wu, (s_sqs, s_sqs.n)],
                        inc=s_sqs)
                    yield
                    P.op("scalar", lambda e, t=t: e.activation(
                        u_f[:, 0, :], xs[:, t, :], AF.Identity,
                        accum_out=cols["s1"][:, t:t + 1]),
                        waits=list(xwait(t)) + [(s_sqs, s_sqs.n)],
                        inc=s_sqs)
                    yield
                    P.op("vector", lambda e, t=t: e.reduce_max(
                        cols["mx"][:, t:t + 1], xs[:, t, :], AX.X),
                        waits=list(xwait(t)))
                    yield
                    P.op("vector", lambda e, t=t: e.tensor_reduce(
                        cols["mn"][:, t:t + 1], xs[:, t, :], AX.X,
                        ALU.min), inc=s_sq)
                    yield

                def rowmath_gen(grp, v_sq, v_sqs):
                    G = len(grp)
                    sl = (slice(None), slice(grp[0], grp[0] + G))
                    rv(lambda e, sl=sl: e.tensor_scalar(cols["m"][sl], cols["s1"][sl],
                                                 1.0 / H, None, ALU.mult),
                       waits=[(s_sq, v_sq), (s_sqs, v_sqs)])
                    yield
                    rv(lambda e, sl=sl: e.tensor_scalar(cols["ve"][sl], cols["s2"][sl],
                                                 1.0 / H, None, ALU.mult))
                    yield
                    rv(lambda e, sl=sl: e.tensor_tensor(cols["t0"][sl], cols["m"][sl],
                                                 cols["m"][sl], ALU.mult))
                    yield
                    rv(lambda e, sl=sl: e.tensor_scalar(cols["t0"][sl], cols["t0"][sl],
                                                 -1.0, 1e-5, ALU.mult, ALU.add))
                    yield
                    rv(lambda e, sl=sl: e.tensor_tensor(cols["ve"][sl], cols["ve"][sl],
                                                 cols["t0"][sl], ALU.add))
                    yield
                    rv(lambda e, sl=sl: e.reciprocal(cols["rq"][sl], cols["ve"][sl]))
                    yield
                    v_row = s_row.n
                    P.op("scalar", lambda e, sl=sl: e.activation(
                        cols["r"][sl], cols["rq"][sl], AF.Sqrt),
                        waits=[(s_row, v_row)], inc=s_sqrt)
                    yield
                    v_sqrt = s_sqrt.n
                    rv(lambda e, sl=sl: e.tensor_tensor(cols["t0"][sl], cols["mx"][sl],
                                                 cols["m"][sl], ALU.subtract),
                       waits=[(s_sqrt, v_sqrt)])
                    yield
                    rv(lambda e, sl=sl: e.tensor_tensor(cols["t1"][sl], cols["m"][sl],
                                                 cols["mn"][sl], ALU.subtract))
                    yield
                    rv(lambda e, sl=sl: e.tensor_tensor(cols["mab"][sl], cols["t0"][sl],
                                                 cols["t1"][sl], ALU.max))
                    yield
                    rv(lambda e, sl=sl: e.tensor_scalar(cols["t0"][sl], cols["mab"][sl],
                                                 1e-12, None, ALU.max))
                    yield
                    rv(lambda e, sl=sl: e.reciprocal(cols["t1"][sl], cols["t0"][sl]))
                    yield
                    rv(lambda e, sl=sl: e.tensor_scalar(cols["sp"][sl], cols["t1"][sl],
                                                 127.0, None, ALU.mult))
                    yield
                    rv(lambda e, sl=sl: e.tensor_tensor(cols["t0"][sl], cols["m"][sl],
                                                 cols["sp"][sl], ALU.mult))
                    yield
                    rv(lambda e, sl=sl: e.tensor_scalar(cols["nb"][sl], cols["t0"][sl],
                                                 -1.0, None, ALU.mult))
                    yield
                    rv(lambda e, sl=sl: e.tensor_tensor(cols["q"][sl], cols["r"][sl],
                                                 cols["mab"][sl], ALU.mult))
                    yield
                    rv(lambda e, sl=sl: e.tensor_scalar(cols["cg"][sl], cols["q"][sl],
                                                 gww[:, 0:1], None, ALU.mult))
                    yield
                    rv(lambda e, sl=sl: e.tensor_scalar(cols["cu"][sl], cols["q"][sl],
                                                 gww[:, 1:2], None, ALU.mult))
                    yield

                def head_gen(grp, out):
                    for ti, t in enumerate(grp):
                        yield from stats_tile(t)
                    yield from rowmath_gen(grp, s_sq.n, s_sqs.n)
                    v_row = s_row.n
                    pend = []
                    for ti, t in enumerate(grp):
                        P.op("scalar", lambda e, t=t: e.activation(
                            u_f[:, 0, :], xs[:, t, :], AF.Identity,
                            bias=cols["nb"][:, t:t + 1],
                            scale=cols["sp"][:, t:t + 1]),
                            waits=[(s_row, v_row),
                                   (s_utm, s_utm.n)
                                   if s_utm.n else None],
                            inc=s_ufw)
                        yield
                        v_ufw = s_ufw.n
                        wtr = (s_uT, uT_rd[ti]) if uT_rd[ti] else None
                        P.op("vector", lambda e, ti=ti:
                             e.tensor_scalar(
                                 u_tm[:, ti, :], u_f[:, 0, :], MAGIC,
                                 MAGIC, ALU.add, ALU.subtract),
                             waits=[(s_ufw, v_ufw), wtr], inc=s_utm)
                        pend.append((ti, s_utm.n))
                        yield
                    out["pend"] = pend

                def emit_utp(grp, pend):
                    uT_base = s_uT.n
                    for ti, (slot, v_utm) in enumerate(pend):
                        wprev = (s_gup, gup_hist[-1]) if gup_hist else None
                        wchain = (s_uT, s_uT.n) if s_uT.n else None
                        P.op("sync", lambda e, ti=ti, slot=slot:
                             e.dma_start_transpose(
                                 uT[:, ti, :, :],
                                 u_tm[:, slot, :].rearrange(
                                     "p (c f) -> p c f", f=128)),
                             waits=[(s_utm, v_utm), wprev, wchain],
                             inc=s_uT, dma=True)
                        uT_rd[slot] = s_uT.n
                    return uT_base

                head = {}
                pard_done = {0: None, 1: None}
                for gi, grp in enumerate(groups):
                    G = len(grp)
                    v_dmm_prev = s_dmg.n
                    nhs = {}
                    ngen = None
                    if gi not in head:
                        hstate = {}
                        for _ in head_gen(grp, hstate):
                            pass
                        uT_base = emit_utp(grp, hstate["pend"])
                    else:
                        uT_base = head[gi]
                    if gi + 1 < len(groups) \
                            and not (app == 1 and gi == 0):
                        ngen = head_gen(groups[gi + 1], nhs)

                    vqt_done = {}

                    vtt_tile = {}

                    def emit_vchain(ti, t):
                        tc = (slice(None), slice(t, t + 1))
                        rv(lambda e, ti=ti, tc=tc: e.tensor_reduce(
                            cols["mt"][tc], vmax[:, ti, :], AX.X, ALU.max),
                           waits=[(s_vtt, vtt_tile[ti])])
                        rv(lambda e, tc=tc: e.tensor_scalar(
                            cols["t0"][tc], cols["mt"][tc], 1e-12, None,
                            ALU.max))
                        rv(lambda e, tc=tc: e.reciprocal(
                            cols["t1"][tc], cols["t0"][tc]))
                        rv(lambda e, tc=tc: e.tensor_scalar(
                            cols["f"][tc], cols["t1"][tc], 127.0, None,
                            ALU.mult))
                        rv(lambda e, tc=tc: e.tensor_tensor(
                            cols["t0"][tc], cols["cu"][tc], cols["mt"][tc],
                            ALU.mult))
                        rv(lambda e, tc=tc: e.tensor_scalar(
                            cols["cd"][tc], cols["t0"][tc], gww[:, 2:3], None,
                            ALU.mult), inc=s_row2)
                        v_row2_t = s_row2.n
                        wvq_h = [(s_vqtp[0], vq_hrd[0]) if vq_hrd[0] else None,
                                 (s_vqtp[0], vq_hrd[1]) if vq_hrd[1] else None]
                        for q8 in range(8):
                            pt2 = s_t2.n % 2
                            P.op("scalar", lambda e, ti=ti, t=t, q8=q8,
                                 pt2=pt2:
                                 e.activation(
                                     t2[:, pt2, :],
                                     vbuf[:, ti, q8 * 512:(q8 + 1) * 512],
                                     AF.Identity,
                                     scale=cols["f"][:, t:t + 1]),
                                 waits=[(s_row2, v_row2_t), (smem, v_mem),
                                        (s_vq8, s_t2.n - 1)
                                        if s_t2.n >= 2 else None,
                                        wvq_h[q8 // 4]
                                        if q8 % 4 == 0 else None],
                                 inc=s_t2)
                            v_t2 = s_t2.n
                            P.op("vector", lambda e, q8=q8, pt2=pt2:
                                 e.tensor_scalar(
                                     vq[:, 0, q8 * 512:(q8 + 1) * 512],
                                     t2[:, pt2, :], MAGIC, MAGIC,
                                     ALU.add, ALU.subtract),
                                 waits=[(s_t2, v_t2)], inc=s_vq8)
                            if q8 == 3 or q8 == 7:
                                hh = q8 // 4
                                v_vq8 = s_vq8.n
                                wch = (s_vqtp[0], s_vqtp[0].n) \
                                    if s_vqtp[0].n else None
                                P.op("sync", lambda e, ti=ti, hh=hh:
                                     e.dma_start_transpose(
                                         vqT[:, ti, 16 * hh:16 * hh + 16, :],
                                         vq[:, 0, 2048 * hh:2048 * hh + 2048]
                                         .rearrange("p (c f) -> p c f",
                                                    f=128)),
                                     waits=[(s_vq8, v_vq8), wch],
                                     inc=s_vqtp[0], dma=True)
                                vq_hrd[hh] = s_vqtp[0].n
                        vqt_done[ti] = (s_vqtp[0], s_vqtp[0].n)

                    for oc in range(8):
                        par = oc % 2
                        wclob = (s_gup, gup_hist[-2]) if len(gup_hist) >= 2 \
                            else None
                        if app == 1 and (gi == 0 or DBG_QUANT_ALL):
                            vg = None
                            for wi, (wv, scr) in enumerate(((wgT_q, scr_g),
                                                            (wuT_q, scr_u))):
                                for hh in range(2):
                                    vg = quant_chunk(
                                        wv[:, 4 * hh:4 * hh + 4, oc, :],
                                        wgup[:, par, wi, 4 * hh:4 * hh + 4, :],
                                        scr[oc][:, 4 * hh:4 * hh + 4, :],
                                        wi, extra_w=wclob)
                            wq_wait = [(sqdve, vg)]
                        else:
                            P.op("gpsimd", lambda e, oc=oc, par=par:
                                 e.dma_start(
                                     wgup[:, par, 0, :, :], scr_g[oc]),
                                 waits=[wclob, (sscrw, sscrw.n)],
                                 inc=s_wqlp[par], dma=True)
                            P.op("gpsimd", lambda e, oc=oc, par=par:
                                 e.dma_start(
                                     wgup[:, par, 1, :, :], scr_u[oc]),
                                 inc=s_wqlp[par], dma=True)
                            wq_wait = [(s_wqlp[par], s_wqlp[par].n)]
                        for ti, t in enumerate(grp):
                            pp = s_gup.n % 3
                            wg_, wu_ = pacq(2 * pp), pacq(2 * pp + 1)
                            for icc in range(8):
                                wts = (wq_wait +
                                       [(s_uT, uT_base + 16 * (ti + 1)),
                                        wg_, wu_] if icc == 0 else [])
                                P.op("tensor", lambda e, icc=icc, ti=ti,
                                     par=par, pp=pp:
                                     e.matmul(ppool[:, 2 * pp, :],
                                              uT[:, ti, icc, :],
                                              wgup[:, par, 0, icc, :],
                                              start=(icc == 0), stop=(icc == 7)),
                                     waits=wts)
                                P.op("tensor", lambda e, icc=icc, ti=ti,
                                     par=par, pp=pp:
                                     e.matmul(ppool[:, 2 * pp + 1, :],
                                              uT[:, ti, icc, :],
                                              wgup[:, par, 1, icc, :],
                                              start=(icc == 0), stop=(icc == 7)),
                                     inc=(s_gup if icc == 7 else None))
                            v_gup = s_gup.n
                            sp_ = s_silu.n % 2
                            P.op("scalar", lambda e, t=t, pp=pp, sp_=sp_:
                                 e.activation(silub[:, sp_, :],
                                              ppool[:, 2 * pp, :], AF.Silu,
                                              scale=cols["cg"][:, t:t + 1]),
                                 waits=[(s_gup, v_gup),
                                        (s_vw, s_vw.n - 1)
                                        if s_silu.n >= 2 else None],
                                 inc=s_silu)
                            v_silu = s_silu.n
                            P.op("vector", lambda e, ti=ti, oc=oc, pp=pp,
                                 sp_=sp_:
                                 e.tensor_tensor(
                                     vbuf[:, ti, oc * 512:(oc + 1) * 512],
                                     silub[:, sp_, :], ppool[:, 2 * pp + 1, :],
                                     ALU.mult),
                                 waits=[(s_silu, v_silu),
                                        (s_dmg, v_dmm_prev)],
                                 inc=[s_vw, psl[2 * pp], psl[2 * pp + 1]])
                            P.op("vector", lambda e, ti=ti, oc=oc:
                                 e.reduce_max(
                                     vmax[:, ti, oc:oc + 1],
                                     vbuf[:, ti, oc * 512:(oc + 1) * 512],
                                     AX.X, apply_absolute_value=True),
                                 waits=[(s_vw, s_vw.n)], inc=s_vtt)
                            if oc == 7:
                                vtt_tile[ti] = s_vtt.n
                            if ngen is not None and oc >= 1:
                                for _ in range(1 if oc < 5 else 2):
                                    if next(ngen, "END") == "END":
                                        ngen = None
                                        break
                        gup_hist.append(s_gup.n)


                    for ti, t in enumerate(grp):
                        emit_vchain(ti, t)

                    preload = {}
                    if ngen is not None:
                        for _ in ngen:
                            pass
                        ngen = None
                    if "pend" in nhs:
                        head[gi + 1] = emit_utp(groups[gi + 1], nhs["pend"])

                    dmg_tile = {}
                    dwn_tile = {}
                    for hc in range(2):
                        if hc == 0 and G > 4:
                            sched = [(0, list(range(0, 4))),
                                     (1, list(range(0, 4))),
                                     (0, list(range(4, G))),
                                     (1, list(range(4, G))),
                                     (2, list(range(G))),
                                     (3, list(range(G)))]
                        else:
                            sched = [(icg, list(range(G)))
                                     for icg in range(4)]
                        loaded = {}
                        done_cnt = {k: 0 for k in range(4)}
                        for icg, tis in sched:
                            if not tis:
                                continue
                            pard = (hc * 4 + icg) % 2
                            first_chunk = icg not in loaded
                            if first_chunk:
                                wdclob = (s_dmg, pard_done[pard]) \
                                    if pard_done[pard] else None
                                if (hc, icg) in preload:
                                    loaded[icg] = preload[(hc, icg)]
                                elif app == 1 and (gi == 0
                                                   or DBG_QUANT_ALL):
                                    vg = None
                                    for hh in range(2):
                                        vg = quant_chunk(
                                            wdT_q[:, icg * 8 + 4 * hh:
                                                  icg * 8 + 4 * hh + 4,
                                                  hc * 512:(hc + 1) * 512],
                                            wdb[:, pard,
                                                4 * hh:4 * hh + 4, :],
                                            scr_d[hc, icg]
                                            [:, 4 * hh:4 * hh + 4, :],
                                            2, extra_w=wdclob)
                                    loaded[icg] = [(sqdve, vg)]
                                else:
                                    P.op("gpsimd", lambda e, hc=hc, icg=icg,
                                         pard=pard: e.dma_start(
                                             wdb[:, pard, :, :],
                                             scr_d[hc, icg]),
                                         waits=[wdclob, (sscrw, sscrw.n)],
                                         inc=s_wdp[pard], dma=True)
                                    loaded[icg] = \
                                        [(s_wdp[pard], s_wdp[pard].n)]
                            for ti in tis:
                                for icc in range(8):
                                    wts = []
                                    if icc == 0:
                                        if ti == tis[0] and first_chunk:
                                            wts += loaded[icg]
                                        if icg == 0 and hc == 0:
                                            wts += [vqt_done[ti], pacq(ti)]
                                        if icg == 0 and hc == 1:
                                            wts.append(
                                                (s_dwn, dwn_tile[ti]))
                                    st = (icg == 0 and icc == 0)
                                    sp2 = (icg == 3 and icc == 7)
                                    incs = [s_dmg] if icc == 7 else None
                                    P.op("tensor", lambda e, icc=icc, ti=ti,
                                         icg=icg, pard=pard, st=st, sp2=sp2:
                                         e.matmul(ppool[:, ti, :],
                                                  vqT[:, ti, icg * 8 + icc, :],
                                                  wdb[:, pard, icc, :],
                                                  start=st, stop=sp2,
                                                  skip_group_check=True),
                                         waits=wts, inc=incs)
                                if icg == 3:
                                    dmg_tile[ti] = s_dmg.n
                            done_cnt[icg] += len(tis)
                            if done_cnt[icg] == G:
                                pard_done[pard] = s_dmg.n
                        dmg_hist.append(s_dmg.n)
                        for ti, t in enumerate(grp):
                            last = (hc == 1)
                            if app == 1:
                                P.op("vector", lambda e, ti=ti, t=t, hc=hc:
                                     e.scalar_tensor_tensor(
                                         xs[:, t, hc * 512:(hc + 1) * 512],
                                         ppool[:, ti, :],
                                         cols["cd"][:, t:t + 1],
                                         xs[:, t, hc * 512:(hc + 1) * 512],
                                         ALU.mult, ALU.add),
                                     waits=[(s_dmg, dmg_tile[ti])],
                                     inc=([s_dwn, psl[ti]] if last
                                          else [s_dwn]))
                            else:
                                gidx = gi * GA + ti
                                slot = gidx % 3
                                strm = (gidx % 3) + 3 * hc
                                wout = (s_outp[strm],
                                        out_store.get((gidx - 3, hc), 0)) \
                                    if gidx >= 3 else None
                                P.op("vector", lambda e, ti=ti, t=t, hc=hc,
                                     slot=slot: e.scalar_tensor_tensor(
                                         bufs["outf"][:, slot,
                                                      hc * 512:(hc + 1) * 512],
                                         ppool[:, ti, :],
                                         cols["cd"][:, t:t + 1],
                                         xs[:, t, hc * 512:(hc + 1) * 512],
                                         ALU.mult, ALU.add),
                                     waits=[(s_dmg, dmg_tile[ti]), wout],
                                     inc=([s_dwn, psl[ti]] if last
                                          else [s_dwn]))
                                out_t = out.rearrange("(t p) h -> t p h",
                                                      p=128)
                                P.op("sync", lambda e, t=t, slot=slot, hc=hc:
                                     e.dma_start(
                                         out_t[t - 2][:, hc * 512:(hc + 1) * 512],
                                         bufs["outf"][:, slot,
                                                      hc * 512:(hc + 1) * 512]),
                                     waits=[(s_dwn, s_dwn.n)],
                                     inc=s_outp[strm], dma=True)
                                out_store[(gidx, hc)] = s_outp[strm].n
                            if hc == 0:
                                dwn_tile[ti] = s_dwn.n

            ffn_app(1, list(range(NT)), lambda t: [(sx, 16 * NT)], bufs)

            sbar = S("sbar")

            def barrier():
                base = sbar.n
                for e in Prog.ENGS:
                    P.op(e, lambda x, e=e: P.carrier(e, x), inc=sbar)
                for e in Prog.ENGS:
                    P.op(e, lambda x: x.nop(), waits=[(sbar, base + 5)])

            if debug_taps:
                barrier()
                dxm = taps_out["dbg_xmid"].rearrange("(t p) h -> t p h", p=128)
                sdbg = S("sdbg")
                for t in range(NT):
                    P.op("gpsimd", lambda e, t=t: e.dma_start(dxm[t], xs[:, t, :]),
                         inc=sdbg, dma=True)
                P.op("gpsimd", lambda e: e.nop(), waits=[(sdbg, 16 * NT)])
            barrier()

        with ExitStack() as bctx:
            zf = sb(bctx, "zf", [128, ZR, CH], BF16)
            winT = sb(bctx, "winT", [128, 8, CH], BF16)
            woutT = sb(bctx, "woutT", [128, 16, H], BF16)
            tabs = sb(bctx, "tabs", [128, 5, CH], BF16)
            Csb = sb(bctx, "Csb", [128, 4, 4, 128], BF16)
            iDsb = sb(bctx, "iDsb", [128, 4, 2, 128], BF16)
            bjnk = sb(bctx, "bjnk", [128, H], BF16)
            brows = sb(bctx, "brows", [128, CH], BF16)
            binb = brows[0:1, :]
            bab = brows[32:33, :]
            boutb = brows[64:65, 0:H]
            u2tm = sb(bctx, "u2tm", [128, 1, H], BF16)
            u2T = sb(bctx, "u2T", [128, 2, 8, 128], BF16)

            s_pm = S("s_pm")
            u2Tp = [S("u2Tp0"), S("u2Tp1")]
            sB = {k: S("b_" + k) for k in
                  ("cload", "wint", "woutt", "tapt", "tab", "tabev", "zrow",
                   "zsqrt", "u2", "u2t", "u2T", "zmm", "zev", "fwd", "fev",
                   "pch", "idft", "sig", "gat", "catp", "catev", "omm", "oev",
                   "stat")}

            P.op("gpsimd", lambda e: e.dma_start(Csb[:, :, :, :], Cd),
                 inc=sB["cload"], dma=True)
            P.op("gpsimd", lambda e: e.dma_start(iDsb[:, :, :, :], iDd),
                 inc=sB["cload"], dma=True)
            P.op("gpsimd", lambda e: e.dma_start(binb[:, :], b_in),
                 inc=sB["cload"], dma=True)
            P.op("gpsimd", lambda e: e.dma_start(bab[:, 0:H], b_a),
                 inc=sB["cload"], dma=True)
            P.op("gpsimd", lambda e: e.dma_start(bab[:, H:CH], b_b),
                 inc=sB["cload"], dma=True)
            P.op("gpsimd", lambda e: e.dma_start(boutb[:, :], b_out),
                 inc=sB["cload"], dma=True)
            v_cload = sB["cload"].n

            with ExitStack() as wctx:
                wstg = sb(wctx, "wstg", [128, 2, 4096], BF16)
                tapb = sb(wctx, "tapb", [128, 16, 256], BF16)
                tapT = sb(wctx, "tapT", [128, 2, CH], BF16)
                D2sb = sb(wctx, "D2sb", [128, 2, 5, 128], BF16)
                sldp = [S("b_sld0"), S("b_sld1")]
                sD2 = S("b_sD2")
                staps = S("b_staps")
                wintp = [S("b_wint0"), S("b_wint1")]
                wouttp = [S("b_woutt0"), S("b_woutt1")]
                P.op("gpsimd", lambda e: e.dma_start(D2sb[:, :, :, :], D2d),
                     inc=sD2, dma=True)
                w_in_r = w_in.rearrange("(g p) h -> p g h", p=128)
                w_out_r = w_out.rearrange("(g p) c -> p g c", p=128)
                wstg_v = [wstg[:, 0, :].rearrange("p (g h) -> p g h", h=H),
                          wstg[:, 1, :].rearrange("p (g h) -> p g h", h=H)]
                tpar_hist = {0: [], 1: []}
                for pc in range(4):
                    par = pc % 2
                    w = (wintp[par], tpar_hist[par][-1]) \
                        if tpar_hist[par] else None
                    P.op("gpsimd", lambda e, pc=pc, par=par: e.dma_start(
                        wstg_v[par], w_in_r[:, 4 * pc:4 * pc + 4, :]),
                        waits=[w], inc=sldp[par], dma=True)
                    v_ld = sldp[par].n
                    for g in range(4):
                        cc = 4 * pc + g
                        P.op("sync", lambda e, par=par, cc=cc, g=g:
                             e.dma_start_transpose(
                                 winT[:, :, cc * 128:(cc + 1) * 128],
                                 wstg[:, par, g * H:(g + 1) * H]
                                 .rearrange("p (i f) -> p i f", f=128)),
                             waits=[(sldp[par], v_ld)],
                             inc=wintp[par], dma=True)
                    tpar_hist[par].append(wintp[par].n)
                opar_hist = {0: [], 1: []}
                for pc in range(4):
                    par = pc % 2
                    w1 = (wintp[par], tpar_hist[par][-1])
                    w2 = (wouttp[par], opar_hist[par][-1]) \
                        if opar_hist[par] else None
                    P.op("gpsimd", lambda e, pc=pc, par=par: e.dma_start(
                        wstg[:, par, :].rearrange("p (g c) -> p g c", c=CH),
                        w_out_r[:, 2 * pc:2 * pc + 2, :]),
                        waits=[w1, w2], inc=sldp[par], dma=True)
                    v_ld = sldp[par].n
                    for g in range(2):
                        hc = 2 * pc + g
                        P.op("sync", lambda e, par=par, hc=hc, g=g:
                             e.dma_start_transpose(
                                 woutT[:, :, hc * 128:(hc + 1) * 128],
                                 wstg[:, par, g * CH:(g + 1) * CH]
                                 .rearrange("p (i f) -> p i f", f=128)),
                             waits=[(sldp[par], v_ld)],
                             inc=wouttp[par], dma=True)
                    opar_hist[par].append(wouttp[par].n)
                wa_r = wa.rearrange("(g p) k -> p g k", p=128)
                wb_r = wb.rearrange("(g p) k -> p g k", p=128)
                P.op("gpsimd", lambda e: e.dma_start(tapb[:, 0:8, :], wa_r),
                     inc=staps, dma=True)
                P.op("gpsimd", lambda e: e.dma_start(tapb[:, 8:16, :], wb_r),
                     inc=staps, dma=True)
                v_tapl = staps.n
                for g in range(16):
                    P.op("sync", lambda e, g=g: e.dma_start_transpose(
                        tapT[:, :, g * 128:(g + 1) * 128],
                        tapb[:, g, :].rearrange("p (j f) -> p j f", f=128)),
                        waits=[(staps, v_tapl)], inc=sB["tapt"], dma=True)
                v_tapt = sB["tapt"].n
                for tb in range(5):
                    for fc in range(4):
                        slw = pacq(6)
                        for jc in range(2):
                            P.op("tensor", lambda e, tb=tb, fc=fc, jc=jc:
                                 e.matmul(ppool[:, 6, :],
                                          D2sb[:, jc, tb, :],
                                          tapT[:, jc, fc * 512:(fc + 1) * 512],
                                          start=(jc == 0), stop=(jc == 1)),
                                 waits=([(sB["tapt"], v_tapt),
                                         (sD2, 16), slw]
                                        if jc == 0 else []),
                                 inc=(sB["tab"] if jc == 1 else None))
                        v_tab = sB["tab"].n
                        P.op("vector", lambda e, tb=tb, fc=fc: e.tensor_copy(
                            tabs[:, tb, fc * 512:(fc + 1) * 512],
                            ppool[:, 6, :]),
                            waits=[(sB["tab"], v_tab)],
                            inc=[psl[6], sB["tabev"]])
                v_tabev = sB["tabev"].n
                win_waits = [(wintp[0], wintp[0].n), (wintp[1], wintp[1].n)]
                wout_waits = [(wouttp[0], wouttp[0].n),
                              (wouttp[1], wouttp[1].n)]
                s_bdrain = S("s_bdrain")
                P.op("sync", lambda e: e.nop(),
                     waits=[(sB["tapt"], sB["tapt"].n),
                            (wintp[0], wintp[0].n), (wintp[1], wintp[1].n),
                            (wouttp[0], wouttp[0].n),
                            (wouttp[1], wouttp[1].n)], inc=s_bdrain)
                P.op("tensor", lambda e: e.nop(),
                     waits=[(sB["tabev"], sB["tabev"].n)], inc=s_bdrain)
                s_wprep = S("s_wprep")
                P.op("sync", lambda e: e.nop(),
                     waits=[(sB["tapt"], sB["tapt"].n),
                            (wintp[0], wintp[0].n), (wintp[1], wintp[1].n),
                            (wouttp[0], wouttp[0].n), (wouttp[1], wouttp[1].n),
                            (sldp[0], sldp[0].n), (sldp[1], sldp[1].n),
                            (sD2, sD2.n), (staps, staps.n)], inc=s_wprep)

            if True:
                with ExitStack() as pctx:
                    zh = sb(pctx, "zh", [128, 4, CH], BF16)
                    Pb = sb(pctx, "Pb", [128, 4, CH], BF16)
                    ptmp = sb(pctx, "ptmp", [128, CH], BF16)
                    sigb = sb(pctx, "sigb", [128, 2, 512], BF16)
                    gat = sb(pctx, "gat", [128, CH], BF16)
                    catT = sb(pctx, "catT", [128, 16, 128], BF16)

                    fwd_done_after_block = []
                    idft_done_after_block = []
                    zmm_hist = []
                    head_tp = {}
                    pch_hist = []
                    pairctr = [0]
                    u2tp_last = [None]

                    def tile_head(t):
                        pf = sB["u2"].n % 2
                        tslc = (slice(None), slice(t, t + 1))
                        P.op("scalar", lambda e, t=t, tslc=tslc:
                             e.activation(
                                 bjnk[:, :], xs[:, t, :], AF.Square,
                                 accum_out=cols["s2"][tslc]),
                             waits=[(sB["stat"], sB["stat"].n),
                                    (s_wprep, 1)],
                             inc=sB["stat"])
                        def rz(fn, waits=(), inc=None):
                            ws = list(waits) + [(sB["zrow"], sB["zrow"].n)]
                            ics = [sB["zrow"]] + ([] if inc is None else [inc])
                            P.op("vector", fn, waits=ws, inc=ics)
                        rz(lambda e, t=t, tslc=tslc: e.reduce_sum(
                            cols["s1"][tslc], xs[:, t, :], AX.X))
                        rz(lambda e, tslc=tslc: e.tensor_scalar(
                            cols["m"][tslc], cols["s1"][tslc], 1.0 / H, None,
                            ALU.mult))
                        rz(lambda e, tslc=tslc: e.tensor_scalar(
                            cols["ve"][tslc], cols["s2"][tslc], 1.0 / H, None,
                            ALU.mult), waits=[(sB["stat"], sB["stat"].n)])
                        rz(lambda e, tslc=tslc: e.tensor_tensor(
                            cols["t0"][tslc], cols["m"][tslc], cols["m"][tslc],
                            ALU.mult))
                        rz(lambda e, tslc=tslc: e.tensor_scalar(
                            cols["t0"][tslc], cols["t0"][tslc], -1.0, 1e-5,
                            ALU.mult, ALU.add))
                        rz(lambda e, tslc=tslc: e.tensor_tensor(
                            cols["ve"][tslc], cols["ve"][tslc], cols["t0"][tslc],
                            ALU.add))
                        rz(lambda e, tslc=tslc: e.reciprocal(
                            cols["rq"][tslc], cols["ve"][tslc]))
                        v_zr = sB["zrow"].n
                        P.op("scalar", lambda e, tslc=tslc: e.activation(
                            cols["r"][tslc], cols["rq"][tslc], AF.Sqrt),
                            waits=[(sB["zrow"], v_zr)], inc=sB["zsqrt"])
                        v_zs = sB["zsqrt"].n
                        rz(lambda e, tslc=tslc: e.tensor_tensor(
                            cols["t0"][tslc], cols["m"][tslc], cols["r"][tslc],
                            ALU.mult), waits=[(sB["zsqrt"], v_zs)])
                        rz(lambda e, tslc=tslc: e.tensor_scalar(
                            cols["nb"][tslc], cols["t0"][tslc], -1.0, None,
                            ALU.mult))
                        v_zr = sB["zrow"].n
                        P.op("scalar", lambda e, t=t: e.activation(
                            u2tm[:, 0, :], xs[:, t, :], AF.Identity,
                            bias=cols["nb"][tslc], scale=cols["r"][tslc]),
                            waits=[(sB["zrow"], v_zr), u2tp_last[0]],
                            inc=sB["u2"])
                        v_u2 = sB["u2"].n
                        P.op("sync", lambda e, pf=pf: e.dma_start_transpose(
                            u2T[:, pf, :, :],
                            u2tm[:, 0, :].rearrange("p (c f) -> p c f",
                                                    f=128)),
                            waits=[(sB["u2"], v_u2),
                                   (sB["zmm"], zmm_hist[-1])
                                   if len(zmm_hist) >= 1 else None],
                            inc=u2Tp[pf], dma=True)
                        u2tp_last[0] = (u2Tp[pf], u2Tp[pf].n)
                        head_tp[t] = (u2Tp[pf].n, pf)

                    def tile_zproj(t):
                        v_u2T, pf = head_tp[t]
                        for cc in range(4):
                            sidx = sB["zmm"].n % 2
                            zw = pacq(sidx)
                            for icc in range(8):
                                wts = []
                                if icc == 0:
                                    wts = [(u2Tp[pf], v_u2T),
                                           (sB["cload"], v_cload), zw] \
                                        + win_waits
                                P.op("tensor", lambda e, icc=icc, cc=cc,
                                     sidx=sidx, pf=pf:
                                     e.matmul(ppool[:, sidx, :],
                                              u2T[:, pf, icc, :],
                                              winT[:, icc,
                                                   cc * 512:(cc + 1) * 512],
                                              start=(icc == 0), stop=False),
                                     waits=wts)
                            P.op("tensor", lambda e, cc=cc, sidx=sidx:
                                 e.matmul(ppool[:, sidx, :], ones1[0:1, :],
                                          binb[:, cc * 512:(cc + 1) * 512],
                                          start=False, stop=True),
                                 waits=[(smem, v_mem)], inc=sB["zmm"])
                            v_zmm = sB["zmm"].n
                            slot = t % ZR
                            if t < 2 and not DBG_NOMASK:
                                P.op("vector", lambda e, slot=slot, cc=cc,
                                     sidx=sidx: e.tensor_scalar(
                                         zf[:, slot, cc * 512:(cc + 1) * 512],
                                         ppool[:, sidx, :], zmk[:, :], None,
                                         ALU.mult),
                                     waits=[(sB["zmm"], v_zmm),
                                            (szmk, 16)],
                                     inc=[psl[sidx], sB["zev"]])
                            else:
                                P.op("scalar", lambda e, slot=slot, cc=cc,
                                     sidx=sidx: e.activation(
                                         zf[:, slot, cc * 512:(cc + 1) * 512],
                                         ppool[:, sidx, :], AF.Identity),
                                     waits=[(sB["zmm"], v_zmm)],
                                     inc=[psl[sidx], sB["zev"]])
                        zmm_hist.append(sB["zmm"].n)

                    def emit_block_gen(b):
                        bd = (s_bdrain, 2) if b == 0 else None
                        v_zev_need = 4 * (2 * b + 4)
                        pair_specs = [
                            ((0, 2), [(0, 0, 0, 1, 2, ALU.subtract),
                                      (2, 1, 4, 0, 2, ALU.add)]),
                            ((1, 3), [(1, 0, 1, 1, 3, ALU.subtract),
                                      (3, 1, 1, 0, 3, ALU.add)]),
                        ]
                        pdata = []
                        for pk, ((poA, poB), steps) in enumerate(pair_specs):
                            zbase = (pairctr[0] % 2) * 2
                            pairctr[0] += 1
                            wzh = (sB["pch"], pch_hist[-2 + pk]) \
                                if len(pch_hist) >= 2 else None
                            for zi, po in enumerate((poA, poB)):
                                for fc in range(4):
                                    sidx = 2 + sB["fwd"].n % 2
                                    fw = pacq(sidx)
                                    for icc in range(4):
                                        wts = []
                                        if icc == 0:
                                            wts = [(sB["zev"], v_zev_need),
                                                   (sB["cload"], v_cload), fw]
                                        P.op("tensor", lambda e, icc=icc,
                                             po=po, fc=fc, b=b, sidx=sidx:
                                             e.matmul(
                                                 ppool[:, sidx, :],
                                                 Csb[:, icc, po, :],
                                                 zf[:, (2 * b + icc) % ZR,
                                                    fc * 512:(fc + 1) * 512],
                                                 start=(icc == 0),
                                                 stop=(icc == 3)),
                                             waits=wts,
                                             inc=(sB["fwd"] if icc == 3
                                                  else None))
                                    v_fwd = sB["fwd"].n
                                    P.op("scalar", lambda e, zi=zi, fc=fc,
                                         sidx=sidx, zbase=zbase:
                                         e.activation(
                                             zh[:, zbase + zi,
                                                fc * 512:(fc + 1) * 512],
                                             ppool[:, sidx, :], AF.Identity),
                                         waits=[(sB["fwd"], v_fwd), bd,
                                                wzh if fc == 0 and zi == 0
                                                else None],
                                         inc=[psl[sidx], sB["fev"]])
                            pdata.append((zbase, steps, sB["fev"].n))
                        yield
                        v_idft_prev = idft_done_after_block[-1] \
                            if idft_done_after_block else 0
                        for zbase, steps, v_fev in pdata:
                            for (pidx, za, ta, zb2, tb2, opf) in steps:
                                P.op("vector", lambda e, pidx=pidx, za=za,
                                     ta=ta, zbase=zbase: e.tensor_tensor(
                                         Pb[:, pidx, :],
                                         zh[:, zbase + za, :],
                                         tabs[:, ta, :], ALU.mult),
                                     waits=[(sB["fev"], v_fev), bd,
                                            (sB["tabev"], v_tabev),
                                            (sB["idft"], v_idft_prev),
                                            (s_pm, s_pm.n)], inc=s_pm)
                                P.op("vector", lambda e, zb2=zb2, tb2=tb2,
                                     zbase=zbase:
                                     e.tensor_tensor(ptmp[:, :],
                                                     zh[:, zbase + zb2, :],
                                                     tabs[:, tb2, :],
                                                     ALU.mult),
                                     waits=[(s_pm, s_pm.n)], inc=s_pm)
                                P.op("vector", lambda e, pidx=pidx, opf=opf:
                                     e.tensor_tensor(Pb[:, pidx, :],
                                                     Pb[:, pidx, :],
                                                     ptmp[:, :], opf),
                                     waits=[(s_pm, s_pm.n)],
                                     inc=[s_pm, sB["pch"]])
                            pch_hist.append(sB["pch"].n)
                        v_pch = sB["pch"].n
                        yield

                        for po in range(2):
                            tgt = 2 + 2 * b + po
                            for pc in range(2):
                                wA, wB2 = pacq(4), pacq(5)
                                for half, (sidx, fc) in enumerate(
                                        ((4, pc), (5, pc + 2))):
                                    for icc in range(4):
                                        wts = []
                                        if icc == 0:
                                            wts = [(sB["pch"], v_pch),
                                                   wA if half == 0 else wB2]
                                        P.op("tensor", lambda e, icc=icc,
                                             po=po, fc=fc, sidx=sidx:
                                             e.matmul(
                                                 ppool[:, sidx, :],
                                                 iDsb[:, icc, po, :],
                                                 Pb[:, icc,
                                                    fc * 512:(fc + 1) * 512],
                                                 start=(icc == 0), stop=False),
                                             waits=wts)
                                    P.op("tensor", lambda e, fc=fc, sidx=sidx:
                                         e.matmul(ppool[:, sidx, :],
                                                  ones1[32:33, :],
                                                  bab[:, fc * 512:
                                                      (fc + 1) * 512],
                                                  start=False, stop=True),
                                         inc=sB["idft"])
                                v_idft = sB["idft"].n
                                P.op("scalar", lambda e: e.activation(
                                    sigb[:, 0, :], ppool[:, 4, :], AF.Sigmoid),
                                    waits=[(sB["idft"], v_idft), bd],
                                    inc=sB["sig"])
                                P.op("scalar", lambda e: e.activation(
                                    sigb[:, 1, :], ppool[:, 5, :], AF.Sigmoid),
                                    inc=sB["sig"])
                                v_sig = sB["sig"].n
                                P.op("vector", lambda e, pc=pc: e.tensor_tensor(
                                    gat[:, pc * 512:(pc + 1) * 512],
                                    ppool[:, 4, :], sigb[:, 1, :], ALU.mult),
                                    waits=[(sB["sig"], v_sig), bd,
                                           (sB["catev"], sB["catev"].n)
                                           if pc == 0 else None])
                                P.op("vector", lambda e, pc=pc: e.tensor_tensor(
                                    gat[:, (pc + 2) * 512:(pc + 3) * 512],
                                    ppool[:, 5, :], sigb[:, 0, :], ALU.mult),
                                    inc=[sB["gat"], psl[4], psl[5]])
                            v_gat = sB["gat"].n
                            P.op("sync", lambda e: e.dma_start_transpose(
                                catT[:, :, :],
                                gat[:, :].rearrange("p (c f) -> p c f",
                                                    f=128)),
                                waits=[(sB["gat"], v_gat), bd,
                                       (sB["omm"], sB["omm"].n)],
                                inc=sB["catev"], dma=True)
                            v_catev = sB["catev"].n
                            for hc in range(2):
                                ow = pacq(6)
                                for ec in range(16):
                                    wts = []
                                    if ec == 0:
                                        wts = [(sB["catev"], v_catev),
                                               ow] + wout_waits
                                    P.op("tensor", lambda e, ec=ec, hc=hc:
                                         e.matmul(
                                             ppool[:, 6, :],
                                             catT[:, ec, :],
                                             woutT[:, ec,
                                                   hc * 512:(hc + 1) * 512],
                                             start=(ec == 0), stop=False),
                                         waits=wts)
                                P.op("tensor", lambda e, hc=hc: e.matmul(
                                    ppool[:, 6, :], ones1[64:65, :],
                                    boutb[:, hc * 512:(hc + 1) * 512],
                                    start=False, stop=True),
                                    inc=sB["omm"])
                                v_omm = sB["omm"].n
                                P.op("vector", lambda e, tgt=tgt, hc=hc:
                                     e.tensor_tensor(
                                         xs[:, tgt, hc * 512:(hc + 1) * 512],
                                         ppool[:, 6, :],
                                         xs[:, tgt, hc * 512:(hc + 1) * 512],
                                         ALU.add),
                                     waits=[(sB["omm"], v_omm)],
                                     inc=[psl[6], sB["oev"]])
                            if po == 0:
                                yield
                        idft_done_after_block.append(sB["idft"].n)

                    def do_zproj(t):
                        if t + 1 < NT and (t + 1) not in head_tp:
                            tile_head(t + 1)
                        tile_zproj(t)

                    tile_head(0)
                    for t in range(4):
                        do_zproj(t)
                    NB = (NT - 2) // 2
                    gens = {}
                    for b in range(NB):
                        gens[b] = emit_block_gen(b)
                        next(gens[b])
                        if 2 * b + 4 < NT:
                            do_zproj(2 * b + 4)
                        if b >= 1:
                            next(gens[b - 1])
                        if 2 * b + 5 < NT:
                            do_zproj(2 * b + 5)
                        if b >= 1:
                            for _ in gens.pop(b - 1):
                                pass
                        next(gens[b])
                    next(gens[NB - 1])
                    for _ in gens.pop(NB - 1):
                        pass
                    if debug_taps:
                        barrier()
                        dz = taps_out["dbg_z"].rearrange("(t p) c -> t p c",
                                                         p=128)
                        sdbg2 = S("sdbg2")
                        for t in range(NT - ZR, NT):
                            P.op("gpsimd", lambda e, t=t: e.dma_start(
                                dz[t], zf[:, t % ZR, :]), inc=sdbg2, dma=True)
                        P.op("gpsimd", lambda e: e.nop(),
                             waits=[(sdbg2, 16 * ZR)])
                        dx3 = taps_out["dbg_x3"].rearrange(
                            "(t p) h -> t p h", p=128)
                        sdbg3 = S("sdbg3")
                        for t in range(NV):
                            P.op("gpsimd", lambda e, t=t: e.dma_start(
                                dx3[t], xs[:, t + 2, :]), inc=sdbg3, dma=True)
                        P.op("gpsimd", lambda e: e.nop(),
                             waits=[(sdbg3, 16 * NV)])
            barrier()

        with ExitStack() as cctx:
            bufs2 = dict(
                uT=sb(cctx, "uT2", [128, GA, 8, 128], BF16),
                u_f=sb(cctx, "u_f2", [128, 1, H], F32),
                u_tm=sb(cctx, "u_tm2", [128, GA, H], BF16),
                vbuf=sb(cctx, "vbuf2", [128, GA, INNER], BF16),
                t2=sb(cctx, "t22", [128, 2, 512], F32),
                vq=sb(cctx, "vq2", [128, 1, INNER], BF16),
                wgup=sb(cctx, "wgup2", [128, 2, 2, 8, 512], BF16),
                wdb=sb(cctx, "wdb2", [128, 2, 8, 512], BF16),
                silu=sb(cctx, "silu2", [128, 2, 512], F32),
                outf=sb(cctx, "outf", [128, 3, H], F32),
            )
            bufs2["vqT"] = nc.alloc_sbuf_tensor_at(
                "vqT2", [128, GA, 32, 128], BF16,
                offset=nc.lookup_mloc(bufs2["vbuf"]).addr)
            ffn_app(2, list(range(2, NT)), lambda t: [], bufs2)
            P.op("sync", lambda e: e.nop(),
                 waits=[(s, s.n) for s in s_outp])

        with nc.Block() as block:
            @block.sync
            def _(eng):
                P.emit("sync", eng)

            @block.gpsimd
            def _(eng):
                P.emit("gpsimd", eng)

            @block.tensor
            def _(eng):
                P.emit("tensor", eng)

            @block.vector
            def _(eng):
                P.emit("vector", eng)

            @block.scalar
            def _(eng):
                P.emit("scalar", eng)

    return nc


def kernel(L0_attn_x, L0_attn_w_in, L0_attn_b_in, L0_attn_w_a, L0_attn_b_a,
           L0_attn_w_b, L0_attn_b_b, L0_attn_w_out, L0_attn_b_out,
           L0_ffn_w_gate, L0_ffn_w_up, L0_ffn_w_down, _debug_taps=False,
           _cores=None):
    x = np.asarray(L0_attn_x, np.float32)
    B, T, Hd = x.shape
    xf = np.ascontiguousarray(x.reshape(B * T, Hd))

    Cd, iDd, D2d = _dft_consts()
    shared = dict(
        wgT=np.ascontiguousarray(np.asarray(L0_ffn_w_gate, np.float32).T),
        wuT=np.ascontiguousarray(np.asarray(L0_ffn_w_up, np.float32).T),
        wdT=np.ascontiguousarray(np.asarray(L0_ffn_w_down, np.float32).T),
        w_in=np.ascontiguousarray(np.asarray(L0_attn_w_in, np.float32)),
        w_out=np.ascontiguousarray(np.asarray(L0_attn_w_out, np.float32)),
        wa=np.ascontiguousarray(np.asarray(L0_attn_w_a, np.float32)[:, 0, :]),
        wb=np.ascontiguousarray(np.asarray(L0_attn_w_b, np.float32)[:, 0, :]),
        b_in=np.ascontiguousarray(np.asarray(L0_attn_b_in, np.float32)[None, :]),
        b_a=np.ascontiguousarray(np.asarray(L0_attn_b_a, np.float32)[None, :]),
        b_b=np.ascontiguousarray(np.asarray(L0_attn_b_b, np.float32)[None, :]),
        b_out=np.ascontiguousarray(np.asarray(L0_attn_b_out,
                                              np.float32)[None, :]),
        Cd=Cd, iDd=iDd, D2d=D2d,
        identd=np.eye(128, dtype=BFnp),
    )

    wgT_s = shared["wgT"].reshape(16, 128, 2048)
    wuT_s = shared["wuT"].reshape(16, 128, 2048)
    wdT_s = shared["wdT"].reshape(16, 128, 2048)

    nc = build_program(debug_taps=_debug_taps)
    cores = list(range(8)) if _cores is None else _cores
    in_maps = []
    for core in cores:
        tok0 = core * TPC
        seq_start = (core % 4 == 0)
        halo = (np.zeros((256, Hd), np.float32) if seq_start
                else xf[tok0 - 256: tok0])
        xhv = np.ascontiguousarray(
            np.concatenate([halo, xf[tok0:tok0 + TPC]], 0))
        zmv = np.full((128, 1), 0.0 if seq_start else 1.0, np.float32)
        wsclv = np.ascontiguousarray(np.stack(
            [wgT_s[2 * core:2 * core + 2],
             wuT_s[2 * core:2 * core + 2],
             wdT_s[2 * core:2 * core + 2]]))
        in_maps.append(dict(xh=xhv, zmask=zmv, wscl=wsclv, **shared))

    res = run_bass_kernel_spmd(nc, in_maps, core_ids=list(range(len(cores))))
    if _debug_taps:
        return res
    outs = [np.asarray(r["out"]) for r in res.results]
    return np.concatenate(outs, axis=0).reshape(B, T, Hd)


# version 32
# speedup vs baseline: 1.0417x; 1.0417x over previous
import numpy as np
import ml_dtypes
import concourse.bass as bass
import concourse.mybir as mybir
from concourse.bass_utils import run_bass_kernel_spmd

F32 = mybir.dt.float32
BF16 = mybir.dt.bfloat16
AF = mybir.ActivationFunctionType
ALU = mybir.AluOpType
AX = mybir.AxisListType
BFnp = ml_dtypes.bfloat16

MAGIC = float(np.float32(1.5 * 2**23))
DBG_QUANT_ALL = False
DBG_TRUNC = None
DBG_RACE = False
DBG_NOMASK = False
H = 1024
INNER = 4096
CH = 2048
NT = 18
NV = 16
TPC = 2048
NBLK = 8
GA = 6
ZR = 4


class Sem:
    def __init__(self, nc, name):
        self.h = nc.alloc_semaphore(name)
        self.n = 0


class Prog:
    ENGS = ("sync", "gpsimd", "tensor", "vector", "scalar")

    def __init__(self, nc):
        self.nc = nc
        self.ops = {e: [] for e in self.ENGS}
        self.sems = {}

    def sem(self, name):
        if name not in self.sems:
            self.sems[name] = Sem(self.nc, name)
        return self.sems[name]

    def op(self, eng, fn, waits=(), inc=None, dma=False):
        amt = 16 if dma else 1
        incs = []
        self.gidx = getattr(self, 'gidx', 0) + 1
        ws = [(s, v) for w in waits if w is not None for (s, v) in [w] if v > 0]
        for s, v in ws:
            assert v <= s.n, (f"wait on {s.h.name} for {v} but only {s.n} "
                              f"produced so far (eng {eng})")
        if inc is not None:
            for it in (inc if isinstance(inc, list) else [inc]):
                if isinstance(it, tuple):
                    s, a = it
                else:
                    s, a = it, amt
                s.n += a
                incs.append((s, a, s.n))
        self.ops[eng].append((fn, ws, incs, self.gidx))
        return incs[0][2] if incs else None

    def carrier(self, eng_name, eng):
        if eng_name in ("vector", "scalar", "gpsimd"):
            ent = self.carriers[eng_name]
            buf, k, cap = ent
            assert k < cap, f"carrier slots exhausted for {eng_name}"
            ent[1] = k + 1
            p, c = (k % 4) * 32, k // 4
            if eng_name == "scalar":
                return eng.memzero(buf[p:p + 1, c:c + 1])
            return eng.memset(buf[p:p + 1, c:c + 1], 0.0)
        return eng.nop()

    def emit(self, eng_name, eng):
        ops = self.ops[eng_name]
        if DBG_TRUNC is not None:
            if isinstance(DBG_TRUNC, float):
                ops = ops[:max(1, int(len(ops) * DBG_TRUNC))]
            else:
                ops = [o for o in ops if o[3] <= DBG_TRUNC]
        for fn, ws, incs, _gi in ops:
            for s, v in ws:
                eng.wait_ge(s.h, v)
            ins = fn(eng)
            if incs:
                ins.then_inc(incs[0][0].h, incs[0][1])
                for s, a, _ in incs[1:]:
                    self.carrier(eng_name, eng).then_inc(s.h, a)


def _dft_consts():
    N, K, OUT = 512, 256, 256
    t = np.arange(N)
    C = np.zeros((N, N), np.float64)
    for fr in range(N):
        if fr <= 256:
            C[:, fr] = np.cos(2 * np.pi * fr * t / N)
        else:
            C[:, fr] = np.sin(2 * np.pi * (fr - 256) * t / N)
    tau = np.arange(OUT) + (N - OUT)
    iD = np.zeros((N, OUT), np.float64)
    for fr in range(N):
        if fr <= 256:
            c = 1.0 if fr in (0, 256) else 2.0
            iD[fr] = c / N * np.cos(2 * np.pi * fr * tau / N)
        else:
            iD[fr] = 2.0 / N * np.sin(2 * np.pi * (fr - 256) * tau / N)
    j = np.arange(K)
    jr = (K - 1 - j)[:, None]
    Wc = np.cos(2 * np.pi * np.arange(257)[None, :] * jr / N)
    Ws = np.sin(2 * np.pi * np.arange(256)[None, :] * jr / N)
    U0 = Wc[:, 0:128].copy()
    U1 = Wc[:, 128:256].copy()
    V0 = np.zeros((K, 128)); V0[:, 1:] = Ws[:, 1:128]
    V1 = Ws[:, 128:256].copy()
    Nt = np.zeros((K, 128)); Nt[:, 0] = Wc[:, 256]; Nt[:, 1:] = Wc[:, 1:128]
    D2 = np.stack([U0, U1, V0, V1, Nt], axis=1)
    Cd = np.ascontiguousarray(
        C.reshape(4, 128, 4, 128).transpose(1, 0, 2, 3)).astype(BFnp)
    iDd = np.ascontiguousarray(
        iD.reshape(4, 128, 2, 128).transpose(1, 0, 2, 3)).astype(BFnp)
    D2d = np.ascontiguousarray(
        D2.reshape(2, 128, 5, 128).transpose(1, 0, 2, 3)).astype(BFnp)
    return Cd, iDd, D2d


def build_program(debug_taps=False):
    nc = bass.Bass("TRN2", target_bir_lowering=False,
                   dynamic_dma_scratch_size=16384,
                   detect_race_conditions=DBG_RACE)

    def din(name, shp, dt=F32):
        return nc.dram_tensor(name, list(shp), dt, kind="ExternalInput").ap()

    xh = din("xh", [NT * 128, H])
    zmask = din("zmask", [128, 1])
    wgT = din("wgT", [H, INNER])
    wuT = din("wuT", [H, INNER])
    wdT = din("wdT", [INNER, H])
    w_in = din("w_in", [CH, H])
    w_out = din("w_out", [H, CH])
    wa = din("wa", [H, 256])
    wb = din("wb", [H, 256])
    b_in = din("b_in", [1, CH])
    b_a = din("b_a", [1, H])
    b_b = din("b_b", [1, H])
    b_out = din("b_out", [1, H])
    wscl = din("wscl", [3, 2, 128, 2048])
    Cd = din("Cd", [128, 4, 4, 128], BF16)
    iDd = din("iDd", [128, 4, 2, 128], BF16)
    D2d = din("D2d", [128, 2, 5, 128], BF16)
    identd = din("identd", [128, 128], BF16)

    out = nc.dram_tensor("out", [TPC, H], F32, kind="ExternalOutput").ap()
    taps_out = {}
    if debug_taps:
        for nm, shp in (("dbg_xmid", [NT * 128, H]), ("dbg_z", [NT * 128, CH]),
                        ("dbg_x3", [NV * 128, H]),):
            taps_out[nm] = nc.dram_tensor(nm, shp, F32,
                                          kind="ExternalOutput").ap()

    scr_g = nc.dram_tensor("scr_g", [8, 128, 8, 512], BF16, kind="Internal").ap()
    scr_u = nc.dram_tensor("scr_u", [8, 128, 8, 512], BF16, kind="Internal").ap()
    scr_d = nc.dram_tensor("scr_d", [2, 4, 128, 8, 512], BF16,
                           kind="Internal").ap()
    cc_in = nc.dram_tensor("cc_in", [128, 3], F32, kind="Internal").ap()
    cc_out = nc.dram_tensor("cc_out", [128, 3], F32, kind="Internal",
                            addr_space="Shared").ap()

    P = Prog(nc)
    S = P.sem

    from contextlib import ExitStack
    with ExitStack() as top:
        def sb(ctx, name, shp, dt):
            return ctx.enter_context(nc.sbuf_tensor(name, shp, dt))

        xs = sb(top, "xs", [128, NT, H], BF16)
        dumv = sb(top, "dumv", [128, 256], F32)
        dums = sb(top, "dums", [128, 128], F32)
        dumg = sb(top, "dumg", [128, 128], F32)
        P.carriers = {"vector": [dumv, 0, 1024], "scalar": [dums, 0, 512],
                      "gpsimd": [dumg, 0, 512]}
        ident = sb(top, "ident", [128, 128], BF16)
        ones1 = sb(top, "ones1", [128, 128], BF16)
        ones128f = sb(top, "ones128f", [128, 128], F32)
        magicrow = sb(top, "magicrow", [128, 1], F32)
        hp = sb(top, "hp", [128, 1], F32)
        hn = sb(top, "hn", [128, 1], F32)
        zmk = sb(top, "zmk", [128, 1], F32)
        wsum = sb(top, "wsum", [128, 3], F32)
        acol = sb(top, "acol", [128, 3, 1], F32)
        nwcol = sb(top, "nwcol", [128, 3], F32)
        swv = sb(top, "swv", [128, 3], F32)
        gww = sb(top, "gww", [128, 3], F32)
        cols = {}
        for nm in ("s1", "s2", "mx", "mn", "m", "ve", "rq", "r", "mab", "sp",
                   "nb", "q", "cg", "cu", "mt", "f", "cd", "t0", "t1"):
            cols[nm] = sb(top, "col_" + nm, [128, NT], F32)
        vmax = sb(top, "vmax", [128, GA, 8], F32)
        ppool = top.enter_context(nc.psum_tensor("ppool", [128, 7, 512], F32))
        ptr = top.enter_context(nc.psum_tensor("ptr", [128, 8, 128], BF16))

        psl = [S(f"psl{i}") for i in range(7)]
        puse = [0] * 7
        ptrsl = S("pslT")
        ptruse = [0]

        def pacq(i):
            w = (psl[i], puse[i]) if puse[i] else None
            puse[i] += 1
            return w

        def ptracq():
            w = (ptrsl, ptruse[0]) if ptruse[0] else None
            ptruse[0] += 1
            return w

        sx = S("sx")
        sinit = S("sinit")
        smem = S("smem")
        szmk = S("szmk")
        sident = S("sident")
        P.op("gpsimd", lambda g: g.memset(ones1[:, :], 1.0))
        P.op("gpsimd", lambda g: g.memset(ones128f[:, :], 1.0))
        P.op("gpsimd", lambda g: g.memset(magicrow[:, :], MAGIC))
        P.op("gpsimd", lambda g: g.memset(hp[:, :], 0.5))
        P.op("gpsimd", lambda g: g.memset(hn[:, :], -0.5), inc=smem)
        v_mem = smem.n
        P.op("gpsimd", lambda g: g.dma_start(zmk[:, :], zmask), inc=szmk, dma=True)
        P.op("gpsimd", lambda g: g.dma_start(ident[:, :], identd), inc=sident,
             dma=True)
        v_init = sident.n
        xh_t = xh.rearrange("(t p) h -> t p h", p=128)
        for t in range(NT):
            P.op("gpsimd", lambda g, t=t: g.dma_start(xs[:, t, :], xh_t[t]),
                 inc=sx, dma=True)

        with ExitStack() as actx:
            wfp = sb(actx, "wfp", [128, 2, 4, 512], F32)
            wsg = sb(actx, "wsg", [128, 4, 512], BF16)
            bufs = dict(
                uT=sb(actx, "uT", [128, GA, 8, 128], BF16),
                u_f=sb(actx, "u_f", [128, 1, H], F32),
                u_tm=sb(actx, "u_tm", [128, GA, H], BF16),
                vbuf=sb(actx, "vbuf", [128, GA, INNER], BF16),
                t2=sb(actx, "t2", [128, 2, 512], F32),
                vq=sb(actx, "vq", [128, 1, INNER], BF16),
                wgup=sb(actx, "wgup", [128, 2, 2, 8, 512], BF16),
                wdb=sb(actx, "wdb", [128, 2, 8, 512], BF16),
                silu=sb(actx, "silu", [128, 2, 512], F32),
                outf=None,
            )
            bufs["vqT"] = nc.alloc_sbuf_tensor_at(
                "vqT", [128, GA, 32, 128], BF16,
                offset=nc.lookup_mloc(bufs["vbuf"]).addr)

            swfp = [S("swf0"), S("swf1")]
            sabs = S("sabs")
            sacc = S("sacc")
            sscale = S("sscale")
            s_ccl = S("s_ccl")
            s_ccd = S("s_ccd")
            kglob = 0
            for wi in range(3):
                for k in range(2):
                    par = kglob % 2
                    w = (sabs, kglob - 1) if kglob >= 2 else None
                    P.op("sync", lambda e, wi=wi, k=k, par=par: e.dma_start(
                        wfp[:, par, :, :].rearrange("p c o -> p (c o)"),
                        wscl[wi, k]),
                        waits=[w], inc=swfp[par], dma=True)
                    kglob += 1
                    P.op("scalar", lambda e, wi=wi, par=par: e.activation(
                        wsg[:, :, :], wfp[:, par, :, :], AF.Abs,
                        accum_out=acol[:, wi, :]),
                        waits=[(swfp[par], swfp[par].n), (sabs, sabs.n),
                               (sacc, sacc.n)],
                        inc=sabs)
                    if k == 0:
                        P.op("vector", lambda e, wi=wi: e.tensor_copy(
                            wsum[:, wi:wi + 1], acol[:, wi, :]),
                            waits=[(sabs, kglob), (sacc, sacc.n)], inc=sacc)
                    else:
                        P.op("vector", lambda e, wi=wi: e.tensor_add(
                            wsum[:, wi:wi + 1], wsum[:, wi:wi + 1],
                            acol[:, wi, :]),
                            waits=[(sabs, kglob), (sacc, sacc.n)], inc=sacc)
            NSC = kglob
            P.op("gpsimd", lambda e: e.dma_start(cc_in, wsum[:, :]),
                 waits=[(sacc, NSC)], inc=s_ccl, dma=True)
            P.op("gpsimd", lambda e: e.collective_compute(
                "AllReduce", ALU.add,
                replica_groups=[[0, 1, 2, 3, 4, 5, 6, 7]],
                ins=[cc_in], outs=[cc_out]),
                waits=[(s_ccl, 16)], inc=s_ccd)
            P.op("gpsimd", lambda e: e.dma_start(wsum[:, :], cc_out),
                 waits=[(s_ccd, 1)], inc=s_ccl, dma=True)
            sred = S("sred")
            wsl = pacq(6)
            P.op("tensor", lambda e: e.matmul(ppool[:, 6, 0:3], ones128f[:, :],
                                              wsum[:, :], start=True, stop=True),
                 waits=[(s_ccl, 32), wsl, (smem, v_mem)], inc=sred)
            P.op("vector", lambda e: e.tensor_copy(nwcol[:, :], ppool[:, 6, 0:3]),
                 waits=[(sred, 1)], inc=[psl[6], sscale])
            P.op("vector", lambda e: e.tensor_scalar(
                nwcol[:, :], nwcol[:, :], 1.0 / (H * INNER), 1e-5,
                ALU.mult, ALU.max), waits=[(sscale, sscale.n)], inc=sscale)
            P.op("vector", lambda e: e.reciprocal(swv[:, :], nwcol[:, :]),
                 waits=[(sscale, sscale.n)], inc=sscale)
            P.op("vector", lambda e: e.tensor_scalar(
                gww[:, 0:2], nwcol[:, 0:2], 1.0 / 254.0, None, ALU.mult),
                waits=[(sscale, sscale.n)], inc=sscale)
            P.op("vector", lambda e: e.tensor_scalar(
                gww[:, 2:3], nwcol[:, 2:3], 0.25 / 127.0, None, ALU.mult),
                waits=[(sscale, sscale.n)], inc=sscale)
            v_scale = sscale.n

            sqfp = [S("sqf0"), S("sqf1")]
            ssgn = S("ssgn")
            sqdve = S("sqdve")
            sscrw = S("sscrw")
            qk = [0]

            sq_hist = []

            def quant_chunk(src_view, dst_sb, dst_scr, wi, extra_w=None):
                par = qk[0] % 2
                w = (ssgn, 2 * (qk[0] - 1)) if qk[0] >= 2 else (sabs, 6)
                P.op("sync", lambda e: e.dma_start(wfp[:, par, :, :], src_view),
                     waits=[w], inc=sqfp[par], dma=True)
                qk[0] += 1
                vqf = sqfp[par].n
                P.op("scalar", lambda e, par=par, wi=wi: e.activation(
                    wsg[:, :, :], wfp[:, par, :, :], AF.Sign,
                    bias=hn[:, :], scale=swv[:, wi:wi + 1]),
                    waits=[(sqfp[par], vqf), (sscale, v_scale), (smem, v_mem),
                           (sqdve, sqdve.n), (sabs, 6)],
                    inc=ssgn)
                P.op("scalar", lambda e, par=par, wi=wi, dst_sb=dst_sb:
                     e.activation(dst_sb, wfp[:, par, :, :], AF.Sign,
                                  bias=hp[:, :], scale=swv[:, wi:wi + 1]),
                     waits=[extra_w, (sscrw, sscrw.n)], inc=ssgn)
                vs = ssgn.n
                P.op("vector", lambda e, dst_sb=dst_sb: e.tensor_add(
                    dst_sb, dst_sb, wsg[:, :, :]),
                    waits=[(ssgn, vs)], inc=sqdve)
                vq_ = sqdve.n
                sq_hist.append(vq_)
                P.op("sync", lambda e, dst_scr=dst_scr, dst_sb=dst_sb:
                     e.dma_start(dst_scr, dst_sb),
                     waits=[(sqdve, vq_)], inc=sscrw, dma=True)
                return vq_

            s_sq = S("s_sq")
            s_sqs = S("s_sqs")
            s_row = S("s_row")
            s_sqrt = S("s_sqrt")
            s_ufw = S("s_ufw")
            s_utm = S("s_utm")
            s_utp = S("s_utp")
            s_uT = S("s_uT")
            s_wqlp = [S("s_wql0"), S("s_wql1")]
            s_wulp = [S("s_wul0"), S("s_wul1")]
            s_wdhp = [S("s_wdh0"), S("s_wdh1")]
            s_gup = S("s_gup")
            s_silu = S("s_silu")
            s_vtt = S("s_vtt")
            s_vw = S("s_vw")
            s_row2 = S("s_row2")
            s_t2 = S("s_t2")
            s_vq8 = S("s_vq8")
            s_vqtp = [S("s_vqt0"), S("s_vqt1")]
            s_wdp = [S("s_wd0"), S("s_wd1")]
            s_dwn = S("s_dwn")
            s_out = S("s_out")

            wgT_q = wgT.rearrange("(a p) (c o) -> p a c o", p=128, o=512)
            wuT_q = wuT.rearrange("(a p) (c o) -> p a c o", p=128, o=512)
            wdT_q = wdT.rearrange("(a p) h -> p a h", p=128)

            s_dmg = S("s_dmg")
            uT_rd = {i: None for i in range(GA)}
            vq_hrd = {0: None, 1: None}
            gup_hist = []
            out_store = {}
            s_outp = [S(f"s_outp{i}") for i in range(6)]
            dmg_hist = []
            out_hist = []

            def ffn_app(app, tiles, xwait, bufs):
                uT, u_f, u_tm = bufs["uT"], bufs["u_f"], bufs["u_tm"]
                vbuf, t2, vq = bufs["vbuf"], bufs["t2"], bufs["vq"]
                wgup, wdb, silub = bufs["wgup"], bufs["wdb"], bufs["silu"]
                vqT, outf = bufs["vqT"], bufs["outf"]
                groups = [tiles[i:i + GA] for i in range(0, len(tiles), GA)]

                def rv(fn, waits=(), inc=None):
                    ws = list(waits) + [(s_row, s_row.n)]
                    ics = [s_row] + ([] if inc is None else
                                     (inc if isinstance(inc, list)
                                      else [inc]))
                    P.op("vector", fn, waits=ws, inc=ics)

                def stats_tile(t):
                    wu = (s_utm, s_utm.n) if s_utm.n else None
                    P.op("scalar", lambda e, t=t: e.activation(
                        u_f[:, 0, :], xs[:, t, :], AF.Square,
                        accum_out=cols["s2"][:, t:t + 1]),
                        waits=list(xwait(t)) + [wu, (s_sqs, s_sqs.n)],
                        inc=s_sqs)
                    yield
                    P.op("scalar", lambda e, t=t: e.activation(
                        u_f[:, 0, :], xs[:, t, :], AF.Identity,
                        accum_out=cols["s1"][:, t:t + 1]),
                        waits=list(xwait(t)) + [(s_sqs, s_sqs.n)],
                        inc=s_sqs)
                    yield
                    P.op("vector", lambda e, t=t: e.reduce_max(
                        cols["mx"][:, t:t + 1], xs[:, t, :], AX.X),
                        waits=list(xwait(t)))
                    yield
                    P.op("vector", lambda e, t=t: e.tensor_reduce(
                        cols["mn"][:, t:t + 1], xs[:, t, :], AX.X,
                        ALU.min), inc=s_sq)
                    yield

                def rowmath_gen(grp, v_sq, v_sqs):
                    G = len(grp)
                    sl = (slice(None), slice(grp[0], grp[0] + G))
                    rv(lambda e, sl=sl: e.tensor_scalar(cols["m"][sl], cols["s1"][sl],
                                                 1.0 / H, None, ALU.mult),
                       waits=[(s_sq, v_sq), (s_sqs, v_sqs)])
                    yield
                    rv(lambda e, sl=sl: e.tensor_scalar(cols["ve"][sl], cols["s2"][sl],
                                                 1.0 / H, None, ALU.mult))
                    yield
                    rv(lambda e, sl=sl: e.tensor_tensor(cols["t0"][sl], cols["m"][sl],
                                                 cols["m"][sl], ALU.mult))
                    yield
                    rv(lambda e, sl=sl: e.tensor_scalar(cols["t0"][sl], cols["t0"][sl],
                                                 -1.0, 1e-5, ALU.mult, ALU.add))
                    yield
                    rv(lambda e, sl=sl: e.tensor_tensor(cols["ve"][sl], cols["ve"][sl],
                                                 cols["t0"][sl], ALU.add))
                    yield
                    rv(lambda e, sl=sl: e.reciprocal(cols["rq"][sl], cols["ve"][sl]))
                    yield
                    v_row = s_row.n
                    P.op("scalar", lambda e, sl=sl: e.activation(
                        cols["r"][sl], cols["rq"][sl], AF.Sqrt),
                        waits=[(s_row, v_row)], inc=s_sqrt)
                    yield
                    v_sqrt = s_sqrt.n
                    rv(lambda e, sl=sl: e.tensor_tensor(cols["t0"][sl], cols["mx"][sl],
                                                 cols["m"][sl], ALU.subtract),
                       waits=[(s_sqrt, v_sqrt)])
                    yield
                    rv(lambda e, sl=sl: e.tensor_tensor(cols["t1"][sl], cols["m"][sl],
                                                 cols["mn"][sl], ALU.subtract))
                    yield
                    rv(lambda e, sl=sl: e.tensor_tensor(cols["mab"][sl], cols["t0"][sl],
                                                 cols["t1"][sl], ALU.max))
                    yield
                    rv(lambda e, sl=sl: e.tensor_scalar(cols["t0"][sl], cols["mab"][sl],
                                                 1e-12, None, ALU.max))
                    yield
                    rv(lambda e, sl=sl: e.reciprocal(cols["t1"][sl], cols["t0"][sl]))
                    yield
                    rv(lambda e, sl=sl: e.tensor_scalar(cols["sp"][sl], cols["t1"][sl],
                                                 127.0, None, ALU.mult))
                    yield
                    rv(lambda e, sl=sl: e.tensor_tensor(cols["t0"][sl], cols["m"][sl],
                                                 cols["sp"][sl], ALU.mult))
                    yield
                    rv(lambda e, sl=sl: e.tensor_scalar(cols["nb"][sl], cols["t0"][sl],
                                                 -1.0, None, ALU.mult))
                    yield
                    rv(lambda e, sl=sl: e.tensor_tensor(cols["q"][sl], cols["r"][sl],
                                                 cols["mab"][sl], ALU.mult))
                    yield
                    rv(lambda e, sl=sl: e.tensor_scalar(cols["cg"][sl], cols["q"][sl],
                                                 gww[:, 0:1], None, ALU.mult))
                    yield
                    rv(lambda e, sl=sl: e.tensor_scalar(cols["cu"][sl], cols["q"][sl],
                                                 gww[:, 1:2], None, ALU.mult))
                    yield

                def head_gen(grp, out):
                    for ti, t in enumerate(grp):
                        yield from stats_tile(t)
                    yield from rowmath_gen(grp, s_sq.n, s_sqs.n)
                    v_row = s_row.n
                    pend = []
                    for ti, t in enumerate(grp):
                        P.op("scalar", lambda e, t=t: e.activation(
                            u_f[:, 0, :], xs[:, t, :], AF.Identity,
                            bias=cols["nb"][:, t:t + 1],
                            scale=cols["sp"][:, t:t + 1]),
                            waits=[(s_row, v_row),
                                   (s_utm, s_utm.n)
                                   if s_utm.n else None],
                            inc=s_ufw)
                        yield
                        v_ufw = s_ufw.n
                        wtr = (s_uT, uT_rd[ti]) if uT_rd[ti] else None
                        P.op("vector", lambda e, ti=ti:
                             e.tensor_scalar(
                                 u_tm[:, ti, :], u_f[:, 0, :], MAGIC,
                                 MAGIC, ALU.add, ALU.subtract),
                             waits=[(s_ufw, v_ufw), wtr], inc=s_utm)
                        pend.append((ti, s_utm.n))
                        yield
                    out["pend"] = pend

                def emit_utp(grp, pend):
                    uT_base = s_uT.n
                    for ti, (slot, v_utm) in enumerate(pend):
                        wprev = (s_gup, gup_hist[-1]) if gup_hist else None
                        wchain = (s_uT, s_uT.n) if s_uT.n else None
                        P.op("sync", lambda e, ti=ti, slot=slot:
                             e.dma_start_transpose(
                                 uT[:, ti, :, :],
                                 u_tm[:, slot, :].rearrange(
                                     "p (c f) -> p c f", f=128)),
                             waits=[(s_utm, v_utm), wprev, wchain],
                             inc=s_uT, dma=True)
                        uT_rd[slot] = s_uT.n
                    return uT_base

                head = {}
                pard_done = {0: None, 1: None}
                for gi, grp in enumerate(groups):
                    G = len(grp)
                    v_dmm_prev = s_dmg.n
                    nhs = {}
                    ngen = None
                    if gi not in head:
                        hstate = {}
                        for _ in head_gen(grp, hstate):
                            pass
                        uT_base = emit_utp(grp, hstate["pend"])
                    else:
                        uT_base = head[gi]
                    if gi + 1 < len(groups) \
                            and not (app == 1 and gi == 0):
                        ngen = head_gen(groups[gi + 1], nhs)

                    vqt_done = {}

                    vtt_tile = {}

                    def emit_vchain(ti, t):
                        tc = (slice(None), slice(t, t + 1))
                        rv(lambda e, ti=ti, tc=tc: e.tensor_reduce(
                            cols["mt"][tc], vmax[:, ti, :], AX.X, ALU.max),
                           waits=[(s_vtt, vtt_tile[ti])])
                        rv(lambda e, tc=tc: e.tensor_scalar(
                            cols["t0"][tc], cols["mt"][tc], 1e-12, None,
                            ALU.max))
                        rv(lambda e, tc=tc: e.reciprocal(
                            cols["t1"][tc], cols["t0"][tc]))
                        rv(lambda e, tc=tc: e.tensor_scalar(
                            cols["f"][tc], cols["t1"][tc], 127.0, None,
                            ALU.mult))
                        rv(lambda e, tc=tc: e.tensor_tensor(
                            cols["t0"][tc], cols["cu"][tc], cols["mt"][tc],
                            ALU.mult))
                        rv(lambda e, tc=tc: e.tensor_scalar(
                            cols["cd"][tc], cols["t0"][tc], gww[:, 2:3], None,
                            ALU.mult), inc=s_row2)
                        v_row2_t = s_row2.n
                        wvq_h = [(s_vqtp[0], vq_hrd[0]) if vq_hrd[0] else None,
                                 (s_vqtp[0], vq_hrd[1]) if vq_hrd[1] else None]
                        for q8 in range(8):
                            pt2 = s_t2.n % 2
                            P.op("scalar", lambda e, ti=ti, t=t, q8=q8,
                                 pt2=pt2:
                                 e.activation(
                                     t2[:, pt2, :],
                                     vbuf[:, ti, q8 * 512:(q8 + 1) * 512],
                                     AF.Identity,
                                     scale=cols["f"][:, t:t + 1]),
                                 waits=[(s_row2, v_row2_t), (smem, v_mem),
                                        (s_vq8, s_t2.n - 1)
                                        if s_t2.n >= 2 else None,
                                        wvq_h[q8 // 4]
                                        if q8 % 4 == 0 else None],
                                 inc=s_t2)
                            v_t2 = s_t2.n
                            P.op("vector", lambda e, q8=q8, pt2=pt2:
                                 e.tensor_scalar(
                                     vq[:, 0, q8 * 512:(q8 + 1) * 512],
                                     t2[:, pt2, :], MAGIC, MAGIC,
                                     ALU.add, ALU.subtract),
                                 waits=[(s_t2, v_t2)], inc=s_vq8)
                            if q8 == 3 or q8 == 7:
                                hh = q8 // 4
                                v_vq8 = s_vq8.n
                                wch = (s_vqtp[0], s_vqtp[0].n) \
                                    if s_vqtp[0].n else None
                                P.op("sync", lambda e, ti=ti, hh=hh:
                                     e.dma_start_transpose(
                                         vqT[:, ti, 16 * hh:16 * hh + 16, :],
                                         vq[:, 0, 2048 * hh:2048 * hh + 2048]
                                         .rearrange("p (c f) -> p c f",
                                                    f=128)),
                                     waits=[(s_vq8, v_vq8), wch],
                                     inc=s_vqtp[0], dma=True)
                                vq_hrd[hh] = s_vqtp[0].n
                        vqt_done[ti] = (s_vqtp[0], s_vqtp[0].n)

                    for oc in range(8):
                        par = oc % 2
                        wclob = (s_gup, gup_hist[-2]) if len(gup_hist) >= 2 \
                            else None
                        if app == 1 and (gi == 0 or DBG_QUANT_ALL):
                            vg = None
                            for wi, (wv, scr) in enumerate(((wgT_q, scr_g),
                                                            (wuT_q, scr_u))):
                                for hh in range(2):
                                    vg = quant_chunk(
                                        wv[:, 4 * hh:4 * hh + 4, oc, :],
                                        wgup[:, par, wi, 4 * hh:4 * hh + 4, :],
                                        scr[oc][:, 4 * hh:4 * hh + 4, :],
                                        wi, extra_w=wclob)
                            wq_wait = [(sqdve, vg)]
                        else:
                            P.op("gpsimd", lambda e, oc=oc, par=par:
                                 e.dma_start(
                                     wgup[:, par, 0, :, :], scr_g[oc]),
                                 waits=[wclob, (sscrw, sscrw.n)],
                                 inc=s_wqlp[par], dma=True)
                            P.op("gpsimd", lambda e, oc=oc, par=par:
                                 e.dma_start(
                                     wgup[:, par, 1, :, :], scr_u[oc]),
                                 inc=s_wqlp[par], dma=True)
                            wq_wait = [(s_wqlp[par], s_wqlp[par].n)]
                        for ti, t in enumerate(grp):
                            pp = s_gup.n % 3
                            wg_, wu_ = pacq(2 * pp), pacq(2 * pp + 1)
                            for icc in range(8):
                                wts = (wq_wait +
                                       [(s_uT, uT_base + 16 * (ti + 1)),
                                        wg_, wu_] if icc == 0 else [])
                                P.op("tensor", lambda e, icc=icc, ti=ti,
                                     par=par, pp=pp:
                                     e.matmul(ppool[:, 2 * pp, :],
                                              uT[:, ti, icc, :],
                                              wgup[:, par, 0, icc, :],
                                              start=(icc == 0), stop=(icc == 7)),
                                     waits=wts)
                                P.op("tensor", lambda e, icc=icc, ti=ti,
                                     par=par, pp=pp:
                                     e.matmul(ppool[:, 2 * pp + 1, :],
                                              uT[:, ti, icc, :],
                                              wgup[:, par, 1, icc, :],
                                              start=(icc == 0), stop=(icc == 7)),
                                     inc=(s_gup if icc == 7 else None))
                            v_gup = s_gup.n
                            sp_ = s_silu.n % 2
                            P.op("scalar", lambda e, t=t, pp=pp, sp_=sp_:
                                 e.activation(silub[:, sp_, :],
                                              ppool[:, 2 * pp, :], AF.Silu,
                                              scale=cols["cg"][:, t:t + 1]),
                                 waits=[(s_gup, v_gup),
                                        (s_vw, s_vw.n - 1)
                                        if s_silu.n >= 2 else None],
                                 inc=s_silu)
                            v_silu = s_silu.n
                            P.op("vector", lambda e, ti=ti, oc=oc, pp=pp,
                                 sp_=sp_:
                                 e.tensor_tensor(
                                     vbuf[:, ti, oc * 512:(oc + 1) * 512],
                                     silub[:, sp_, :], ppool[:, 2 * pp + 1, :],
                                     ALU.mult),
                                 waits=[(s_silu, v_silu),
                                        (s_dmg, v_dmm_prev)],
                                 inc=[s_vw, psl[2 * pp], psl[2 * pp + 1]])
                            P.op("vector", lambda e, ti=ti, oc=oc:
                                 e.reduce_max(
                                     vmax[:, ti, oc:oc + 1],
                                     vbuf[:, ti, oc * 512:(oc + 1) * 512],
                                     AX.X, apply_absolute_value=True),
                                 waits=[(s_vw, s_vw.n)], inc=s_vtt)
                            if oc == 7:
                                vtt_tile[ti] = s_vtt.n
                            if ngen is not None and oc >= 1:
                                for _ in range(1 if oc < 5 else 2):
                                    if next(ngen, "END") == "END":
                                        ngen = None
                                        break
                        gup_hist.append(s_gup.n)


                    for ti, t in enumerate(grp):
                        emit_vchain(ti, t)

                    preload = {}
                    if ngen is not None:
                        for _ in ngen:
                            pass
                        ngen = None
                    if "pend" in nhs:
                        head[gi + 1] = emit_utp(groups[gi + 1], nhs["pend"])

                    dmg_tile = {}
                    dwn_tile = {}
                    for hc in range(2):
                        if hc == 0 and G > 4:
                            sched = [(0, list(range(0, 4))),
                                     (1, list(range(0, 4))),
                                     (0, list(range(4, G))),
                                     (1, list(range(4, G))),
                                     (2, list(range(G))),
                                     (3, list(range(G)))]
                        else:
                            sched = [(icg, list(range(G)))
                                     for icg in range(4)]
                        loaded = {}
                        done_cnt = {k: 0 for k in range(4)}
                        for icg, tis in sched:
                            if not tis:
                                continue
                            pard = (hc * 4 + icg) % 2
                            first_chunk = icg not in loaded
                            if first_chunk:
                                wdclob = (s_dmg, pard_done[pard]) \
                                    if pard_done[pard] else None
                                if (hc, icg) in preload:
                                    loaded[icg] = preload[(hc, icg)]
                                elif app == 1 and (gi == 0
                                                   or DBG_QUANT_ALL):
                                    vg = None
                                    for hh in range(2):
                                        vg = quant_chunk(
                                            wdT_q[:, icg * 8 + 4 * hh:
                                                  icg * 8 + 4 * hh + 4,
                                                  hc * 512:(hc + 1) * 512],
                                            wdb[:, pard,
                                                4 * hh:4 * hh + 4, :],
                                            scr_d[hc, icg]
                                            [:, 4 * hh:4 * hh + 4, :],
                                            2, extra_w=wdclob)
                                    loaded[icg] = [(sqdve, vg)]
                                else:
                                    P.op("gpsimd", lambda e, hc=hc, icg=icg,
                                         pard=pard: e.dma_start(
                                             wdb[:, pard, :, :],
                                             scr_d[hc, icg]),
                                         waits=[wdclob, (sscrw, sscrw.n)],
                                         inc=s_wdp[pard], dma=True)
                                    loaded[icg] = \
                                        [(s_wdp[pard], s_wdp[pard].n)]
                            for ti in tis:
                                for icc in range(8):
                                    wts = []
                                    if icc == 0:
                                        if ti == tis[0] and first_chunk:
                                            wts += loaded[icg]
                                        if icg == 0 and hc == 0:
                                            wts += [vqt_done[ti], pacq(ti)]
                                        if icg == 0 and hc == 1:
                                            wts.append(
                                                (s_dwn, dwn_tile[ti]))
                                    st = (icg == 0 and icc == 0)
                                    sp2 = (icg == 3 and icc == 7)
                                    incs = [s_dmg] if icc == 7 else None
                                    P.op("tensor", lambda e, icc=icc, ti=ti,
                                         icg=icg, pard=pard, st=st, sp2=sp2:
                                         e.matmul(ppool[:, ti, :],
                                                  vqT[:, ti, icg * 8 + icc, :],
                                                  wdb[:, pard, icc, :],
                                                  start=st, stop=sp2,
                                                  skip_group_check=True),
                                         waits=wts, inc=incs)
                                if icg == 3:
                                    dmg_tile[ti] = s_dmg.n
                            done_cnt[icg] += len(tis)
                            if done_cnt[icg] == G:
                                pard_done[pard] = s_dmg.n
                        dmg_hist.append(s_dmg.n)
                        for ti, t in enumerate(grp):
                            last = (hc == 1)
                            if app == 1:
                                P.op("vector", lambda e, ti=ti, t=t, hc=hc:
                                     e.scalar_tensor_tensor(
                                         xs[:, t, hc * 512:(hc + 1) * 512],
                                         ppool[:, ti, :],
                                         cols["cd"][:, t:t + 1],
                                         xs[:, t, hc * 512:(hc + 1) * 512],
                                         ALU.mult, ALU.add),
                                     waits=[(s_dmg, dmg_tile[ti])],
                                     inc=([s_dwn, psl[ti]] if last
                                          else [s_dwn]))
                            else:
                                gidx = gi * GA + ti
                                slot = gidx % 3
                                strm = (gidx % 3) + 3 * hc
                                wout = (s_outp[strm],
                                        out_store.get((gidx - 3, hc), 0)) \
                                    if gidx >= 3 else None
                                P.op("vector", lambda e, ti=ti, t=t, hc=hc,
                                     slot=slot: e.scalar_tensor_tensor(
                                         bufs["outf"][:, slot,
                                                      hc * 512:(hc + 1) * 512],
                                         ppool[:, ti, :],
                                         cols["cd"][:, t:t + 1],
                                         xs[:, t, hc * 512:(hc + 1) * 512],
                                         ALU.mult, ALU.add),
                                     waits=[(s_dmg, dmg_tile[ti]), wout],
                                     inc=([s_dwn, psl[ti]] if last
                                          else [s_dwn]))
                                out_t = out.rearrange("(t p) h -> t p h",
                                                      p=128)
                                P.op("sync", lambda e, t=t, slot=slot, hc=hc:
                                     e.dma_start(
                                         out_t[t - 2][:, hc * 512:(hc + 1) * 512],
                                         bufs["outf"][:, slot,
                                                      hc * 512:(hc + 1) * 512]),
                                     waits=[(s_dwn, s_dwn.n)],
                                     inc=s_outp[strm], dma=True)
                                out_store[(gidx, hc)] = s_outp[strm].n
                            if hc == 0:
                                dwn_tile[ti] = s_dwn.n

            ffn_app(1, list(range(NT)), lambda t: [(sx, 16 * NT)], bufs)

            sbar = S("sbar")

            def barrier():
                base = sbar.n
                for e in Prog.ENGS:
                    P.op(e, lambda x, e=e: P.carrier(e, x), inc=sbar)
                for e in Prog.ENGS:
                    P.op(e, lambda x: x.nop(), waits=[(sbar, base + 5)])

            if debug_taps:
                barrier()
                dxm = taps_out["dbg_xmid"].rearrange("(t p) h -> t p h", p=128)
                sdbg = S("sdbg")
                for t in range(NT):
                    P.op("gpsimd", lambda e, t=t: e.dma_start(dxm[t], xs[:, t, :]),
                         inc=sdbg, dma=True)
                P.op("gpsimd", lambda e: e.nop(), waits=[(sdbg, 16 * NT)])
            barrier()

        with ExitStack() as bctx:
            zf = sb(bctx, "zf", [128, ZR, CH], BF16)
            winT = sb(bctx, "winT", [128, 8, CH], BF16)
            woutT = sb(bctx, "woutT", [128, 16, H], BF16)
            tabs = sb(bctx, "tabs", [128, 5, CH], BF16)
            Csb = sb(bctx, "Csb", [128, 4, 4, 128], BF16)
            iDsb = sb(bctx, "iDsb", [128, 4, 2, 128], BF16)
            bjnk = sb(bctx, "bjnk", [128, H], BF16)
            brows = sb(bctx, "brows", [128, CH], BF16)
            binb = brows[0:1, :]
            bab = brows[32:33, :]
            boutb = brows[64:65, 0:H]
            u2tm = sb(bctx, "u2tm", [128, 1, H], BF16)
            u2T = sb(bctx, "u2T", [128, 2, 8, 128], BF16)

            s_pm = S("s_pm")
            u2Tp = [S("u2Tp0"), S("u2Tp1")]
            sB = {k: S("b_" + k) for k in
                  ("cload", "wint", "woutt", "tapt", "tab", "tabev", "zrow",
                   "zsqrt", "u2", "u2t", "u2T", "zmm", "zev", "fwd", "fev",
                   "pch", "idft", "sig", "gat", "catp", "catev", "omm", "oev",
                   "stat")}

            P.op("gpsimd", lambda e: e.dma_start(Csb[:, :, :, :], Cd),
                 inc=sB["cload"], dma=True)
            P.op("gpsimd", lambda e: e.dma_start(iDsb[:, :, :, :], iDd),
                 inc=sB["cload"], dma=True)
            P.op("gpsimd", lambda e: e.dma_start(binb[:, :], b_in),
                 inc=sB["cload"], dma=True)
            P.op("gpsimd", lambda e: e.dma_start(bab[:, 0:H], b_a),
                 inc=sB["cload"], dma=True)
            P.op("gpsimd", lambda e: e.dma_start(bab[:, H:CH], b_b),
                 inc=sB["cload"], dma=True)
            P.op("gpsimd", lambda e: e.dma_start(boutb[:, :], b_out),
                 inc=sB["cload"], dma=True)
            v_cload = sB["cload"].n

            with ExitStack() as wctx:
                wstg = sb(wctx, "wstg", [128, 2, 4096], BF16)
                tapb = sb(wctx, "tapb", [128, 16, 256], BF16)
                tapT = sb(wctx, "tapT", [128, 2, CH], BF16)
                D2sb = sb(wctx, "D2sb", [128, 2, 5, 128], BF16)
                sldp = [S("b_sld0"), S("b_sld1")]
                sD2 = S("b_sD2")
                staps = S("b_staps")
                wintp = [S("b_wint0"), S("b_wint1")]
                wouttp = [S("b_woutt0"), S("b_woutt1")]
                P.op("gpsimd", lambda e: e.dma_start(D2sb[:, :, :, :], D2d),
                     inc=sD2, dma=True)
                w_in_r = w_in.rearrange("(g p) h -> p g h", p=128)
                w_out_r = w_out.rearrange("(g p) c -> p g c", p=128)
                wstg_v = [wstg[:, 0, :].rearrange("p (g h) -> p g h", h=H),
                          wstg[:, 1, :].rearrange("p (g h) -> p g h", h=H)]
                tpar_hist = {0: [], 1: []}
                for pc in range(4):
                    par = pc % 2
                    w = (wintp[par], tpar_hist[par][-1]) \
                        if tpar_hist[par] else None
                    P.op("gpsimd", lambda e, pc=pc, par=par: e.dma_start(
                        wstg_v[par], w_in_r[:, 4 * pc:4 * pc + 4, :]),
                        waits=[w], inc=sldp[par], dma=True)
                    v_ld = sldp[par].n
                    for g in range(4):
                        cc = 4 * pc + g
                        P.op("sync", lambda e, par=par, cc=cc, g=g:
                             e.dma_start_transpose(
                                 winT[:, :, cc * 128:(cc + 1) * 128],
                                 wstg[:, par, g * H:(g + 1) * H]
                                 .rearrange("p (i f) -> p i f", f=128)),
                             waits=[(sldp[par], v_ld)],
                             inc=wintp[par], dma=True)
                    tpar_hist[par].append(wintp[par].n)
                opar_hist = {0: [], 1: []}
                for pc in range(4):
                    par = pc % 2
                    w1 = (wintp[par], tpar_hist[par][-1])
                    w2 = (wouttp[par], opar_hist[par][-1]) \
                        if opar_hist[par] else None
                    P.op("gpsimd", lambda e, pc=pc, par=par: e.dma_start(
                        wstg[:, par, :].rearrange("p (g c) -> p g c", c=CH),
                        w_out_r[:, 2 * pc:2 * pc + 2, :]),
                        waits=[w1, w2], inc=sldp[par], dma=True)
                    v_ld = sldp[par].n
                    for g in range(2):
                        hc = 2 * pc + g
                        P.op("sync", lambda e, par=par, hc=hc, g=g:
                             e.dma_start_transpose(
                                 woutT[:, :, hc * 128:(hc + 1) * 128],
                                 wstg[:, par, g * CH:(g + 1) * CH]
                                 .rearrange("p (i f) -> p i f", f=128)),
                             waits=[(sldp[par], v_ld)],
                             inc=wouttp[par], dma=True)
                    opar_hist[par].append(wouttp[par].n)
                wa_r = wa.rearrange("(g p) k -> p g k", p=128)
                wb_r = wb.rearrange("(g p) k -> p g k", p=128)
                P.op("gpsimd", lambda e: e.dma_start(tapb[:, 0:8, :], wa_r),
                     inc=staps, dma=True)
                P.op("gpsimd", lambda e: e.dma_start(tapb[:, 8:16, :], wb_r),
                     inc=staps, dma=True)
                v_tapl = staps.n
                for g in range(16):
                    P.op("sync", lambda e, g=g: e.dma_start_transpose(
                        tapT[:, :, g * 128:(g + 1) * 128],
                        tapb[:, g, :].rearrange("p (j f) -> p j f", f=128)),
                        waits=[(staps, v_tapl)], inc=sB["tapt"], dma=True)
                v_tapt = sB["tapt"].n
                for tb in range(5):
                    for fc in range(4):
                        slw = pacq(6)
                        for jc in range(2):
                            P.op("tensor", lambda e, tb=tb, fc=fc, jc=jc:
                                 e.matmul(ppool[:, 6, :],
                                          D2sb[:, jc, tb, :],
                                          tapT[:, jc, fc * 512:(fc + 1) * 512],
                                          start=(jc == 0), stop=(jc == 1)),
                                 waits=([(sB["tapt"], v_tapt),
                                         (sD2, 16), slw]
                                        if jc == 0 else []),
                                 inc=(sB["tab"] if jc == 1 else None))
                        v_tab = sB["tab"].n
                        P.op("vector", lambda e, tb=tb, fc=fc: e.tensor_copy(
                            tabs[:, tb, fc * 512:(fc + 1) * 512],
                            ppool[:, 6, :]),
                            waits=[(sB["tab"], v_tab)],
                            inc=[psl[6], sB["tabev"]])
                v_tabev = sB["tabev"].n
                win_waits = [(wintp[0], wintp[0].n), (wintp[1], wintp[1].n)]
                wout_waits = [(wouttp[0], wouttp[0].n),
                              (wouttp[1], wouttp[1].n)]
                s_bdrain = S("s_bdrain")
                P.op("sync", lambda e: e.nop(),
                     waits=[(sB["tapt"], sB["tapt"].n),
                            (wintp[0], wintp[0].n), (wintp[1], wintp[1].n),
                            (wouttp[0], wouttp[0].n),
                            (wouttp[1], wouttp[1].n)], inc=s_bdrain)
                P.op("tensor", lambda e: e.nop(),
                     waits=[(sB["tabev"], sB["tabev"].n)], inc=s_bdrain)
                s_wprep = S("s_wprep")
                P.op("sync", lambda e: e.nop(),
                     waits=[(sB["tapt"], sB["tapt"].n),
                            (wintp[0], wintp[0].n), (wintp[1], wintp[1].n),
                            (wouttp[0], wouttp[0].n), (wouttp[1], wouttp[1].n),
                            (sldp[0], sldp[0].n), (sldp[1], sldp[1].n),
                            (sD2, sD2.n), (staps, staps.n)], inc=s_wprep)

            if True:
                with ExitStack() as pctx:
                    zh = sb(pctx, "zh", [128, 4, CH], BF16)
                    Pb = sb(pctx, "Pb", [128, 4, CH], BF16)
                    ptmp = sb(pctx, "ptmp", [128, CH], BF16)
                    sigb = sb(pctx, "sigb", [128, 2, 512], BF16)
                    gat = sb(pctx, "gat", [128, CH], BF16)
                    catT = sb(pctx, "catT", [128, 16, 128], BF16)

                    fwd_done_after_block = []
                    idft_done_after_block = []
                    zmm_hist = []
                    head_tp = {}
                    pch_hist = []
                    pairctr = [0]
                    u2tp_last = [None]

                    def tile_head(t):
                        pf = sB["u2"].n % 2
                        tslc = (slice(None), slice(t, t + 1))
                        P.op("scalar", lambda e, t=t, tslc=tslc:
                             e.activation(
                                 bjnk[:, :], xs[:, t, :], AF.Square,
                                 accum_out=cols["s2"][tslc]),
                             waits=[(sB["stat"], sB["stat"].n),
                                    (s_wprep, 1)],
                             inc=sB["stat"])
                        def rz(fn, waits=(), inc=None):
                            ws = list(waits) + [(sB["zrow"], sB["zrow"].n)]
                            ics = [sB["zrow"]] + ([] if inc is None else [inc])
                            P.op("vector", fn, waits=ws, inc=ics)
                        rz(lambda e, t=t, tslc=tslc: e.reduce_sum(
                            cols["s1"][tslc], xs[:, t, :], AX.X))
                        rz(lambda e, tslc=tslc: e.tensor_scalar(
                            cols["m"][tslc], cols["s1"][tslc], 1.0 / H, None,
                            ALU.mult))
                        rz(lambda e, tslc=tslc: e.tensor_scalar(
                            cols["ve"][tslc], cols["s2"][tslc], 1.0 / H, None,
                            ALU.mult), waits=[(sB["stat"], sB["stat"].n)])
                        rz(lambda e, tslc=tslc: e.tensor_tensor(
                            cols["t0"][tslc], cols["m"][tslc], cols["m"][tslc],
                            ALU.mult))
                        rz(lambda e, tslc=tslc: e.tensor_scalar(
                            cols["t0"][tslc], cols["t0"][tslc], -1.0, 1e-5,
                            ALU.mult, ALU.add))
                        rz(lambda e, tslc=tslc: e.tensor_tensor(
                            cols["ve"][tslc], cols["ve"][tslc], cols["t0"][tslc],
                            ALU.add))
                        rz(lambda e, tslc=tslc: e.reciprocal(
                            cols["rq"][tslc], cols["ve"][tslc]))
                        v_zr = sB["zrow"].n
                        P.op("scalar", lambda e, tslc=tslc: e.activation(
                            cols["r"][tslc], cols["rq"][tslc], AF.Sqrt),
                            waits=[(sB["zrow"], v_zr)], inc=sB["zsqrt"])
                        v_zs = sB["zsqrt"].n
                        rz(lambda e, tslc=tslc: e.tensor_tensor(
                            cols["t0"][tslc], cols["m"][tslc], cols["r"][tslc],
                            ALU.mult), waits=[(sB["zsqrt"], v_zs)])
                        rz(lambda e, tslc=tslc: e.tensor_scalar(
                            cols["nb"][tslc], cols["t0"][tslc], -1.0, None,
                            ALU.mult))
                        v_zr = sB["zrow"].n
                        P.op("scalar", lambda e, t=t: e.activation(
                            u2tm[:, 0, :], xs[:, t, :], AF.Identity,
                            bias=cols["nb"][tslc], scale=cols["r"][tslc]),
                            waits=[(sB["zrow"], v_zr), u2tp_last[0]],
                            inc=sB["u2"])
                        v_u2 = sB["u2"].n
                        P.op("sync", lambda e, pf=pf: e.dma_start_transpose(
                            u2T[:, pf, :, :],
                            u2tm[:, 0, :].rearrange("p (c f) -> p c f",
                                                    f=128)),
                            waits=[(sB["u2"], v_u2),
                                   (sB["zmm"], zmm_hist[-1])
                                   if len(zmm_hist) >= 1 else None],
                            inc=u2Tp[pf], dma=True)
                        u2tp_last[0] = (u2Tp[pf], u2Tp[pf].n)
                        head_tp[t] = (u2Tp[pf].n, pf)

                    def tile_zproj(t):
                        v_u2T, pf = head_tp[t]
                        for cc in range(4):
                            sidx = sB["zmm"].n % 2
                            zw = pacq(sidx)
                            for icc in range(8):
                                wts = []
                                if icc == 0:
                                    wts = [(u2Tp[pf], v_u2T),
                                           (sB["cload"], v_cload), zw] \
                                        + win_waits
                                P.op("tensor", lambda e, icc=icc, cc=cc,
                                     sidx=sidx, pf=pf:
                                     e.matmul(ppool[:, sidx, :],
                                              u2T[:, pf, icc, :],
                                              winT[:, icc,
                                                   cc * 512:(cc + 1) * 512],
                                              start=(icc == 0), stop=False),
                                     waits=wts)
                            P.op("tensor", lambda e, cc=cc, sidx=sidx:
                                 e.matmul(ppool[:, sidx, :], ones1[0:1, :],
                                          binb[:, cc * 512:(cc + 1) * 512],
                                          start=False, stop=True),
                                 waits=[(smem, v_mem)], inc=sB["zmm"])
                            v_zmm = sB["zmm"].n
                            slot = t % ZR
                            if t < 2 and not DBG_NOMASK:
                                P.op("vector", lambda e, slot=slot, cc=cc,
                                     sidx=sidx: e.tensor_scalar(
                                         zf[:, slot, cc * 512:(cc + 1) * 512],
                                         ppool[:, sidx, :], zmk[:, :], None,
                                         ALU.mult),
                                     waits=[(sB["zmm"], v_zmm),
                                            (szmk, 16)],
                                     inc=[psl[sidx], sB["zev"]])
                            else:
                                P.op("scalar", lambda e, slot=slot, cc=cc,
                                     sidx=sidx: e.activation(
                                         zf[:, slot, cc * 512:(cc + 1) * 512],
                                         ppool[:, sidx, :], AF.Identity),
                                     waits=[(sB["zmm"], v_zmm)],
                                     inc=[psl[sidx], sB["zev"]])
                        zmm_hist.append(sB["zmm"].n)

                    def emit_block_gen(b):
                        bd = (s_bdrain, 2) if b == 0 else None
                        v_zev_need = 4 * (2 * b + 4)
                        pair_specs = [
                            ((0, 2), [(0, 0, 0, 1, 2, ALU.subtract),
                                      (2, 1, 4, 0, 2, ALU.add)]),
                            ((1, 3), [(1, 0, 1, 1, 3, ALU.subtract),
                                      (3, 1, 1, 0, 3, ALU.add)]),
                        ]
                        pdata = []
                        for pk, ((poA, poB), steps) in enumerate(pair_specs):
                            zbase = (pairctr[0] % 2) * 2
                            pairctr[0] += 1
                            wzh = (sB["pch"], pch_hist[-2 + pk]) \
                                if len(pch_hist) >= 2 else None
                            for zi, po in enumerate((poA, poB)):
                                for fc in range(4):
                                    sidx = 2 + sB["fwd"].n % 2
                                    fw = pacq(sidx)
                                    for icc in range(4):
                                        wts = []
                                        if icc == 0:
                                            wts = [(sB["zev"], v_zev_need),
                                                   (sB["cload"], v_cload), fw]
                                        P.op("tensor", lambda e, icc=icc,
                                             po=po, fc=fc, b=b, sidx=sidx:
                                             e.matmul(
                                                 ppool[:, sidx, :],
                                                 Csb[:, icc, po, :],
                                                 zf[:, (2 * b + icc) % ZR,
                                                    fc * 512:(fc + 1) * 512],
                                                 start=(icc == 0),
                                                 stop=(icc == 3)),
                                             waits=wts,
                                             inc=(sB["fwd"] if icc == 3
                                                  else None))
                                    v_fwd = sB["fwd"].n
                                    P.op("scalar", lambda e, zi=zi, fc=fc,
                                         sidx=sidx, zbase=zbase:
                                         e.activation(
                                             zh[:, zbase + zi,
                                                fc * 512:(fc + 1) * 512],
                                             ppool[:, sidx, :], AF.Identity),
                                         waits=[(sB["fwd"], v_fwd), bd,
                                                wzh if fc == 0 and zi == 0
                                                else None],
                                         inc=[psl[sidx], sB["fev"]])
                            pdata.append((zbase, steps, sB["fev"].n))
                        yield
                        v_idft_prev = idft_done_after_block[-1] \
                            if idft_done_after_block else 0
                        for zbase, steps, v_fev in pdata:
                            for (pidx, za, ta, zb2, tb2, opf) in steps:
                                P.op("vector", lambda e, pidx=pidx, za=za,
                                     ta=ta, zbase=zbase: e.tensor_tensor(
                                         Pb[:, pidx, :],
                                         zh[:, zbase + za, :],
                                         tabs[:, ta, :], ALU.mult),
                                     waits=[(sB["fev"], v_fev), bd,
                                            (sB["tabev"], v_tabev),
                                            (sB["idft"], v_idft_prev),
                                            (s_pm, s_pm.n)], inc=s_pm)
                                P.op("vector", lambda e, zb2=zb2, tb2=tb2,
                                     zbase=zbase:
                                     e.tensor_tensor(ptmp[:, :],
                                                     zh[:, zbase + zb2, :],
                                                     tabs[:, tb2, :],
                                                     ALU.mult),
                                     waits=[(s_pm, s_pm.n)], inc=s_pm)
                                P.op("vector", lambda e, pidx=pidx, opf=opf:
                                     e.tensor_tensor(Pb[:, pidx, :],
                                                     Pb[:, pidx, :],
                                                     ptmp[:, :], opf),
                                     waits=[(s_pm, s_pm.n)],
                                     inc=[s_pm, sB["pch"]])
                            pch_hist.append(sB["pch"].n)
                        v_pch = sB["pch"].n
                        yield

                        for po in range(2):
                            tgt = 2 + 2 * b + po
                            for pc in range(2):
                                wA, wB2 = pacq(4), pacq(5)
                                for half, (sidx, fc) in enumerate(
                                        ((4, pc), (5, pc + 2))):
                                    for icc in range(4):
                                        wts = []
                                        if icc == 0:
                                            wts = [(sB["pch"], v_pch),
                                                   wA if half == 0 else wB2]
                                        P.op("tensor", lambda e, icc=icc,
                                             po=po, fc=fc, sidx=sidx:
                                             e.matmul(
                                                 ppool[:, sidx, :],
                                                 iDsb[:, icc, po, :],
                                                 Pb[:, icc,
                                                    fc * 512:(fc + 1) * 512],
                                                 start=(icc == 0), stop=False),
                                             waits=wts)
                                    P.op("tensor", lambda e, fc=fc, sidx=sidx:
                                         e.matmul(ppool[:, sidx, :],
                                                  ones1[32:33, :],
                                                  bab[:, fc * 512:
                                                      (fc + 1) * 512],
                                                  start=False, stop=True),
                                         inc=sB["idft"])
                                v_idft = sB["idft"].n
                                P.op("scalar", lambda e: e.activation(
                                    sigb[:, 0, :], ppool[:, 4, :], AF.Sigmoid),
                                    waits=[(sB["idft"], v_idft), bd],
                                    inc=sB["sig"])
                                P.op("scalar", lambda e: e.activation(
                                    sigb[:, 1, :], ppool[:, 5, :], AF.Sigmoid),
                                    inc=sB["sig"])
                                v_sig = sB["sig"].n
                                P.op("vector", lambda e, pc=pc: e.tensor_tensor(
                                    gat[:, pc * 512:(pc + 1) * 512],
                                    ppool[:, 4, :], sigb[:, 1, :], ALU.mult),
                                    waits=[(sB["sig"], v_sig), bd,
                                           (sB["catev"], sB["catev"].n)
                                           if pc == 0 else None])
                                P.op("vector", lambda e, pc=pc: e.tensor_tensor(
                                    gat[:, (pc + 2) * 512:(pc + 3) * 512],
                                    ppool[:, 5, :], sigb[:, 0, :], ALU.mult),
                                    inc=[sB["gat"], psl[4], psl[5]])
                            v_gat = sB["gat"].n
                            P.op("sync", lambda e: e.dma_start_transpose(
                                catT[:, :, :],
                                gat[:, :].rearrange("p (c f) -> p c f",
                                                    f=128)),
                                waits=[(sB["gat"], v_gat), bd,
                                       (sB["omm"], sB["omm"].n)],
                                inc=sB["catev"], dma=True)
                            v_catev = sB["catev"].n
                            yield
                            for hc in range(2):
                                ow = pacq(6)
                                for ec in range(16):
                                    wts = []
                                    if ec == 0:
                                        wts = [(sB["catev"], v_catev),
                                               ow] + wout_waits
                                    P.op("tensor", lambda e, ec=ec, hc=hc:
                                         e.matmul(
                                             ppool[:, 6, :],
                                             catT[:, ec, :],
                                             woutT[:, ec,
                                                   hc * 512:(hc + 1) * 512],
                                             start=(ec == 0), stop=False),
                                         waits=wts)
                                P.op("tensor", lambda e, hc=hc: e.matmul(
                                    ppool[:, 6, :], ones1[64:65, :],
                                    boutb[:, hc * 512:(hc + 1) * 512],
                                    start=False, stop=True),
                                    inc=sB["omm"])
                                v_omm = sB["omm"].n
                                P.op("vector", lambda e, tgt=tgt, hc=hc:
                                     e.tensor_tensor(
                                         xs[:, tgt, hc * 512:(hc + 1) * 512],
                                         ppool[:, 6, :],
                                         xs[:, tgt, hc * 512:(hc + 1) * 512],
                                         ALU.add),
                                     waits=[(sB["omm"], v_omm)],
                                     inc=[psl[6], sB["oev"]])
                        idft_done_after_block.append(sB["idft"].n)

                    def do_zproj(t):
                        if t + 1 < NT and (t + 1) not in head_tp:
                            tile_head(t + 1)
                        tile_zproj(t)

                    tile_head(0)
                    for t in range(4):
                        do_zproj(t)
                    NB = (NT - 2) // 2
                    gens = {}
                    for b in range(NB):
                        gens[b] = emit_block_gen(b)
                        next(gens[b])
                        if b >= 1:
                            next(gens[b - 1])
                        if 2 * b + 4 < NT:
                            do_zproj(2 * b + 4)
                        if b >= 1:
                            next(gens[b - 1])
                        if 2 * b + 5 < NT:
                            do_zproj(2 * b + 5)
                        if b >= 1:
                            for _ in gens.pop(b - 1):
                                pass
                        next(gens[b])
                    next(gens[NB - 1])
                    for _ in gens.pop(NB - 1):
                        pass
                    if debug_taps:
                        barrier()
                        dz = taps_out["dbg_z"].rearrange("(t p) c -> t p c",
                                                         p=128)
                        sdbg2 = S("sdbg2")
                        for t in range(NT - ZR, NT):
                            P.op("gpsimd", lambda e, t=t: e.dma_start(
                                dz[t], zf[:, t % ZR, :]), inc=sdbg2, dma=True)
                        P.op("gpsimd", lambda e: e.nop(),
                             waits=[(sdbg2, 16 * ZR)])
                        dx3 = taps_out["dbg_x3"].rearrange(
                            "(t p) h -> t p h", p=128)
                        sdbg3 = S("sdbg3")
                        for t in range(NV):
                            P.op("gpsimd", lambda e, t=t: e.dma_start(
                                dx3[t], xs[:, t + 2, :]), inc=sdbg3, dma=True)
                        P.op("gpsimd", lambda e: e.nop(),
                             waits=[(sdbg3, 16 * NV)])
            barrier()

        with ExitStack() as cctx:
            bufs2 = dict(
                uT=sb(cctx, "uT2", [128, GA, 8, 128], BF16),
                u_f=sb(cctx, "u_f2", [128, 1, H], F32),
                u_tm=sb(cctx, "u_tm2", [128, GA, H], BF16),
                vbuf=sb(cctx, "vbuf2", [128, GA, INNER], BF16),
                t2=sb(cctx, "t22", [128, 2, 512], F32),
                vq=sb(cctx, "vq2", [128, 1, INNER], BF16),
                wgup=sb(cctx, "wgup2", [128, 2, 2, 8, 512], BF16),
                wdb=sb(cctx, "wdb2", [128, 2, 8, 512], BF16),
                silu=sb(cctx, "silu2", [128, 2, 512], F32),
                outf=sb(cctx, "outf", [128, 3, H], F32),
            )
            bufs2["vqT"] = nc.alloc_sbuf_tensor_at(
                "vqT2", [128, GA, 32, 128], BF16,
                offset=nc.lookup_mloc(bufs2["vbuf"]).addr)
            ffn_app(2, list(range(2, NT)), lambda t: [], bufs2)
            P.op("sync", lambda e: e.nop(),
                 waits=[(s, s.n) for s in s_outp])

        with nc.Block() as block:
            @block.sync
            def _(eng):
                P.emit("sync", eng)

            @block.gpsimd
            def _(eng):
                P.emit("gpsimd", eng)

            @block.tensor
            def _(eng):
                P.emit("tensor", eng)

            @block.vector
            def _(eng):
                P.emit("vector", eng)

            @block.scalar
            def _(eng):
                P.emit("scalar", eng)

    return nc


def kernel(L0_attn_x, L0_attn_w_in, L0_attn_b_in, L0_attn_w_a, L0_attn_b_a,
           L0_attn_w_b, L0_attn_b_b, L0_attn_w_out, L0_attn_b_out,
           L0_ffn_w_gate, L0_ffn_w_up, L0_ffn_w_down, _debug_taps=False,
           _cores=None):
    x = np.asarray(L0_attn_x, np.float32)
    B, T, Hd = x.shape
    xf = np.ascontiguousarray(x.reshape(B * T, Hd))

    Cd, iDd, D2d = _dft_consts()
    shared = dict(
        wgT=np.ascontiguousarray(np.asarray(L0_ffn_w_gate, np.float32).T),
        wuT=np.ascontiguousarray(np.asarray(L0_ffn_w_up, np.float32).T),
        wdT=np.ascontiguousarray(np.asarray(L0_ffn_w_down, np.float32).T),
        w_in=np.ascontiguousarray(np.asarray(L0_attn_w_in, np.float32)),
        w_out=np.ascontiguousarray(np.asarray(L0_attn_w_out, np.float32)),
        wa=np.ascontiguousarray(np.asarray(L0_attn_w_a, np.float32)[:, 0, :]),
        wb=np.ascontiguousarray(np.asarray(L0_attn_w_b, np.float32)[:, 0, :]),
        b_in=np.ascontiguousarray(np.asarray(L0_attn_b_in, np.float32)[None, :]),
        b_a=np.ascontiguousarray(np.asarray(L0_attn_b_a, np.float32)[None, :]),
        b_b=np.ascontiguousarray(np.asarray(L0_attn_b_b, np.float32)[None, :]),
        b_out=np.ascontiguousarray(np.asarray(L0_attn_b_out,
                                              np.float32)[None, :]),
        Cd=Cd, iDd=iDd, D2d=D2d,
        identd=np.eye(128, dtype=BFnp),
    )

    wgT_s = shared["wgT"].reshape(16, 128, 2048)
    wuT_s = shared["wuT"].reshape(16, 128, 2048)
    wdT_s = shared["wdT"].reshape(16, 128, 2048)

    nc = build_program(debug_taps=_debug_taps)
    cores = list(range(8)) if _cores is None else _cores
    in_maps = []
    for core in cores:
        tok0 = core * TPC
        seq_start = (core % 4 == 0)
        halo = (np.zeros((256, Hd), np.float32) if seq_start
                else xf[tok0 - 256: tok0])
        xhv = np.ascontiguousarray(
            np.concatenate([halo, xf[tok0:tok0 + TPC]], 0))
        zmv = np.full((128, 1), 0.0 if seq_start else 1.0, np.float32)
        wsclv = np.ascontiguousarray(np.stack(
            [wgT_s[2 * core:2 * core + 2],
             wuT_s[2 * core:2 * core + 2],
             wdT_s[2 * core:2 * core + 2]]))
        in_maps.append(dict(xh=xhv, zmask=zmv, wscl=wsclv, **shared))

    res = run_bass_kernel_spmd(nc, in_maps, core_ids=list(range(len(cores))))
    if _debug_taps:
        return res
    outs = [np.asarray(r["out"]) for r in res.results]
    return np.concatenate(outs, axis=0).reshape(B, T, Hd)


# version 33
# speedup vs baseline: 1.0533x; 1.0112x over previous
import numpy as np
import ml_dtypes
import concourse.bass as bass
import concourse.mybir as mybir
from concourse.bass_utils import run_bass_kernel_spmd

F32 = mybir.dt.float32
BF16 = mybir.dt.bfloat16
AF = mybir.ActivationFunctionType
ALU = mybir.AluOpType
AX = mybir.AxisListType
BFnp = ml_dtypes.bfloat16

MAGIC = float(np.float32(1.5 * 2**23))
DBG_QUANT_ALL = False
DBG_TRUNC = None
DBG_RACE = False
DBG_NOMASK = False
H = 1024
INNER = 4096
CH = 2048
NT = 18
NV = 16
TPC = 2048
NBLK = 8
GA = 6
ZR = 4


class Sem:
    def __init__(self, nc, name):
        self.h = nc.alloc_semaphore(name)
        self.n = 0


class Prog:
    ENGS = ("sync", "gpsimd", "tensor", "vector", "scalar")

    def __init__(self, nc):
        self.nc = nc
        self.ops = {e: [] for e in self.ENGS}
        self.sems = {}

    def sem(self, name):
        if name not in self.sems:
            self.sems[name] = Sem(self.nc, name)
        return self.sems[name]

    def op(self, eng, fn, waits=(), inc=None, dma=False):
        amt = 16 if dma else 1
        incs = []
        self.gidx = getattr(self, 'gidx', 0) + 1
        ws = [(s, v) for w in waits if w is not None for (s, v) in [w] if v > 0]
        for s, v in ws:
            assert v <= s.n, (f"wait on {s.h.name} for {v} but only {s.n} "
                              f"produced so far (eng {eng})")
        if inc is not None:
            for it in (inc if isinstance(inc, list) else [inc]):
                if isinstance(it, tuple):
                    s, a = it
                else:
                    s, a = it, amt
                s.n += a
                incs.append((s, a, s.n))
        self.ops[eng].append((fn, ws, incs, self.gidx))
        return incs[0][2] if incs else None

    def carrier(self, eng_name, eng):
        if eng_name in ("vector", "scalar", "gpsimd"):
            ent = self.carriers[eng_name]
            buf, k, cap = ent
            assert k < cap, f"carrier slots exhausted for {eng_name}"
            ent[1] = k + 1
            p, c = (k % 4) * 32, k // 4
            if eng_name == "scalar":
                return eng.memzero(buf[p:p + 1, c:c + 1])
            return eng.memset(buf[p:p + 1, c:c + 1], 0.0)
        return eng.nop()

    def emit(self, eng_name, eng):
        ops = self.ops[eng_name]
        if DBG_TRUNC is not None:
            if isinstance(DBG_TRUNC, float):
                ops = ops[:max(1, int(len(ops) * DBG_TRUNC))]
            else:
                ops = [o for o in ops if o[3] <= DBG_TRUNC]
        for fn, ws, incs, _gi in ops:
            for s, v in ws:
                eng.wait_ge(s.h, v)
            ins = fn(eng)
            if incs:
                ins.then_inc(incs[0][0].h, incs[0][1])
                for s, a, _ in incs[1:]:
                    self.carrier(eng_name, eng).then_inc(s.h, a)


def _dft_consts():
    N, K, OUT = 512, 256, 256
    t = np.arange(N)
    C = np.zeros((N, N), np.float64)
    for fr in range(N):
        if fr <= 256:
            C[:, fr] = np.cos(2 * np.pi * fr * t / N)
        else:
            C[:, fr] = np.sin(2 * np.pi * (fr - 256) * t / N)
    tau = np.arange(OUT) + (N - OUT)
    iD = np.zeros((N, OUT), np.float64)
    for fr in range(N):
        if fr <= 256:
            c = 1.0 if fr in (0, 256) else 2.0
            iD[fr] = c / N * np.cos(2 * np.pi * fr * tau / N)
        else:
            iD[fr] = 2.0 / N * np.sin(2 * np.pi * (fr - 256) * tau / N)
    j = np.arange(K)
    jr = (K - 1 - j)[:, None]
    Wc = np.cos(2 * np.pi * np.arange(257)[None, :] * jr / N)
    Ws = np.sin(2 * np.pi * np.arange(256)[None, :] * jr / N)
    U0 = Wc[:, 0:128].copy()
    U1 = Wc[:, 128:256].copy()
    V0 = np.zeros((K, 128)); V0[:, 1:] = Ws[:, 1:128]
    V1 = Ws[:, 128:256].copy()
    Nt = np.zeros((K, 128)); Nt[:, 0] = Wc[:, 256]; Nt[:, 1:] = Wc[:, 1:128]
    D2 = np.stack([U0, U1, V0, V1, Nt], axis=1)
    Cd = np.ascontiguousarray(
        C.reshape(4, 128, 4, 128).transpose(1, 0, 2, 3)).astype(BFnp)
    iDd = np.ascontiguousarray(
        iD.reshape(4, 128, 2, 128).transpose(1, 0, 2, 3)).astype(BFnp)
    D2d = np.ascontiguousarray(
        D2.reshape(2, 128, 5, 128).transpose(1, 0, 2, 3)).astype(BFnp)
    return Cd, iDd, D2d


def build_program(debug_taps=False):
    nc = bass.Bass("TRN2", target_bir_lowering=False,
                   dynamic_dma_scratch_size=16384,
                   detect_race_conditions=DBG_RACE)

    def din(name, shp, dt=F32):
        return nc.dram_tensor(name, list(shp), dt, kind="ExternalInput").ap()

    xh = din("xh", [NT * 128, H])
    zmask = din("zmask", [128, 1])
    wgT = din("wgT", [H, INNER])
    wuT = din("wuT", [H, INNER])
    wdT = din("wdT", [INNER, H])
    w_in = din("w_in", [CH, H])
    w_out = din("w_out", [H, CH])
    wa = din("wa", [H, 256])
    wb = din("wb", [H, 256])
    b_in = din("b_in", [1, CH])
    b_a = din("b_a", [1, H])
    b_b = din("b_b", [1, H])
    b_out = din("b_out", [1, H])
    wscl = din("wscl", [3, 2, 128, 2048])
    Cd = din("Cd", [128, 4, 4, 128], BF16)
    iDd = din("iDd", [128, 4, 2, 128], BF16)
    D2d = din("D2d", [128, 2, 5, 128], BF16)
    identd = din("identd", [128, 128], BF16)

    out = nc.dram_tensor("out", [TPC, H], F32, kind="ExternalOutput").ap()
    taps_out = {}
    if debug_taps:
        for nm, shp in (("dbg_xmid", [NT * 128, H]), ("dbg_z", [NT * 128, CH]),
                        ("dbg_x3", [NV * 128, H]),):
            taps_out[nm] = nc.dram_tensor(nm, shp, F32,
                                          kind="ExternalOutput").ap()

    scr_g = nc.dram_tensor("scr_g", [8, 128, 8, 512], BF16, kind="Internal").ap()
    scr_u = nc.dram_tensor("scr_u", [8, 128, 8, 512], BF16, kind="Internal").ap()
    scr_d = nc.dram_tensor("scr_d", [2, 4, 128, 8, 512], BF16,
                           kind="Internal").ap()
    cc_in = nc.dram_tensor("cc_in", [128, 3], F32, kind="Internal").ap()
    cc_out = nc.dram_tensor("cc_out", [128, 3], F32, kind="Internal",
                            addr_space="Shared").ap()

    P = Prog(nc)
    S = P.sem

    from contextlib import ExitStack
    with ExitStack() as top:
        def sb(ctx, name, shp, dt):
            return ctx.enter_context(nc.sbuf_tensor(name, shp, dt))

        xs = sb(top, "xs", [128, NT, H], BF16)
        dumv = sb(top, "dumv", [128, 256], F32)
        dums = sb(top, "dums", [128, 128], F32)
        dumg = sb(top, "dumg", [128, 128], F32)
        P.carriers = {"vector": [dumv, 0, 1024], "scalar": [dums, 0, 512],
                      "gpsimd": [dumg, 0, 512]}
        ident = sb(top, "ident", [128, 128], BF16)
        ones1 = sb(top, "ones1", [128, 128], BF16)
        ones128f = sb(top, "ones128f", [128, 128], F32)
        magicrow = sb(top, "magicrow", [128, 1], F32)
        hp = sb(top, "hp", [128, 1], F32)
        hn = sb(top, "hn", [128, 1], F32)
        zmk = sb(top, "zmk", [128, 1], F32)
        wsum = sb(top, "wsum", [128, 3], F32)
        acol = sb(top, "acol", [128, 3, 1], F32)
        nwcol = sb(top, "nwcol", [128, 3], F32)
        swv = sb(top, "swv", [128, 3], F32)
        gww = sb(top, "gww", [128, 3], F32)
        cols = {}
        for nm in ("s1", "s2", "mx", "mn", "m", "ve", "rq", "r", "mab", "sp",
                   "nb", "q", "cg", "cu", "mt", "f", "cd", "t0", "t1"):
            cols[nm] = sb(top, "col_" + nm, [128, NT], F32)
        vmax = sb(top, "vmax", [128, GA, 8], F32)
        ppool = top.enter_context(nc.psum_tensor("ppool", [128, 7, 512], F32))
        ptr = top.enter_context(nc.psum_tensor("ptr", [128, 8, 128], BF16))

        psl = [S(f"psl{i}") for i in range(7)]
        puse = [0] * 7
        ptrsl = S("pslT")
        ptruse = [0]

        def pacq(i):
            w = (psl[i], puse[i]) if puse[i] else None
            puse[i] += 1
            return w

        def ptracq():
            w = (ptrsl, ptruse[0]) if ptruse[0] else None
            ptruse[0] += 1
            return w

        sx = S("sx")
        sinit = S("sinit")
        smem = S("smem")
        szmk = S("szmk")
        sident = S("sident")
        P.op("gpsimd", lambda g: g.memset(ones1[:, :], 1.0))
        P.op("gpsimd", lambda g: g.memset(ones128f[:, :], 1.0))
        P.op("gpsimd", lambda g: g.memset(magicrow[:, :], MAGIC))
        P.op("gpsimd", lambda g: g.memset(hp[:, :], 0.5))
        P.op("gpsimd", lambda g: g.memset(hn[:, :], -0.5), inc=smem)
        v_mem = smem.n
        P.op("gpsimd", lambda g: g.dma_start(zmk[:, :], zmask), inc=szmk, dma=True)
        P.op("gpsimd", lambda g: g.dma_start(ident[:, :], identd), inc=sident,
             dma=True)
        v_init = sident.n
        xh_t = xh.rearrange("(t p) h -> t p h", p=128)
        for t in range(NT):
            P.op("gpsimd", lambda g, t=t: g.dma_start(xs[:, t, :], xh_t[t]),
                 inc=sx, dma=True)

        with ExitStack() as actx:
            wfp = sb(actx, "wfp", [128, 2, 4, 512], F32)
            wsg = sb(actx, "wsg", [128, 4, 512], BF16)
            bufs = dict(
                uT=sb(actx, "uT", [128, GA, 8, 128], BF16),
                u_f=sb(actx, "u_f", [128, 1, H], F32),
                u_tm=sb(actx, "u_tm", [128, GA, H], BF16),
                vbuf=sb(actx, "vbuf", [128, GA, INNER], BF16),
                t2=sb(actx, "t2", [128, 2, 512], F32),
                vq=sb(actx, "vq", [128, 1, INNER], BF16),
                wgup=sb(actx, "wgup", [128, 2, 2, 8, 512], BF16),
                wdb=sb(actx, "wdb", [128, 2, 8, 512], BF16),
                silu=sb(actx, "silu", [128, 2, 512], F32),
                outf=None,
            )
            bufs["vqT"] = nc.alloc_sbuf_tensor_at(
                "vqT", [128, GA, 32, 128], BF16,
                offset=nc.lookup_mloc(bufs["vbuf"]).addr)

            swfp = [S("swf0"), S("swf1")]
            sabs = S("sabs")
            sacc = S("sacc")
            sscale = S("sscale")
            s_ccl = S("s_ccl")
            s_ccd = S("s_ccd")
            kglob = 0
            for wi in range(3):
                for k in range(2):
                    par = kglob % 2
                    w = (sabs, kglob - 1) if kglob >= 2 else None
                    P.op("sync", lambda e, wi=wi, k=k, par=par: e.dma_start(
                        wfp[:, par, :, :].rearrange("p c o -> p (c o)"),
                        wscl[wi, k]),
                        waits=[w], inc=swfp[par], dma=True)
                    kglob += 1
                    P.op("scalar", lambda e, wi=wi, par=par: e.activation(
                        wsg[:, :, :], wfp[:, par, :, :], AF.Abs,
                        accum_out=acol[:, wi, :]),
                        waits=[(swfp[par], swfp[par].n), (sabs, sabs.n),
                               (sacc, sacc.n)],
                        inc=sabs)
                    if k == 0:
                        P.op("vector", lambda e, wi=wi: e.tensor_copy(
                            wsum[:, wi:wi + 1], acol[:, wi, :]),
                            waits=[(sabs, kglob), (sacc, sacc.n)], inc=sacc)
                    else:
                        P.op("vector", lambda e, wi=wi: e.tensor_add(
                            wsum[:, wi:wi + 1], wsum[:, wi:wi + 1],
                            acol[:, wi, :]),
                            waits=[(sabs, kglob), (sacc, sacc.n)], inc=sacc)
            NSC = kglob
            P.op("gpsimd", lambda e: e.dma_start(cc_in, wsum[:, :]),
                 waits=[(sacc, NSC)], inc=s_ccl, dma=True)
            P.op("gpsimd", lambda e: e.collective_compute(
                "AllReduce", ALU.add,
                replica_groups=[[0, 1, 2, 3, 4, 5, 6, 7]],
                ins=[cc_in], outs=[cc_out]),
                waits=[(s_ccl, 16)], inc=s_ccd)
            P.op("gpsimd", lambda e: e.dma_start(wsum[:, :], cc_out),
                 waits=[(s_ccd, 1)], inc=s_ccl, dma=True)
            sred = S("sred")
            wsl = pacq(6)
            P.op("tensor", lambda e: e.matmul(ppool[:, 6, 0:3], ones128f[:, :],
                                              wsum[:, :], start=True, stop=True),
                 waits=[(s_ccl, 32), wsl, (smem, v_mem)], inc=sred)
            P.op("vector", lambda e: e.tensor_copy(nwcol[:, :], ppool[:, 6, 0:3]),
                 waits=[(sred, 1)], inc=[psl[6], sscale])
            P.op("vector", lambda e: e.tensor_scalar(
                nwcol[:, :], nwcol[:, :], 1.0 / (H * INNER), 1e-5,
                ALU.mult, ALU.max), waits=[(sscale, sscale.n)], inc=sscale)
            P.op("vector", lambda e: e.reciprocal(swv[:, :], nwcol[:, :]),
                 waits=[(sscale, sscale.n)], inc=sscale)
            P.op("vector", lambda e: e.tensor_scalar(
                gww[:, 0:2], nwcol[:, 0:2], 1.0 / 254.0, None, ALU.mult),
                waits=[(sscale, sscale.n)], inc=sscale)
            P.op("vector", lambda e: e.tensor_scalar(
                gww[:, 2:3], nwcol[:, 2:3], 0.25 / 127.0, None, ALU.mult),
                waits=[(sscale, sscale.n)], inc=sscale)
            v_scale = sscale.n

            sqfp = [S("sqf0"), S("sqf1")]
            ssgn = S("ssgn")
            sqdve = S("sqdve")
            sscrw = S("sscrw")
            qk = [0]

            sq_hist = []

            def quant_chunk(src_view, dst_sb, dst_scr, wi, extra_w=None):
                par = qk[0] % 2
                w = (ssgn, 2 * (qk[0] - 1)) if qk[0] >= 2 else (sabs, 6)
                P.op("sync", lambda e: e.dma_start(wfp[:, par, :, :], src_view),
                     waits=[w], inc=sqfp[par], dma=True)
                qk[0] += 1
                vqf = sqfp[par].n
                P.op("scalar", lambda e, par=par, wi=wi: e.activation(
                    wsg[:, :, :], wfp[:, par, :, :], AF.Sign,
                    bias=hn[:, :], scale=swv[:, wi:wi + 1]),
                    waits=[(sqfp[par], vqf), (sscale, v_scale), (smem, v_mem),
                           (sqdve, sqdve.n), (sabs, 6)],
                    inc=ssgn)
                P.op("scalar", lambda e, par=par, wi=wi, dst_sb=dst_sb:
                     e.activation(dst_sb, wfp[:, par, :, :], AF.Sign,
                                  bias=hp[:, :], scale=swv[:, wi:wi + 1]),
                     waits=[extra_w, (sscrw, sscrw.n)], inc=ssgn)
                vs = ssgn.n
                P.op("vector", lambda e, dst_sb=dst_sb: e.tensor_add(
                    dst_sb, dst_sb, wsg[:, :, :]),
                    waits=[(ssgn, vs)], inc=sqdve)
                vq_ = sqdve.n
                sq_hist.append(vq_)
                P.op("sync", lambda e, dst_scr=dst_scr, dst_sb=dst_sb:
                     e.dma_start(dst_scr, dst_sb),
                     waits=[(sqdve, vq_)], inc=sscrw, dma=True)
                return vq_

            s_sq = S("s_sq")
            s_sqs = S("s_sqs")
            s_row = S("s_row")
            s_sqrt = S("s_sqrt")
            s_ufw = S("s_ufw")
            s_utm = S("s_utm")
            s_utp = S("s_utp")
            s_uT = S("s_uT")
            s_wqlp = [S("s_wql0"), S("s_wql1")]
            s_wulp = [S("s_wul0"), S("s_wul1")]
            s_wdhp = [S("s_wdh0"), S("s_wdh1")]
            s_gup = S("s_gup")
            s_silu = S("s_silu")
            s_vtt = S("s_vtt")
            s_vw = S("s_vw")
            s_row2 = S("s_row2")
            s_t2 = S("s_t2")
            s_vq8 = S("s_vq8")
            s_vqtp = [S("s_vqt0"), S("s_vqt1")]
            s_wdp = [S("s_wd0"), S("s_wd1")]
            s_dwn = S("s_dwn")
            s_out = S("s_out")

            wgT_q = wgT.rearrange("(a p) (c o) -> p a c o", p=128, o=512)
            wuT_q = wuT.rearrange("(a p) (c o) -> p a c o", p=128, o=512)
            wdT_q = wdT.rearrange("(a p) h -> p a h", p=128)

            s_dmg = S("s_dmg")
            uT_rd = {i: None for i in range(GA)}
            vq_hrd = {0: None, 1: None}
            gup_hist = []
            out_store = {}
            s_outp = [S(f"s_outp{i}") for i in range(6)]
            dmg_hist = []
            out_hist = []

            def ffn_app(app, tiles, xwait, bufs):
                uT, u_f, u_tm = bufs["uT"], bufs["u_f"], bufs["u_tm"]
                vbuf, t2, vq = bufs["vbuf"], bufs["t2"], bufs["vq"]
                wgup, wdb, silub = bufs["wgup"], bufs["wdb"], bufs["silu"]
                vqT, outf = bufs["vqT"], bufs["outf"]
                groups = [tiles[i:i + GA] for i in range(0, len(tiles), GA)]

                def rv(fn, waits=(), inc=None):
                    ws = list(waits) + [(s_row, s_row.n)]
                    ics = [s_row] + ([] if inc is None else
                                     (inc if isinstance(inc, list)
                                      else [inc]))
                    P.op("vector", fn, waits=ws, inc=ics)

                def stats_tile(t):
                    wu = (s_utm, s_utm.n) if s_utm.n else None
                    P.op("scalar", lambda e, t=t: e.activation(
                        u_f[:, 0, :], xs[:, t, :], AF.Square,
                        accum_out=cols["s2"][:, t:t + 1]),
                        waits=list(xwait(t)) + [wu, (s_sqs, s_sqs.n)],
                        inc=s_sqs)
                    yield
                    P.op("scalar", lambda e, t=t: e.activation(
                        u_f[:, 0, :], xs[:, t, :], AF.Identity,
                        accum_out=cols["s1"][:, t:t + 1]),
                        waits=list(xwait(t)) + [(s_sqs, s_sqs.n)],
                        inc=s_sqs)
                    yield
                    P.op("vector", lambda e, t=t: e.reduce_max(
                        cols["mx"][:, t:t + 1], xs[:, t, :], AX.X),
                        waits=list(xwait(t)))
                    yield
                    P.op("vector", lambda e, t=t: e.tensor_reduce(
                        cols["mn"][:, t:t + 1], xs[:, t, :], AX.X,
                        ALU.min), inc=s_sq)
                    yield

                def rowmath_gen(grp, v_sq, v_sqs):
                    G = len(grp)
                    sl = (slice(None), slice(grp[0], grp[0] + G))
                    rv(lambda e, sl=sl: e.tensor_scalar(cols["m"][sl], cols["s1"][sl],
                                                 1.0 / H, None, ALU.mult),
                       waits=[(s_sq, v_sq), (s_sqs, v_sqs)])
                    yield
                    rv(lambda e, sl=sl: e.tensor_scalar(cols["ve"][sl], cols["s2"][sl],
                                                 1.0 / H, None, ALU.mult))
                    yield
                    rv(lambda e, sl=sl: e.tensor_tensor(cols["t0"][sl], cols["m"][sl],
                                                 cols["m"][sl], ALU.mult))
                    yield
                    rv(lambda e, sl=sl: e.tensor_scalar(cols["t0"][sl], cols["t0"][sl],
                                                 -1.0, 1e-5, ALU.mult, ALU.add))
                    yield
                    rv(lambda e, sl=sl: e.tensor_tensor(cols["ve"][sl], cols["ve"][sl],
                                                 cols["t0"][sl], ALU.add))
                    yield
                    rv(lambda e, sl=sl: e.reciprocal(cols["rq"][sl], cols["ve"][sl]))
                    yield
                    v_row = s_row.n
                    P.op("scalar", lambda e, sl=sl: e.activation(
                        cols["r"][sl], cols["rq"][sl], AF.Sqrt),
                        waits=[(s_row, v_row)], inc=s_sqrt)
                    yield
                    v_sqrt = s_sqrt.n
                    rv(lambda e, sl=sl: e.tensor_tensor(cols["t0"][sl], cols["mx"][sl],
                                                 cols["m"][sl], ALU.subtract),
                       waits=[(s_sqrt, v_sqrt)])
                    yield
                    rv(lambda e, sl=sl: e.tensor_tensor(cols["t1"][sl], cols["m"][sl],
                                                 cols["mn"][sl], ALU.subtract))
                    yield
                    rv(lambda e, sl=sl: e.tensor_tensor(cols["mab"][sl], cols["t0"][sl],
                                                 cols["t1"][sl], ALU.max))
                    yield
                    rv(lambda e, sl=sl: e.tensor_scalar(cols["t0"][sl], cols["mab"][sl],
                                                 1e-12, None, ALU.max))
                    yield
                    rv(lambda e, sl=sl: e.reciprocal(cols["t1"][sl], cols["t0"][sl]))
                    yield
                    rv(lambda e, sl=sl: e.tensor_scalar(cols["sp"][sl], cols["t1"][sl],
                                                 127.0, None, ALU.mult))
                    yield
                    rv(lambda e, sl=sl: e.tensor_tensor(cols["t0"][sl], cols["m"][sl],
                                                 cols["sp"][sl], ALU.mult))
                    yield
                    rv(lambda e, sl=sl: e.tensor_scalar(cols["nb"][sl], cols["t0"][sl],
                                                 -1.0, None, ALU.mult))
                    yield
                    rv(lambda e, sl=sl: e.tensor_tensor(cols["q"][sl], cols["r"][sl],
                                                 cols["mab"][sl], ALU.mult))
                    yield
                    rv(lambda e, sl=sl: e.tensor_scalar(cols["cg"][sl], cols["q"][sl],
                                                 gww[:, 0:1], None, ALU.mult))
                    yield
                    rv(lambda e, sl=sl: e.tensor_scalar(cols["cu"][sl], cols["q"][sl],
                                                 gww[:, 1:2], None, ALU.mult))
                    yield

                def head_gen(grp, out):
                    for ti, t in enumerate(grp):
                        yield from stats_tile(t)
                    yield from rowmath_gen(grp, s_sq.n, s_sqs.n)
                    v_row = s_row.n
                    pend = []
                    for ti, t in enumerate(grp):
                        P.op("scalar", lambda e, t=t: e.activation(
                            u_f[:, 0, :], xs[:, t, :], AF.Identity,
                            bias=cols["nb"][:, t:t + 1],
                            scale=cols["sp"][:, t:t + 1]),
                            waits=[(s_row, v_row),
                                   (s_utm, s_utm.n)
                                   if s_utm.n else None],
                            inc=s_ufw)
                        yield
                        v_ufw = s_ufw.n
                        wtr = (s_uT, uT_rd[ti]) if uT_rd[ti] else None
                        P.op("vector", lambda e, ti=ti:
                             e.tensor_scalar(
                                 u_tm[:, ti, :], u_f[:, 0, :], MAGIC,
                                 MAGIC, ALU.add, ALU.subtract),
                             waits=[(s_ufw, v_ufw), wtr], inc=s_utm)
                        pend.append((ti, s_utm.n))
                        yield
                    out["pend"] = pend

                def emit_utp(grp, pend):
                    uT_base = s_uT.n
                    for ti, (slot, v_utm) in enumerate(pend):
                        wprev = (s_gup, gup_hist[-1]) if gup_hist else None
                        wchain = (s_uT, s_uT.n) if s_uT.n else None
                        P.op("sync", lambda e, ti=ti, slot=slot:
                             e.dma_start_transpose(
                                 uT[:, ti, :, :],
                                 u_tm[:, slot, :].rearrange(
                                     "p (c f) -> p c f", f=128)),
                             waits=[(s_utm, v_utm), wprev, wchain],
                             inc=s_uT, dma=True)
                        uT_rd[slot] = s_uT.n
                    return uT_base

                head = {}
                pard_done = {0: None, 1: None}
                for gi, grp in enumerate(groups):
                    G = len(grp)
                    v_dmm_prev = s_dmg.n
                    nhs = {}
                    ngen = None
                    if gi not in head:
                        hstate = {}
                        for _ in head_gen(grp, hstate):
                            pass
                        uT_base = emit_utp(grp, hstate["pend"])
                    else:
                        uT_base = head[gi]
                    if gi + 1 < len(groups) \
                            and not (app == 1 and gi == 0):
                        ngen = head_gen(groups[gi + 1], nhs)

                    vqt_done = {}

                    vtt_tile = {}

                    def emit_vchain(ti, t):
                        tc = (slice(None), slice(t, t + 1))
                        rv(lambda e, ti=ti, tc=tc: e.tensor_reduce(
                            cols["mt"][tc], vmax[:, ti, :], AX.X, ALU.max),
                           waits=[(s_vtt, vtt_tile[ti])])
                        rv(lambda e, tc=tc: e.tensor_scalar(
                            cols["t0"][tc], cols["mt"][tc], 1e-12, None,
                            ALU.max))
                        rv(lambda e, tc=tc: e.reciprocal(
                            cols["t1"][tc], cols["t0"][tc]))
                        rv(lambda e, tc=tc: e.tensor_scalar(
                            cols["f"][tc], cols["t1"][tc], 127.0, None,
                            ALU.mult))
                        rv(lambda e, tc=tc: e.tensor_tensor(
                            cols["t0"][tc], cols["cu"][tc], cols["mt"][tc],
                            ALU.mult))
                        rv(lambda e, tc=tc: e.tensor_scalar(
                            cols["cd"][tc], cols["t0"][tc], gww[:, 2:3], None,
                            ALU.mult), inc=s_row2)
                        v_row2_t = s_row2.n
                        wvq_h = [(s_vqtp[0], vq_hrd[0]) if vq_hrd[0] else None,
                                 (s_vqtp[0], vq_hrd[1]) if vq_hrd[1] else None]
                        for q8 in range(8):
                            pt2 = s_t2.n % 2
                            P.op("scalar", lambda e, ti=ti, t=t, q8=q8,
                                 pt2=pt2:
                                 e.activation(
                                     t2[:, pt2, :],
                                     vbuf[:, ti, q8 * 512:(q8 + 1) * 512],
                                     AF.Identity,
                                     scale=cols["f"][:, t:t + 1]),
                                 waits=[(s_row2, v_row2_t), (smem, v_mem),
                                        (s_vq8, s_t2.n - 1)
                                        if s_t2.n >= 2 else None,
                                        wvq_h[q8 // 4]
                                        if q8 % 4 == 0 else None],
                                 inc=s_t2)
                            v_t2 = s_t2.n
                            P.op("vector", lambda e, q8=q8, pt2=pt2:
                                 e.tensor_scalar(
                                     vq[:, 0, q8 * 512:(q8 + 1) * 512],
                                     t2[:, pt2, :], MAGIC, MAGIC,
                                     ALU.add, ALU.subtract),
                                 waits=[(s_t2, v_t2)], inc=s_vq8)
                            if q8 == 3 or q8 == 7:
                                hh = q8 // 4
                                v_vq8 = s_vq8.n
                                wch = (s_vqtp[0], s_vqtp[0].n) \
                                    if s_vqtp[0].n else None
                                P.op("sync", lambda e, ti=ti, hh=hh:
                                     e.dma_start_transpose(
                                         vqT[:, ti, 16 * hh:16 * hh + 16, :],
                                         vq[:, 0, 2048 * hh:2048 * hh + 2048]
                                         .rearrange("p (c f) -> p c f",
                                                    f=128)),
                                     waits=[(s_vq8, v_vq8), wch],
                                     inc=s_vqtp[0], dma=True)
                                vq_hrd[hh] = s_vqtp[0].n
                        vqt_done[ti] = (s_vqtp[0], s_vqtp[0].n)

                    for oc in range(8):
                        par = oc % 2
                        wclob = (s_gup, gup_hist[-2]) if len(gup_hist) >= 2 \
                            else None
                        if app == 1 and (gi == 0 or DBG_QUANT_ALL):
                            vg = None
                            for wi, (wv, scr) in enumerate(((wgT_q, scr_g),
                                                            (wuT_q, scr_u))):
                                for hh in range(2):
                                    vg = quant_chunk(
                                        wv[:, 4 * hh:4 * hh + 4, oc, :],
                                        wgup[:, par, wi, 4 * hh:4 * hh + 4, :],
                                        scr[oc][:, 4 * hh:4 * hh + 4, :],
                                        wi, extra_w=wclob)
                            wq_wait = [(sqdve, vg)]
                        else:
                            P.op("gpsimd", lambda e, oc=oc, par=par:
                                 e.dma_start(
                                     wgup[:, par, 0, :, :], scr_g[oc]),
                                 waits=[wclob, (sscrw, sscrw.n)],
                                 inc=s_wqlp[par], dma=True)
                            P.op("gpsimd", lambda e, oc=oc, par=par:
                                 e.dma_start(
                                     wgup[:, par, 1, :, :], scr_u[oc]),
                                 inc=s_wqlp[par], dma=True)
                            wq_wait = [(s_wqlp[par], s_wqlp[par].n)]
                        for ti, t in enumerate(grp):
                            pp = s_gup.n % 3
                            wg_, wu_ = pacq(2 * pp), pacq(2 * pp + 1)
                            for icc in range(8):
                                wts = (wq_wait +
                                       [(s_uT, uT_base + 16 * (ti + 1)),
                                        wg_, wu_] if icc == 0 else [])
                                P.op("tensor", lambda e, icc=icc, ti=ti,
                                     par=par, pp=pp:
                                     e.matmul(ppool[:, 2 * pp, :],
                                              uT[:, ti, icc, :],
                                              wgup[:, par, 0, icc, :],
                                              start=(icc == 0), stop=(icc == 7)),
                                     waits=wts)
                                P.op("tensor", lambda e, icc=icc, ti=ti,
                                     par=par, pp=pp:
                                     e.matmul(ppool[:, 2 * pp + 1, :],
                                              uT[:, ti, icc, :],
                                              wgup[:, par, 1, icc, :],
                                              start=(icc == 0), stop=(icc == 7)),
                                     inc=(s_gup if icc == 7 else None))
                            v_gup = s_gup.n
                            sp_ = s_silu.n % 2
                            P.op("scalar", lambda e, t=t, pp=pp, sp_=sp_:
                                 e.activation(silub[:, sp_, :],
                                              ppool[:, 2 * pp, :], AF.Silu,
                                              scale=cols["cg"][:, t:t + 1]),
                                 waits=[(s_gup, v_gup),
                                        (s_vw, s_vw.n - 1)
                                        if s_silu.n >= 2 else None],
                                 inc=s_silu)
                            v_silu = s_silu.n
                            P.op("vector", lambda e, ti=ti, oc=oc, pp=pp,
                                 sp_=sp_:
                                 e.tensor_tensor(
                                     vbuf[:, ti, oc * 512:(oc + 1) * 512],
                                     silub[:, sp_, :], ppool[:, 2 * pp + 1, :],
                                     ALU.mult),
                                 waits=[(s_silu, v_silu),
                                        (s_dmg, v_dmm_prev)],
                                 inc=[s_vw, psl[2 * pp], psl[2 * pp + 1]])
                            P.op("vector", lambda e, ti=ti, oc=oc:
                                 e.reduce_max(
                                     vmax[:, ti, oc:oc + 1],
                                     vbuf[:, ti, oc * 512:(oc + 1) * 512],
                                     AX.X, apply_absolute_value=True),
                                 waits=[(s_vw, s_vw.n)], inc=s_vtt)
                            if oc == 7:
                                vtt_tile[ti] = s_vtt.n
                            if ngen is not None and oc >= 1:
                                for _ in range(1 if oc < 5 else 2):
                                    if next(ngen, "END") == "END":
                                        ngen = None
                                        break
                        gup_hist.append(s_gup.n)


                    for ti, t in enumerate(grp):
                        emit_vchain(ti, t)

                    preload = {}
                    if ngen is not None:
                        for _ in ngen:
                            pass
                        ngen = None
                    if "pend" in nhs:
                        head[gi + 1] = emit_utp(groups[gi + 1], nhs["pend"])

                    dmg_tile = {}
                    dwn_tile = {}
                    for hc in range(2):
                        if hc == 0 and G > 4:
                            sched = [(0, list(range(0, 4))),
                                     (1, list(range(0, 4))),
                                     (0, list(range(4, G))),
                                     (1, list(range(4, G))),
                                     (2, list(range(G))),
                                     (3, list(range(G)))]
                        else:
                            sched = [(icg, list(range(G)))
                                     for icg in range(4)]
                        loaded = {}
                        done_cnt = {k: 0 for k in range(4)}
                        for icg, tis in sched:
                            if not tis:
                                continue
                            pard = (hc * 4 + icg) % 2
                            first_chunk = icg not in loaded
                            if first_chunk:
                                wdclob = (s_dmg, pard_done[pard]) \
                                    if pard_done[pard] else None
                                if (hc, icg) in preload:
                                    loaded[icg] = preload[(hc, icg)]
                                elif app == 1 and (gi == 0
                                                   or DBG_QUANT_ALL):
                                    vg = None
                                    for hh in range(2):
                                        vg = quant_chunk(
                                            wdT_q[:, icg * 8 + 4 * hh:
                                                  icg * 8 + 4 * hh + 4,
                                                  hc * 512:(hc + 1) * 512],
                                            wdb[:, pard,
                                                4 * hh:4 * hh + 4, :],
                                            scr_d[hc, icg]
                                            [:, 4 * hh:4 * hh + 4, :],
                                            2, extra_w=wdclob)
                                    loaded[icg] = [(sqdve, vg)]
                                else:
                                    P.op("gpsimd", lambda e, hc=hc, icg=icg,
                                         pard=pard: e.dma_start(
                                             wdb[:, pard, :, :],
                                             scr_d[hc, icg]),
                                         waits=[wdclob, (sscrw, sscrw.n)],
                                         inc=s_wdp[pard], dma=True)
                                    loaded[icg] = \
                                        [(s_wdp[pard], s_wdp[pard].n)]
                            for ti in tis:
                                for icc in range(8):
                                    wts = []
                                    if icc == 0:
                                        if ti == tis[0] and first_chunk:
                                            wts += loaded[icg]
                                        if icg == 0 and hc == 0:
                                            wts += [vqt_done[ti], pacq(ti)]
                                        if icg == 0 and hc == 1:
                                            wts.append(
                                                (s_dwn, dwn_tile[ti]))
                                    st = (icg == 0 and icc == 0)
                                    sp2 = (icg == 3 and icc == 7)
                                    incs = [s_dmg] if icc == 7 else None
                                    P.op("tensor", lambda e, icc=icc, ti=ti,
                                         icg=icg, pard=pard, st=st, sp2=sp2:
                                         e.matmul(ppool[:, ti, :],
                                                  vqT[:, ti, icg * 8 + icc, :],
                                                  wdb[:, pard, icc, :],
                                                  start=st, stop=sp2,
                                                  skip_group_check=True),
                                         waits=wts, inc=incs)
                                if icg == 3:
                                    dmg_tile[ti] = s_dmg.n
                            done_cnt[icg] += len(tis)
                            if done_cnt[icg] == G:
                                pard_done[pard] = s_dmg.n
                        dmg_hist.append(s_dmg.n)
                        for ti, t in enumerate(grp):
                            last = (hc == 1)
                            if app == 1:
                                P.op("vector", lambda e, ti=ti, t=t, hc=hc:
                                     e.scalar_tensor_tensor(
                                         xs[:, t, hc * 512:(hc + 1) * 512],
                                         ppool[:, ti, :],
                                         cols["cd"][:, t:t + 1],
                                         xs[:, t, hc * 512:(hc + 1) * 512],
                                         ALU.mult, ALU.add),
                                     waits=[(s_dmg, dmg_tile[ti])],
                                     inc=([s_dwn, psl[ti]] if last
                                          else [s_dwn]))
                            else:
                                gidx = gi * GA + ti
                                slot = gidx % 3
                                strm = (gidx % 3) + 3 * hc
                                wout = (s_outp[strm],
                                        out_store.get((gidx - 3, hc), 0)) \
                                    if gidx >= 3 else None
                                P.op("vector", lambda e, ti=ti, t=t, hc=hc,
                                     slot=slot: e.scalar_tensor_tensor(
                                         bufs["outf"][:, slot,
                                                      hc * 512:(hc + 1) * 512],
                                         ppool[:, ti, :],
                                         cols["cd"][:, t:t + 1],
                                         xs[:, t, hc * 512:(hc + 1) * 512],
                                         ALU.mult, ALU.add),
                                     waits=[(s_dmg, dmg_tile[ti]), wout],
                                     inc=([s_dwn, psl[ti]] if last
                                          else [s_dwn]))
                                out_t = out.rearrange("(t p) h -> t p h",
                                                      p=128)
                                P.op("sync", lambda e, t=t, slot=slot, hc=hc:
                                     e.dma_start(
                                         out_t[t - 2][:, hc * 512:(hc + 1) * 512],
                                         bufs["outf"][:, slot,
                                                      hc * 512:(hc + 1) * 512]),
                                     waits=[(s_dwn, s_dwn.n)],
                                     inc=s_outp[strm], dma=True)
                                out_store[(gidx, hc)] = s_outp[strm].n
                            if hc == 0:
                                dwn_tile[ti] = s_dwn.n

            ffn_app(1, list(range(NT)), lambda t: [(sx, 16 * NT)], bufs)

            sbar = S("sbar")

            def barrier():
                base = sbar.n
                for e in Prog.ENGS:
                    P.op(e, lambda x, e=e: P.carrier(e, x), inc=sbar)
                for e in Prog.ENGS:
                    P.op(e, lambda x: x.nop(), waits=[(sbar, base + 5)])

            if debug_taps:
                barrier()
                dxm = taps_out["dbg_xmid"].rearrange("(t p) h -> t p h", p=128)
                sdbg = S("sdbg")
                for t in range(NT):
                    P.op("gpsimd", lambda e, t=t: e.dma_start(dxm[t], xs[:, t, :]),
                         inc=sdbg, dma=True)
                P.op("gpsimd", lambda e: e.nop(), waits=[(sdbg, 16 * NT)])
            barrier()

        with ExitStack() as bctx:
            zf = sb(bctx, "zf", [128, ZR, CH], BF16)
            winT = sb(bctx, "winT", [128, 8, CH], BF16)
            woutT = sb(bctx, "woutT", [128, 16, H], BF16)
            tabs = sb(bctx, "tabs", [128, 5, CH], BF16)
            Csb = sb(bctx, "Csb", [128, 4, 4, 128], BF16)
            iDsb = sb(bctx, "iDsb", [128, 4, 2, 128], BF16)
            bjnk = sb(bctx, "bjnk", [128, H], BF16)
            brows = sb(bctx, "brows", [128, CH], BF16)
            binb = brows[0:1, :]
            bab = brows[32:33, :]
            boutb = brows[64:65, 0:H]
            u2tm = sb(bctx, "u2tm", [128, 1, H], BF16)
            u2T = sb(bctx, "u2T", [128, 2, 8, 128], BF16)

            s_pm = S("s_pm")
            u2Tp = [S("u2Tp0"), S("u2Tp1")]
            sB = {k: S("b_" + k) for k in
                  ("cload", "wint", "woutt", "tapt", "tab", "tabev", "zrow",
                   "zsqrt", "u2", "u2t", "u2T", "zmm", "zev", "fwd", "fev",
                   "pch", "idft", "sig", "gat", "catp", "catev", "omm", "oev",
                   "stat")}

            P.op("gpsimd", lambda e: e.dma_start(Csb[:, :, :, :], Cd),
                 inc=sB["cload"], dma=True)
            P.op("gpsimd", lambda e: e.dma_start(iDsb[:, :, :, :], iDd),
                 inc=sB["cload"], dma=True)
            P.op("gpsimd", lambda e: e.dma_start(binb[:, :], b_in),
                 inc=sB["cload"], dma=True)
            P.op("gpsimd", lambda e: e.dma_start(bab[:, 0:H], b_a),
                 inc=sB["cload"], dma=True)
            P.op("gpsimd", lambda e: e.dma_start(bab[:, H:CH], b_b),
                 inc=sB["cload"], dma=True)
            P.op("gpsimd", lambda e: e.dma_start(boutb[:, :], b_out),
                 inc=sB["cload"], dma=True)
            v_cload = sB["cload"].n

            with ExitStack() as wctx:
                wstg = sb(wctx, "wstg", [128, 2, 4096], BF16)
                tapb = sb(wctx, "tapb", [128, 16, 256], BF16)
                tapT = sb(wctx, "tapT", [128, 2, CH], BF16)
                D2sb = sb(wctx, "D2sb", [128, 2, 5, 128], BF16)
                sldp = [S("b_sld0"), S("b_sld1")]
                sD2 = S("b_sD2")
                staps = S("b_staps")
                wintp = [S("b_wint0"), S("b_wint1")]
                wouttp = [S("b_woutt0"), S("b_woutt1")]
                P.op("gpsimd", lambda e: e.dma_start(D2sb[:, :, :, :], D2d),
                     inc=sD2, dma=True)
                wa_r = wa.rearrange("(g p) k -> p g k", p=128)
                wb_r = wb.rearrange("(g p) k -> p g k", p=128)
                P.op("gpsimd", lambda e: e.dma_start(tapb[:, 0:8, :], wa_r),
                     inc=staps, dma=True)
                P.op("gpsimd", lambda e: e.dma_start(tapb[:, 8:16, :], wb_r),
                     inc=staps, dma=True)
                v_tapl = staps.n
                for g in range(16):
                    P.op("sync", lambda e, g=g: e.dma_start_transpose(
                        tapT[:, :, g * 128:(g + 1) * 128],
                        tapb[:, g, :].rearrange("p (j f) -> p j f", f=128)),
                        waits=[(staps, v_tapl)], inc=sB["tapt"], dma=True)
                v_tapt = sB["tapt"].n
                for tb in range(5):
                    for fc in range(4):
                        slw = pacq(6)
                        for jc in range(2):
                            P.op("tensor", lambda e, tb=tb, fc=fc, jc=jc:
                                 e.matmul(ppool[:, 6, :],
                                          D2sb[:, jc, tb, :],
                                          tapT[:, jc, fc * 512:(fc + 1) * 512],
                                          start=(jc == 0), stop=(jc == 1)),
                                 waits=([(sB["tapt"], v_tapt),
                                         (sD2, 16), slw]
                                        if jc == 0 else []),
                                 inc=(sB["tab"] if jc == 1 else None))
                        v_tab = sB["tab"].n
                        P.op("vector", lambda e, tb=tb, fc=fc: e.tensor_copy(
                            tabs[:, tb, fc * 512:(fc + 1) * 512],
                            ppool[:, 6, :]),
                            waits=[(sB["tab"], v_tab)],
                            inc=[psl[6], sB["tabev"]])
                v_tabev = sB["tabev"].n
                w_in_r = w_in.rearrange("(g p) h -> p g h", p=128)
                w_out_r = w_out.rearrange("(g p) c -> p g c", p=128)
                wstg_v = [wstg[:, 0, :].rearrange("p (g h) -> p g h", h=H),
                          wstg[:, 1, :].rearrange("p (g h) -> p g h", h=H)]
                tpar_hist = {0: [], 1: []}
                win_cc = []
                for pc in range(4):
                    par = pc % 2
                    w = (wintp[par], tpar_hist[par][-1]) \
                        if tpar_hist[par] else None
                    P.op("gpsimd", lambda e, pc=pc, par=par: e.dma_start(
                        wstg_v[par], w_in_r[:, 4 * pc:4 * pc + 4, :]),
                        waits=[w], inc=sldp[par], dma=True)
                    v_ld = sldp[par].n
                    for g in range(4):
                        cc = 4 * pc + g
                        P.op("sync", lambda e, par=par, cc=cc, g=g:
                             e.dma_start_transpose(
                                 winT[:, :, cc * 128:(cc + 1) * 128],
                                 wstg[:, par, g * H:(g + 1) * H]
                                 .rearrange("p (i f) -> p i f", f=128)),
                             waits=[(sldp[par], v_ld)],
                             inc=wintp[par], dma=True)
                    tpar_hist[par].append(wintp[par].n)
                    win_cc.append((wintp[par], wintp[par].n))
                opar_hist = {0: [], 1: []}
                for pc in range(4):
                    par = pc % 2
                    w1 = (wintp[par], tpar_hist[par][-1])
                    w2 = (wouttp[par], opar_hist[par][-1]) \
                        if opar_hist[par] else None
                    P.op("gpsimd", lambda e, pc=pc, par=par: e.dma_start(
                        wstg[:, par, :].rearrange("p (g c) -> p g c", c=CH),
                        w_out_r[:, 2 * pc:2 * pc + 2, :]),
                        waits=[w1, w2], inc=sldp[par], dma=True)
                    v_ld = sldp[par].n
                    for g in range(2):
                        hc = 2 * pc + g
                        P.op("sync", lambda e, par=par, hc=hc, g=g:
                             e.dma_start_transpose(
                                 woutT[:, :, hc * 128:(hc + 1) * 128],
                                 wstg[:, par, g * CH:(g + 1) * CH]
                                 .rearrange("p (i f) -> p i f", f=128)),
                             waits=[(sldp[par], v_ld)],
                             inc=wouttp[par], dma=True)
                    opar_hist[par].append(wouttp[par].n)
                win_waits = [(wintp[0], wintp[0].n), (wintp[1], wintp[1].n)]
                wout_waits = [(wouttp[0], wouttp[0].n),
                              (wouttp[1], wouttp[1].n)]
                s_bdrain = S("s_bdrain")
                P.op("sync", lambda e: e.nop(),
                     waits=[(sB["tapt"], sB["tapt"].n),
                            (wintp[0], wintp[0].n), (wintp[1], wintp[1].n),
                            (wouttp[0], wouttp[0].n),
                            (wouttp[1], wouttp[1].n)], inc=s_bdrain)
                P.op("tensor", lambda e: e.nop(),
                     waits=[(sB["tabev"], sB["tabev"].n)], inc=s_bdrain)
                s_wprep = S("s_wprep")
                P.op("sync", lambda e: e.nop(),
                     waits=[(sB["tapt"], sB["tapt"].n),
                            (wintp[0], wintp[0].n), (wintp[1], wintp[1].n),
                            (wouttp[0], wouttp[0].n), (wouttp[1], wouttp[1].n),
                            (sldp[0], sldp[0].n), (sldp[1], sldp[1].n),
                            (sD2, sD2.n), (staps, staps.n)], inc=s_wprep)

            if True:
                with ExitStack() as pctx:
                    zh = sb(pctx, "zh", [128, 4, CH], BF16)
                    Pb = sb(pctx, "Pb", [128, 4, CH], BF16)
                    ptmp = sb(pctx, "ptmp", [128, CH], BF16)
                    sigb = sb(pctx, "sigb", [128, 2, 512], BF16)
                    gat = sb(pctx, "gat", [128, CH], BF16)
                    catT = sb(pctx, "catT", [128, 16, 128], BF16)

                    fwd_done_after_block = []
                    idft_done_after_block = []
                    zmm_hist = []
                    head_tp = {}
                    pch_hist = []
                    pairctr = [0]
                    u2tp_last = [None]

                    def tile_head(t):
                        pf = sB["u2"].n % 2
                        tslc = (slice(None), slice(t, t + 1))
                        P.op("scalar", lambda e, t=t, tslc=tslc:
                             e.activation(
                                 bjnk[:, :], xs[:, t, :], AF.Square,
                                 accum_out=cols["s2"][tslc]),
                             waits=[(sB["stat"], sB["stat"].n)],
                             inc=sB["stat"])
                        def rz(fn, waits=(), inc=None):
                            ws = list(waits) + [(sB["zrow"], sB["zrow"].n)]
                            ics = [sB["zrow"]] + ([] if inc is None else [inc])
                            P.op("vector", fn, waits=ws, inc=ics)
                        rz(lambda e, t=t, tslc=tslc: e.reduce_sum(
                            cols["s1"][tslc], xs[:, t, :], AX.X))
                        rz(lambda e, tslc=tslc: e.tensor_scalar(
                            cols["m"][tslc], cols["s1"][tslc], 1.0 / H, None,
                            ALU.mult))
                        rz(lambda e, tslc=tslc: e.tensor_scalar(
                            cols["ve"][tslc], cols["s2"][tslc], 1.0 / H, None,
                            ALU.mult), waits=[(sB["stat"], sB["stat"].n)])
                        rz(lambda e, tslc=tslc: e.tensor_tensor(
                            cols["t0"][tslc], cols["m"][tslc], cols["m"][tslc],
                            ALU.mult))
                        rz(lambda e, tslc=tslc: e.tensor_scalar(
                            cols["t0"][tslc], cols["t0"][tslc], -1.0, 1e-5,
                            ALU.mult, ALU.add))
                        rz(lambda e, tslc=tslc: e.tensor_tensor(
                            cols["ve"][tslc], cols["ve"][tslc], cols["t0"][tslc],
                            ALU.add))
                        rz(lambda e, tslc=tslc: e.reciprocal(
                            cols["rq"][tslc], cols["ve"][tslc]))
                        v_zr = sB["zrow"].n
                        P.op("scalar", lambda e, tslc=tslc: e.activation(
                            cols["r"][tslc], cols["rq"][tslc], AF.Sqrt),
                            waits=[(sB["zrow"], v_zr)], inc=sB["zsqrt"])
                        v_zs = sB["zsqrt"].n
                        rz(lambda e, tslc=tslc: e.tensor_tensor(
                            cols["t0"][tslc], cols["m"][tslc], cols["r"][tslc],
                            ALU.mult), waits=[(sB["zsqrt"], v_zs)])
                        rz(lambda e, tslc=tslc: e.tensor_scalar(
                            cols["nb"][tslc], cols["t0"][tslc], -1.0, None,
                            ALU.mult))
                        v_zr = sB["zrow"].n
                        P.op("scalar", lambda e, t=t: e.activation(
                            u2tm[:, 0, :], xs[:, t, :], AF.Identity,
                            bias=cols["nb"][tslc], scale=cols["r"][tslc]),
                            waits=[(sB["zrow"], v_zr), u2tp_last[0]],
                            inc=sB["u2"])
                        v_u2 = sB["u2"].n
                        P.op("sync", lambda e, pf=pf: e.dma_start_transpose(
                            u2T[:, pf, :, :],
                            u2tm[:, 0, :].rearrange("p (c f) -> p c f",
                                                    f=128)),
                            waits=[(sB["u2"], v_u2),
                                   (sB["zmm"], zmm_hist[-1])
                                   if len(zmm_hist) >= 1 else None],
                            inc=u2Tp[pf], dma=True)
                        u2tp_last[0] = (u2Tp[pf], u2Tp[pf].n)
                        head_tp[t] = (u2Tp[pf].n, pf)

                    def tile_zproj(t):
                        v_u2T, pf = head_tp[t]
                        for cc in range(4):
                            sidx = sB["zmm"].n % 2
                            zw = pacq(sidx)
                            for icc in range(8):
                                wts = []
                                if icc == 0:
                                    wts = [(u2Tp[pf], v_u2T),
                                           (sB["cload"], v_cload), zw,
                                           win_cc[cc]]
                                P.op("tensor", lambda e, icc=icc, cc=cc,
                                     sidx=sidx, pf=pf:
                                     e.matmul(ppool[:, sidx, :],
                                              u2T[:, pf, icc, :],
                                              winT[:, icc,
                                                   cc * 512:(cc + 1) * 512],
                                              start=(icc == 0), stop=False),
                                     waits=wts)
                            P.op("tensor", lambda e, cc=cc, sidx=sidx:
                                 e.matmul(ppool[:, sidx, :], ones1[0:1, :],
                                          binb[:, cc * 512:(cc + 1) * 512],
                                          start=False, stop=True),
                                 waits=[(smem, v_mem)], inc=sB["zmm"])
                            v_zmm = sB["zmm"].n
                            slot = t % ZR
                            if t < 2 and not DBG_NOMASK:
                                P.op("vector", lambda e, slot=slot, cc=cc,
                                     sidx=sidx: e.tensor_scalar(
                                         zf[:, slot, cc * 512:(cc + 1) * 512],
                                         ppool[:, sidx, :], zmk[:, :], None,
                                         ALU.mult),
                                     waits=[(sB["zmm"], v_zmm),
                                            (szmk, 16)],
                                     inc=[psl[sidx], sB["zev"]])
                            else:
                                P.op("scalar", lambda e, slot=slot, cc=cc,
                                     sidx=sidx: e.activation(
                                         zf[:, slot, cc * 512:(cc + 1) * 512],
                                         ppool[:, sidx, :], AF.Identity),
                                     waits=[(sB["zmm"], v_zmm)],
                                     inc=[psl[sidx], sB["zev"]])
                        zmm_hist.append(sB["zmm"].n)

                    def emit_block_gen(b):
                        bd = (s_bdrain, 2) if b == 0 else None
                        v_zev_need = 4 * (2 * b + 4)
                        pair_specs = [
                            ((0, 2), [(0, 0, 0, 1, 2, ALU.subtract),
                                      (2, 1, 4, 0, 2, ALU.add)]),
                            ((1, 3), [(1, 0, 1, 1, 3, ALU.subtract),
                                      (3, 1, 1, 0, 3, ALU.add)]),
                        ]
                        pdata = []
                        for pk, ((poA, poB), steps) in enumerate(pair_specs):
                            zbase = (pairctr[0] % 2) * 2
                            pairctr[0] += 1
                            wzh = (sB["pch"], pch_hist[-2 + pk]) \
                                if len(pch_hist) >= 2 else None
                            for zi, po in enumerate((poA, poB)):
                                for fc in range(4):
                                    sidx = 2 + sB["fwd"].n % 2
                                    fw = pacq(sidx)
                                    for icc in range(4):
                                        wts = []
                                        if icc == 0:
                                            wts = [(sB["zev"], v_zev_need),
                                                   (sB["cload"], v_cload), fw]
                                        P.op("tensor", lambda e, icc=icc,
                                             po=po, fc=fc, b=b, sidx=sidx:
                                             e.matmul(
                                                 ppool[:, sidx, :],
                                                 Csb[:, icc, po, :],
                                                 zf[:, (2 * b + icc) % ZR,
                                                    fc * 512:(fc + 1) * 512],
                                                 start=(icc == 0),
                                                 stop=(icc == 3)),
                                             waits=wts,
                                             inc=(sB["fwd"] if icc == 3
                                                  else None))
                                    v_fwd = sB["fwd"].n
                                    P.op("scalar", lambda e, zi=zi, fc=fc,
                                         sidx=sidx, zbase=zbase:
                                         e.activation(
                                             zh[:, zbase + zi,
                                                fc * 512:(fc + 1) * 512],
                                             ppool[:, sidx, :], AF.Identity),
                                         waits=[(sB["fwd"], v_fwd), bd,
                                                wzh if fc == 0 and zi == 0
                                                else None],
                                         inc=[psl[sidx], sB["fev"]])
                            pdata.append((zbase, steps, sB["fev"].n))
                        yield
                        v_idft_prev = idft_done_after_block[-1] \
                            if idft_done_after_block else 0
                        for zbase, steps, v_fev in pdata:
                            for (pidx, za, ta, zb2, tb2, opf) in steps:
                                P.op("vector", lambda e, pidx=pidx, za=za,
                                     ta=ta, zbase=zbase: e.tensor_tensor(
                                         Pb[:, pidx, :],
                                         zh[:, zbase + za, :],
                                         tabs[:, ta, :], ALU.mult),
                                     waits=[(sB["fev"], v_fev), bd,
                                            (sB["tabev"], v_tabev),
                                            (sB["idft"], v_idft_prev),
                                            (s_pm, s_pm.n)], inc=s_pm)
                                P.op("vector", lambda e, zb2=zb2, tb2=tb2,
                                     zbase=zbase:
                                     e.tensor_tensor(ptmp[:, :],
                                                     zh[:, zbase + zb2, :],
                                                     tabs[:, tb2, :],
                                                     ALU.mult),
                                     waits=[(s_pm, s_pm.n)], inc=s_pm)
                                P.op("vector", lambda e, pidx=pidx, opf=opf:
                                     e.tensor_tensor(Pb[:, pidx, :],
                                                     Pb[:, pidx, :],
                                                     ptmp[:, :], opf),
                                     waits=[(s_pm, s_pm.n)],
                                     inc=[s_pm, sB["pch"]])
                            pch_hist.append(sB["pch"].n)
                        v_pch = sB["pch"].n
                        yield

                        for po in range(2):
                            tgt = 2 + 2 * b + po
                            for pc in range(2):
                                wA, wB2 = pacq(4), pacq(5)
                                for half, (sidx, fc) in enumerate(
                                        ((4, pc), (5, pc + 2))):
                                    for icc in range(4):
                                        wts = []
                                        if icc == 0:
                                            wts = [(sB["pch"], v_pch),
                                                   wA if half == 0 else wB2]
                                        P.op("tensor", lambda e, icc=icc,
                                             po=po, fc=fc, sidx=sidx:
                                             e.matmul(
                                                 ppool[:, sidx, :],
                                                 iDsb[:, icc, po, :],
                                                 Pb[:, icc,
                                                    fc * 512:(fc + 1) * 512],
                                                 start=(icc == 0), stop=False),
                                             waits=wts)
                                    P.op("tensor", lambda e, fc=fc, sidx=sidx:
                                         e.matmul(ppool[:, sidx, :],
                                                  ones1[32:33, :],
                                                  bab[:, fc * 512:
                                                      (fc + 1) * 512],
                                                  start=False, stop=True),
                                         inc=sB["idft"])
                                v_idft = sB["idft"].n
                                P.op("scalar", lambda e: e.activation(
                                    sigb[:, 0, :], ppool[:, 4, :], AF.Sigmoid),
                                    waits=[(sB["idft"], v_idft), bd],
                                    inc=sB["sig"])
                                P.op("scalar", lambda e: e.activation(
                                    sigb[:, 1, :], ppool[:, 5, :], AF.Sigmoid),
                                    inc=sB["sig"])
                                v_sig = sB["sig"].n
                                P.op("vector", lambda e, pc=pc: e.tensor_tensor(
                                    gat[:, pc * 512:(pc + 1) * 512],
                                    ppool[:, 4, :], sigb[:, 1, :], ALU.mult),
                                    waits=[(sB["sig"], v_sig), bd,
                                           (sB["catev"], sB["catev"].n)
                                           if pc == 0 else None])
                                P.op("vector", lambda e, pc=pc: e.tensor_tensor(
                                    gat[:, (pc + 2) * 512:(pc + 3) * 512],
                                    ppool[:, 5, :], sigb[:, 0, :], ALU.mult),
                                    inc=[sB["gat"], psl[4], psl[5]])
                            v_gat = sB["gat"].n
                            P.op("sync", lambda e: e.dma_start_transpose(
                                catT[:, :, :],
                                gat[:, :].rearrange("p (c f) -> p c f",
                                                    f=128)),
                                waits=[(sB["gat"], v_gat), bd,
                                       (sB["omm"], sB["omm"].n)],
                                inc=sB["catev"], dma=True)
                            v_catev = sB["catev"].n
                            yield
                            for hc in range(2):
                                ow = pacq(6)
                                for ec in range(16):
                                    wts = []
                                    if ec == 0:
                                        wts = [(sB["catev"], v_catev),
                                               ow] + wout_waits
                                    P.op("tensor", lambda e, ec=ec, hc=hc:
                                         e.matmul(
                                             ppool[:, 6, :],
                                             catT[:, ec, :],
                                             woutT[:, ec,
                                                   hc * 512:(hc + 1) * 512],
                                             start=(ec == 0), stop=False),
                                         waits=wts)
                                P.op("tensor", lambda e, hc=hc: e.matmul(
                                    ppool[:, 6, :], ones1[64:65, :],
                                    boutb[:, hc * 512:(hc + 1) * 512],
                                    start=False, stop=True),
                                    inc=sB["omm"])
                                v_omm = sB["omm"].n
                                P.op("vector", lambda e, tgt=tgt, hc=hc:
                                     e.tensor_tensor(
                                         xs[:, tgt, hc * 512:(hc + 1) * 512],
                                         ppool[:, 6, :],
                                         xs[:, tgt, hc * 512:(hc + 1) * 512],
                                         ALU.add),
                                     waits=[(sB["omm"], v_omm)],
                                     inc=[psl[6], sB["oev"]])
                        idft_done_after_block.append(sB["idft"].n)

                    def do_zproj(t):
                        if t + 1 < NT and (t + 1) not in head_tp:
                            tile_head(t + 1)
                        tile_zproj(t)

                    tile_head(0)
                    for t in range(4):
                        do_zproj(t)
                    NB = (NT - 2) // 2
                    gens = {}
                    for b in range(NB):
                        gens[b] = emit_block_gen(b)
                        next(gens[b])
                        if b >= 1:
                            next(gens[b - 1])
                        if 2 * b + 4 < NT:
                            do_zproj(2 * b + 4)
                        if b >= 1:
                            next(gens[b - 1])
                        if 2 * b + 5 < NT:
                            do_zproj(2 * b + 5)
                        if b >= 1:
                            for _ in gens.pop(b - 1):
                                pass
                        next(gens[b])
                    next(gens[NB - 1])
                    for _ in gens.pop(NB - 1):
                        pass
                    if debug_taps:
                        barrier()
                        dz = taps_out["dbg_z"].rearrange("(t p) c -> t p c",
                                                         p=128)
                        sdbg2 = S("sdbg2")
                        for t in range(NT - ZR, NT):
                            P.op("gpsimd", lambda e, t=t: e.dma_start(
                                dz[t], zf[:, t % ZR, :]), inc=sdbg2, dma=True)
                        P.op("gpsimd", lambda e: e.nop(),
                             waits=[(sdbg2, 16 * ZR)])
                        dx3 = taps_out["dbg_x3"].rearrange(
                            "(t p) h -> t p h", p=128)
                        sdbg3 = S("sdbg3")
                        for t in range(NV):
                            P.op("gpsimd", lambda e, t=t: e.dma_start(
                                dx3[t], xs[:, t + 2, :]), inc=sdbg3, dma=True)
                        P.op("gpsimd", lambda e: e.nop(),
                             waits=[(sdbg3, 16 * NV)])
            barrier()

        with ExitStack() as cctx:
            bufs2 = dict(
                uT=sb(cctx, "uT2", [128, GA, 8, 128], BF16),
                u_f=sb(cctx, "u_f2", [128, 1, H], F32),
                u_tm=sb(cctx, "u_tm2", [128, GA, H], BF16),
                vbuf=sb(cctx, "vbuf2", [128, GA, INNER], BF16),
                t2=sb(cctx, "t22", [128, 2, 512], F32),
                vq=sb(cctx, "vq2", [128, 1, INNER], BF16),
                wgup=sb(cctx, "wgup2", [128, 2, 2, 8, 512], BF16),
                wdb=sb(cctx, "wdb2", [128, 2, 8, 512], BF16),
                silu=sb(cctx, "silu2", [128, 2, 512], F32),
                outf=sb(cctx, "outf", [128, 3, H], F32),
            )
            bufs2["vqT"] = nc.alloc_sbuf_tensor_at(
                "vqT2", [128, GA, 32, 128], BF16,
                offset=nc.lookup_mloc(bufs2["vbuf"]).addr)
            ffn_app(2, list(range(2, NT)), lambda t: [], bufs2)
            P.op("sync", lambda e: e.nop(),
                 waits=[(s, s.n) for s in s_outp])

        with nc.Block() as block:
            @block.sync
            def _(eng):
                P.emit("sync", eng)

            @block.gpsimd
            def _(eng):
                P.emit("gpsimd", eng)

            @block.tensor
            def _(eng):
                P.emit("tensor", eng)

            @block.vector
            def _(eng):
                P.emit("vector", eng)

            @block.scalar
            def _(eng):
                P.emit("scalar", eng)

    return nc


def kernel(L0_attn_x, L0_attn_w_in, L0_attn_b_in, L0_attn_w_a, L0_attn_b_a,
           L0_attn_w_b, L0_attn_b_b, L0_attn_w_out, L0_attn_b_out,
           L0_ffn_w_gate, L0_ffn_w_up, L0_ffn_w_down, _debug_taps=False,
           _cores=None):
    x = np.asarray(L0_attn_x, np.float32)
    B, T, Hd = x.shape
    xf = np.ascontiguousarray(x.reshape(B * T, Hd))

    Cd, iDd, D2d = _dft_consts()
    shared = dict(
        wgT=np.ascontiguousarray(np.asarray(L0_ffn_w_gate, np.float32).T),
        wuT=np.ascontiguousarray(np.asarray(L0_ffn_w_up, np.float32).T),
        wdT=np.ascontiguousarray(np.asarray(L0_ffn_w_down, np.float32).T),
        w_in=np.ascontiguousarray(np.asarray(L0_attn_w_in, np.float32)),
        w_out=np.ascontiguousarray(np.asarray(L0_attn_w_out, np.float32)),
        wa=np.ascontiguousarray(np.asarray(L0_attn_w_a, np.float32)[:, 0, :]),
        wb=np.ascontiguousarray(np.asarray(L0_attn_w_b, np.float32)[:, 0, :]),
        b_in=np.ascontiguousarray(np.asarray(L0_attn_b_in, np.float32)[None, :]),
        b_a=np.ascontiguousarray(np.asarray(L0_attn_b_a, np.float32)[None, :]),
        b_b=np.ascontiguousarray(np.asarray(L0_attn_b_b, np.float32)[None, :]),
        b_out=np.ascontiguousarray(np.asarray(L0_attn_b_out,
                                              np.float32)[None, :]),
        Cd=Cd, iDd=iDd, D2d=D2d,
        identd=np.eye(128, dtype=BFnp),
    )

    wgT_s = shared["wgT"].reshape(16, 128, 2048)
    wuT_s = shared["wuT"].reshape(16, 128, 2048)
    wdT_s = shared["wdT"].reshape(16, 128, 2048)

    nc = build_program(debug_taps=_debug_taps)
    cores = list(range(8)) if _cores is None else _cores
    in_maps = []
    for core in cores:
        tok0 = core * TPC
        seq_start = (core % 4 == 0)
        halo = (np.zeros((256, Hd), np.float32) if seq_start
                else xf[tok0 - 256: tok0])
        xhv = np.ascontiguousarray(
            np.concatenate([halo, xf[tok0:tok0 + TPC]], 0))
        zmv = np.full((128, 1), 0.0 if seq_start else 1.0, np.float32)
        wsclv = np.ascontiguousarray(np.stack(
            [wgT_s[2 * core:2 * core + 2],
             wuT_s[2 * core:2 * core + 2],
             wdT_s[2 * core:2 * core + 2]]))
        in_maps.append(dict(xh=xhv, zmask=zmv, wscl=wsclv, **shared))

    res = run_bass_kernel_spmd(nc, in_maps, core_ids=list(range(len(cores))))
    if _debug_taps:
        return res
    outs = [np.asarray(r["out"]) for r in res.results]
    return np.concatenate(outs, axis=0).reshape(B, T, Hd)


# version 34 (final)
# speedup vs baseline: 1.1181x; 1.0615x over previous
import numpy as np
import ml_dtypes
import concourse.bass as bass
import concourse.mybir as mybir
from concourse.bass_utils import run_bass_kernel_spmd

F32 = mybir.dt.float32
BF16 = mybir.dt.bfloat16
AF = mybir.ActivationFunctionType
ALU = mybir.AluOpType
AX = mybir.AxisListType
BFnp = ml_dtypes.bfloat16

MAGIC = float(np.float32(1.5 * 2**23))
DBG_QUANT_ALL = False
DBG_TRUNC = None
DBG_RACE = False
DBG_NOMASK = False
H = 1024
INNER = 4096
CH = 2048
NT = 18
NV = 16
TPC = 2048
NBLK = 8
GA = 6
ZR = 4


class Sem:
    def __init__(self, nc, name):
        self.h = nc.alloc_semaphore(name)
        self.n = 0


class Prog:
    ENGS = ("sync", "gpsimd", "tensor", "vector", "scalar")

    def __init__(self, nc):
        self.nc = nc
        self.ops = {e: [] for e in self.ENGS}
        self.sems = {}

    def sem(self, name):
        if name not in self.sems:
            self.sems[name] = Sem(self.nc, name)
        return self.sems[name]

    def op(self, eng, fn, waits=(), inc=None, dma=False):
        amt = 16 if dma else 1
        incs = []
        self.gidx = getattr(self, 'gidx', 0) + 1
        ws = [(s, v) for w in waits if w is not None for (s, v) in [w] if v > 0]
        for s, v in ws:
            assert v <= s.n, (f"wait on {s.h.name} for {v} but only {s.n} "
                              f"produced so far (eng {eng})")
        if inc is not None:
            for it in (inc if isinstance(inc, list) else [inc]):
                if isinstance(it, tuple):
                    s, a = it
                else:
                    s, a = it, amt
                s.n += a
                incs.append((s, a, s.n))
        self.ops[eng].append((fn, ws, incs, self.gidx))
        return incs[0][2] if incs else None

    def carrier(self, eng_name, eng):
        if eng_name in ("vector", "scalar", "gpsimd"):
            ent = self.carriers[eng_name]
            buf, k, cap = ent
            assert k < cap, f"carrier slots exhausted for {eng_name}"
            ent[1] = k + 1
            p, c = (k % 4) * 32, k // 4
            if eng_name == "scalar":
                return eng.memzero(buf[p:p + 1, c:c + 1])
            return eng.memset(buf[p:p + 1, c:c + 1], 0.0)
        return eng.nop()

    def emit(self, eng_name, eng):
        ops = self.ops[eng_name]
        if DBG_TRUNC is not None:
            if isinstance(DBG_TRUNC, float):
                ops = ops[:max(1, int(len(ops) * DBG_TRUNC))]
            else:
                ops = [o for o in ops if o[3] <= DBG_TRUNC]
        for fn, ws, incs, _gi in ops:
            for s, v in ws:
                eng.wait_ge(s.h, v)
            ins = fn(eng)
            if incs:
                ins.then_inc(incs[0][0].h, incs[0][1])
                for s, a, _ in incs[1:]:
                    self.carrier(eng_name, eng).then_inc(s.h, a)


def _dft_consts():
    N, K, OUT = 512, 256, 256
    t = np.arange(N)
    C = np.zeros((N, N), np.float64)
    for fr in range(N):
        if fr <= 256:
            C[:, fr] = np.cos(2 * np.pi * fr * t / N)
        else:
            C[:, fr] = np.sin(2 * np.pi * (fr - 256) * t / N)
    tau = np.arange(OUT) + (N - OUT)
    iD = np.zeros((N, OUT), np.float64)
    for fr in range(N):
        if fr <= 256:
            c = 1.0 if fr in (0, 256) else 2.0
            iD[fr] = c / N * np.cos(2 * np.pi * fr * tau / N)
        else:
            iD[fr] = 2.0 / N * np.sin(2 * np.pi * (fr - 256) * tau / N)
    j = np.arange(K)
    jr = (K - 1 - j)[:, None]
    Wc = np.cos(2 * np.pi * np.arange(257)[None, :] * jr / N)
    Ws = np.sin(2 * np.pi * np.arange(256)[None, :] * jr / N)
    U0 = Wc[:, 0:128].copy()
    U1 = Wc[:, 128:256].copy()
    V0 = np.zeros((K, 128)); V0[:, 1:] = Ws[:, 1:128]
    V1 = Ws[:, 128:256].copy()
    Nt = np.zeros((K, 128)); Nt[:, 0] = Wc[:, 256]; Nt[:, 1:] = Wc[:, 1:128]
    D2 = np.stack([U0, U1, V0, V1, Nt], axis=1)
    Cd = np.ascontiguousarray(
        C.reshape(4, 128, 4, 128).transpose(1, 0, 2, 3)).astype(BFnp)
    iDd = np.ascontiguousarray(
        iD.reshape(4, 128, 2, 128).transpose(1, 0, 2, 3)).astype(BFnp)
    D2d = np.ascontiguousarray(
        D2.reshape(2, 128, 5, 128).transpose(1, 0, 2, 3)).astype(BFnp)
    return Cd, iDd, D2d


def build_program(debug_taps=False):
    nc = bass.Bass("TRN2", target_bir_lowering=False,
                   dynamic_dma_scratch_size=16384,
                   detect_race_conditions=DBG_RACE)

    def din(name, shp, dt=F32):
        return nc.dram_tensor(name, list(shp), dt, kind="ExternalInput").ap()

    xh = din("xh", [NT * 128, H])
    zmask = din("zmask", [128, 1])
    wgT = din("wgT", [H, INNER])
    wuT = din("wuT", [H, INNER])
    wdT = din("wdT", [INNER, H])
    w_in = din("w_in", [CH, H])
    w_out = din("w_out", [H, CH])
    wa = din("wa", [H, 256])
    wb = din("wb", [H, 256])
    b_in = din("b_in", [1, CH])
    b_a = din("b_a", [1, H])
    b_b = din("b_b", [1, H])
    b_out = din("b_out", [1, H])
    wscl = din("wscl", [3, 2, 128, 2048])
    Cd = din("Cd", [128, 4, 4, 128], BF16)
    iDd = din("iDd", [128, 4, 2, 128], BF16)
    D2d = din("D2d", [128, 2, 5, 128], BF16)
    identd = din("identd", [128, 128], BF16)

    out = nc.dram_tensor("out", [TPC, H], F32, kind="ExternalOutput").ap()
    taps_out = {}
    if debug_taps:
        for nm, shp in (("dbg_xmid", [NT * 128, H]), ("dbg_z", [NT * 128, CH]),
                        ("dbg_x3", [NV * 128, H]),):
            taps_out[nm] = nc.dram_tensor(nm, shp, F32,
                                          kind="ExternalOutput").ap()

    scr_g = nc.dram_tensor("scr_g", [8, 128, 8, 512], BF16, kind="Internal").ap()
    scr_u = nc.dram_tensor("scr_u", [8, 128, 8, 512], BF16, kind="Internal").ap()
    scr_d = nc.dram_tensor("scr_d", [2, 4, 128, 8, 512], BF16,
                           kind="Internal").ap()
    cc_in = nc.dram_tensor("cc_in", [128, 3], F32, kind="Internal").ap()
    cc_out = nc.dram_tensor("cc_out", [128, 3], F32, kind="Internal",
                            addr_space="Shared").ap()

    P = Prog(nc)
    S = P.sem

    from contextlib import ExitStack
    with ExitStack() as top:
        def sb(ctx, name, shp, dt):
            return ctx.enter_context(nc.sbuf_tensor(name, shp, dt))

        xs = sb(top, "xs", [128, NT, H], BF16)
        dumv = sb(top, "dumv", [128, 256], F32)
        dums = sb(top, "dums", [128, 128], F32)
        dumg = sb(top, "dumg", [128, 128], F32)
        P.carriers = {"vector": [dumv, 0, 1024], "scalar": [dums, 0, 512],
                      "gpsimd": [dumg, 0, 512]}
        ident = sb(top, "ident", [128, 128], BF16)
        ones1 = sb(top, "ones1", [128, 128], BF16)
        ones128f = sb(top, "ones128f", [128, 128], F32)
        magicrow = sb(top, "magicrow", [128, 1], F32)
        hp = sb(top, "hp", [128, 1], F32)
        hn = sb(top, "hn", [128, 1], F32)
        zmk = sb(top, "zmk", [128, 1], F32)
        wsum = sb(top, "wsum", [128, 3], F32)
        acol = sb(top, "acol", [128, 3, 1], F32)
        nwcol = sb(top, "nwcol", [128, 3], F32)
        swv = sb(top, "swv", [128, 3], F32)
        gww = sb(top, "gww", [128, 3], F32)
        cols = {}
        for nm in ("s1", "s2", "mx", "mn", "m", "ve", "rq", "r", "mab", "sp",
                   "nb", "q", "cg", "cu", "mt", "f", "cd", "t0", "t1"):
            cols[nm] = sb(top, "col_" + nm, [128, NT], F32)
        vmax = sb(top, "vmax", [128, GA, 8], F32)
        ppool = top.enter_context(nc.psum_tensor("ppool", [128, 7, 512], F32))
        ptr = top.enter_context(nc.psum_tensor("ptr", [128, 8, 128], BF16))

        psl = [S(f"psl{i}") for i in range(7)]
        puse = [0] * 7
        ptrsl = S("pslT")
        ptruse = [0]

        def pacq(i):
            w = (psl[i], puse[i]) if puse[i] else None
            puse[i] += 1
            return w

        def ptracq():
            w = (ptrsl, ptruse[0]) if ptruse[0] else None
            ptruse[0] += 1
            return w

        sx = S("sx")
        sinit = S("sinit")
        smem = S("smem")
        szmk = S("szmk")
        sident = S("sident")
        P.op("gpsimd", lambda g: g.memset(ones1[:, :], 1.0))
        P.op("gpsimd", lambda g: g.memset(ones128f[:, :], 1.0))
        P.op("gpsimd", lambda g: g.memset(magicrow[:, :], MAGIC))
        P.op("gpsimd", lambda g: g.memset(hp[:, :], 0.5))
        P.op("gpsimd", lambda g: g.memset(hn[:, :], -0.5), inc=smem)
        v_mem = smem.n
        P.op("gpsimd", lambda g: g.dma_start(zmk[:, :], zmask), inc=szmk, dma=True)
        P.op("gpsimd", lambda g: g.dma_start(ident[:, :], identd), inc=sident,
             dma=True)
        v_init = sident.n
        xh_t = xh.rearrange("(t p) h -> t p h", p=128)
        for t in range(NT):
            P.op("gpsimd", lambda g, t=t: g.dma_start(xs[:, t, :], xh_t[t]),
                 inc=sx, dma=True)

        with ExitStack() as actx:
            wfp = sb(actx, "wfp", [128, 2, 4, 512], F32)
            wsg = sb(actx, "wsg", [128, 2, 4, 512], BF16)
            bufs = dict(
                uT=sb(actx, "uT", [128, GA, 8, 128], BF16),
                u_f=sb(actx, "u_f", [128, 1, H], F32),
                u_tm=sb(actx, "u_tm", [128, GA, H], BF16),
                vbuf=sb(actx, "vbuf", [128, GA, INNER], BF16),
                t2=sb(actx, "t2", [128, 2, 512], F32),
                vq=sb(actx, "vq", [128, 1, INNER], BF16),
                wgup=sb(actx, "wgup", [128, 2, 2, 8, 512], BF16),
                wdb=sb(actx, "wdb", [128, 2, 8, 512], BF16),
                silu=sb(actx, "silu", [128, 2, 512], F32),
                outf=None,
            )
            bufs["vqT"] = nc.alloc_sbuf_tensor_at(
                "vqT", [128, GA, 32, 128], BF16,
                offset=nc.lookup_mloc(bufs["vbuf"]).addr)

            swfp = [S("swf0"), S("swf1")]
            sabs = S("sabs")
            sacc = S("sacc")
            sscale = S("sscale")
            s_ccl = S("s_ccl")
            s_ccd = S("s_ccd")
            kglob = 0
            for wi in range(3):
                for k in range(2):
                    par = kglob % 2
                    w = (sabs, kglob - 1) if kglob >= 2 else None
                    P.op("sync", lambda e, wi=wi, k=k, par=par: e.dma_start(
                        wfp[:, par, :, :].rearrange("p c o -> p (c o)"),
                        wscl[wi, k]),
                        waits=[w], inc=swfp[par], dma=True)
                    kglob += 1
                    P.op("scalar", lambda e, wi=wi, par=par: e.activation(
                        wsg[:, 0, :, :], wfp[:, par, :, :], AF.Abs,
                        accum_out=acol[:, wi, :]),
                        waits=[(swfp[par], swfp[par].n), (sabs, sabs.n),
                               (sacc, sacc.n)],
                        inc=sabs)
                    if k == 0:
                        P.op("vector", lambda e, wi=wi: e.tensor_copy(
                            wsum[:, wi:wi + 1], acol[:, wi, :]),
                            waits=[(sabs, kglob), (sacc, sacc.n)], inc=sacc)
                    else:
                        P.op("vector", lambda e, wi=wi: e.tensor_add(
                            wsum[:, wi:wi + 1], wsum[:, wi:wi + 1],
                            acol[:, wi, :]),
                            waits=[(sabs, kglob), (sacc, sacc.n)], inc=sacc)
            NSC = kglob
            P.op("gpsimd", lambda e: e.dma_start(cc_in, wsum[:, :]),
                 waits=[(sacc, NSC)], inc=s_ccl, dma=True)
            P.op("gpsimd", lambda e: e.collective_compute(
                "AllReduce", ALU.add,
                replica_groups=[[0, 1, 2, 3, 4, 5, 6, 7]],
                ins=[cc_in], outs=[cc_out]),
                waits=[(s_ccl, 16)], inc=s_ccd)
            P.op("gpsimd", lambda e: e.dma_start(wsum[:, :], cc_out),
                 waits=[(s_ccd, 1)], inc=s_ccl, dma=True)
            sred = S("sred")
            wsl = pacq(6)
            P.op("tensor", lambda e: e.matmul(ppool[:, 6, 0:3], ones128f[:, :],
                                              wsum[:, :], start=True, stop=True),
                 waits=[(s_ccl, 32), wsl, (smem, v_mem)], inc=sred)
            P.op("vector", lambda e: e.tensor_copy(nwcol[:, :], ppool[:, 6, 0:3]),
                 waits=[(sred, 1)], inc=[psl[6], sscale])
            P.op("vector", lambda e: e.tensor_scalar(
                nwcol[:, :], nwcol[:, :], 1.0 / (H * INNER), 1e-5,
                ALU.mult, ALU.max), waits=[(sscale, sscale.n)], inc=sscale)
            P.op("vector", lambda e: e.reciprocal(swv[:, :], nwcol[:, :]),
                 waits=[(sscale, sscale.n)], inc=sscale)
            P.op("vector", lambda e: e.tensor_scalar(
                gww[:, 0:2], nwcol[:, 0:2], 1.0 / 254.0, None, ALU.mult),
                waits=[(sscale, sscale.n)], inc=sscale)
            P.op("vector", lambda e: e.tensor_scalar(
                gww[:, 2:3], nwcol[:, 2:3], 0.25 / 127.0, None, ALU.mult),
                waits=[(sscale, sscale.n)], inc=sscale)
            v_scale = sscale.n

            sqfp = [S("sqf0"), S("sqf1")]
            ssgn = S("ssgn")
            sqdve = S("sqdve")
            sscrw = S("sscrw")
            qk = [0]

            sq_hist = []

            def q_load(src_view):
                par = qk[0] % 2
                w = (ssgn, 2 * (qk[0] - 1)) if qk[0] >= 2 else (sabs, 6)
                P.op("sync", lambda e, par=par: e.dma_start(
                    wfp[:, par, :, :], src_view),
                    waits=[w], inc=sqfp[par], dma=True)
                qk[0] += 1
                return par, sqfp[par].n

            def q_compute(ld, dst_sb, dst_scr, wi, extra_w=None):
                par, vqf = ld
                wsg_w = (sqdve, sq_hist[-2]) if len(sq_hist) >= 2 else None
                P.op("scalar", lambda e, par=par, wi=wi: e.activation(
                    wsg[:, par, :, :], wfp[:, par, :, :], AF.Sign,
                    bias=hn[:, :], scale=swv[:, wi:wi + 1]),
                    waits=[(sqfp[par], vqf), (sscale, v_scale), (smem, v_mem),
                           wsg_w, (sabs, 6)],
                    inc=ssgn)
                P.op("scalar", lambda e, par=par, wi=wi, dst_sb=dst_sb:
                     e.activation(dst_sb, wfp[:, par, :, :], AF.Sign,
                                  bias=hp[:, :], scale=swv[:, wi:wi + 1]),
                     waits=[extra_w, (sscrw, sscrw.n)], inc=ssgn)
                vs = ssgn.n
                P.op("vector", lambda e, dst_sb=dst_sb, par=par:
                     e.tensor_add(dst_sb, dst_sb, wsg[:, par, :, :]),
                     waits=[(ssgn, vs)], inc=sqdve)
                vq_ = sqdve.n
                sq_hist.append(vq_)
                P.op("sync", lambda e, dst_scr=dst_scr, dst_sb=dst_sb:
                     e.dma_start(dst_scr, dst_sb),
                     waits=[(sqdve, vq_)], inc=sscrw, dma=True)
                return vq_

            def quant_chunk(src_view, dst_sb, dst_scr, wi, extra_w=None):
                return q_compute(q_load(src_view), dst_sb, dst_scr, wi,
                                 extra_w)

            s_sq = S("s_sq")
            s_sqs = S("s_sqs")
            s_row = S("s_row")
            s_sqrt = S("s_sqrt")
            s_ufw = S("s_ufw")
            s_utm = S("s_utm")
            s_utp = S("s_utp")
            s_uT = S("s_uT")
            s_wqlp = [S("s_wql0"), S("s_wql1")]
            s_wulp = [S("s_wul0"), S("s_wul1")]
            s_wdhp = [S("s_wdh0"), S("s_wdh1")]
            s_gup = S("s_gup")
            s_silu = S("s_silu")
            s_vtt = S("s_vtt")
            s_vw = S("s_vw")
            s_row2 = S("s_row2")
            s_t2 = S("s_t2")
            s_vq8 = S("s_vq8")
            s_vqtp = [S("s_vqt0"), S("s_vqt1")]
            s_wdp = [S("s_wd0"), S("s_wd1")]
            s_dwn = S("s_dwn")
            s_out = S("s_out")

            wgT_q = wgT.rearrange("(a p) (c o) -> p a c o", p=128, o=512)
            wuT_q = wuT.rearrange("(a p) (c o) -> p a c o", p=128, o=512)
            wdT_q = wdT.rearrange("(a p) h -> p a h", p=128)

            s_dmg = S("s_dmg")
            uT_rd = {i: None for i in range(GA)}
            vq_hrd = {0: None, 1: None}
            gup_hist = []
            out_store = {}
            s_outp = [S(f"s_outp{i}") for i in range(6)]
            dmg_hist = []
            out_hist = []

            def ffn_app(app, tiles, xwait, bufs):
                uT, u_f, u_tm = bufs["uT"], bufs["u_f"], bufs["u_tm"]
                vbuf, t2, vq = bufs["vbuf"], bufs["t2"], bufs["vq"]
                wgup, wdb, silub = bufs["wgup"], bufs["wdb"], bufs["silu"]
                vqT, outf = bufs["vqT"], bufs["outf"]
                groups = [tiles[i:i + GA] for i in range(0, len(tiles), GA)]

                def rv(fn, waits=(), inc=None):
                    ws = list(waits) + [(s_row, s_row.n)]
                    ics = [s_row] + ([] if inc is None else
                                     (inc if isinstance(inc, list)
                                      else [inc]))
                    P.op("vector", fn, waits=ws, inc=ics)

                def stats_tile(t):
                    wu = (s_utm, s_utm.n) if s_utm.n else None
                    P.op("scalar", lambda e, t=t: e.activation(
                        u_f[:, 0, :], xs[:, t, :], AF.Square,
                        accum_out=cols["s2"][:, t:t + 1]),
                        waits=list(xwait(t)) + [wu, (s_sqs, s_sqs.n)],
                        inc=s_sqs)
                    yield
                    P.op("scalar", lambda e, t=t: e.activation(
                        u_f[:, 0, :], xs[:, t, :], AF.Identity,
                        accum_out=cols["s1"][:, t:t + 1]),
                        waits=list(xwait(t)) + [(s_sqs, s_sqs.n)],
                        inc=s_sqs)
                    yield
                    P.op("vector", lambda e, t=t: e.reduce_max(
                        cols["mx"][:, t:t + 1], xs[:, t, :], AX.X),
                        waits=list(xwait(t)))
                    yield
                    P.op("vector", lambda e, t=t: e.tensor_reduce(
                        cols["mn"][:, t:t + 1], xs[:, t, :], AX.X,
                        ALU.min), inc=s_sq)
                    yield

                def rowmath_gen(grp, v_sq, v_sqs):
                    G = len(grp)
                    sl = (slice(None), slice(grp[0], grp[0] + G))
                    rv(lambda e, sl=sl: e.tensor_scalar(cols["m"][sl], cols["s1"][sl],
                                                 1.0 / H, None, ALU.mult),
                       waits=[(s_sq, v_sq), (s_sqs, v_sqs)])
                    yield
                    rv(lambda e, sl=sl: e.tensor_scalar(cols["ve"][sl], cols["s2"][sl],
                                                 1.0 / H, None, ALU.mult))
                    yield
                    rv(lambda e, sl=sl: e.tensor_tensor(cols["t0"][sl], cols["m"][sl],
                                                 cols["m"][sl], ALU.mult))
                    yield
                    rv(lambda e, sl=sl: e.tensor_scalar(cols["t0"][sl], cols["t0"][sl],
                                                 -1.0, 1e-5, ALU.mult, ALU.add))
                    yield
                    rv(lambda e, sl=sl: e.tensor_tensor(cols["ve"][sl], cols["ve"][sl],
                                                 cols["t0"][sl], ALU.add))
                    yield
                    rv(lambda e, sl=sl: e.reciprocal(cols["rq"][sl], cols["ve"][sl]))
                    yield
                    v_row = s_row.n
                    P.op("scalar", lambda e, sl=sl: e.activation(
                        cols["r"][sl], cols["rq"][sl], AF.Sqrt),
                        waits=[(s_row, v_row)], inc=s_sqrt)
                    yield
                    v_sqrt = s_sqrt.n
                    rv(lambda e, sl=sl: e.tensor_tensor(cols["t0"][sl], cols["mx"][sl],
                                                 cols["m"][sl], ALU.subtract),
                       waits=[(s_sqrt, v_sqrt)])
                    yield
                    rv(lambda e, sl=sl: e.tensor_tensor(cols["t1"][sl], cols["m"][sl],
                                                 cols["mn"][sl], ALU.subtract))
                    yield
                    rv(lambda e, sl=sl: e.tensor_tensor(cols["mab"][sl], cols["t0"][sl],
                                                 cols["t1"][sl], ALU.max))
                    yield
                    rv(lambda e, sl=sl: e.tensor_scalar(cols["t0"][sl], cols["mab"][sl],
                                                 1e-12, None, ALU.max))
                    yield
                    rv(lambda e, sl=sl: e.reciprocal(cols["t1"][sl], cols["t0"][sl]))
                    yield
                    rv(lambda e, sl=sl: e.tensor_scalar(cols["sp"][sl], cols["t1"][sl],
                                                 127.0, None, ALU.mult))
                    yield
                    rv(lambda e, sl=sl: e.tensor_tensor(cols["t0"][sl], cols["m"][sl],
                                                 cols["sp"][sl], ALU.mult))
                    yield
                    rv(lambda e, sl=sl: e.tensor_scalar(cols["nb"][sl], cols["t0"][sl],
                                                 -1.0, None, ALU.mult))
                    yield
                    rv(lambda e, sl=sl: e.tensor_tensor(cols["q"][sl], cols["r"][sl],
                                                 cols["mab"][sl], ALU.mult))
                    yield
                    rv(lambda e, sl=sl: e.tensor_scalar(cols["cg"][sl], cols["q"][sl],
                                                 gww[:, 0:1], None, ALU.mult))
                    yield
                    rv(lambda e, sl=sl: e.tensor_scalar(cols["cu"][sl], cols["q"][sl],
                                                 gww[:, 1:2], None, ALU.mult))
                    yield

                def head_gen(grp, out):
                    for ti, t in enumerate(grp):
                        yield from stats_tile(t)
                    yield from rowmath_gen(grp, s_sq.n, s_sqs.n)
                    v_row = s_row.n
                    pend = []
                    for ti, t in enumerate(grp):
                        P.op("scalar", lambda e, t=t: e.activation(
                            u_f[:, 0, :], xs[:, t, :], AF.Identity,
                            bias=cols["nb"][:, t:t + 1],
                            scale=cols["sp"][:, t:t + 1]),
                            waits=[(s_row, v_row),
                                   (s_utm, s_utm.n)
                                   if s_utm.n else None],
                            inc=s_ufw)
                        yield
                        v_ufw = s_ufw.n
                        wtr = (s_uT, uT_rd[ti]) if uT_rd[ti] else None
                        P.op("vector", lambda e, ti=ti:
                             e.tensor_scalar(
                                 u_tm[:, ti, :], u_f[:, 0, :], MAGIC,
                                 MAGIC, ALU.add, ALU.subtract),
                             waits=[(s_ufw, v_ufw), wtr], inc=s_utm)
                        pend.append((ti, s_utm.n))
                        yield
                    out["pend"] = pend

                def emit_utp(grp, pend):
                    uT_base = s_uT.n
                    for ti, (slot, v_utm) in enumerate(pend):
                        wprev = (s_gup, gup_hist[-1]) if gup_hist else None
                        wchain = (s_uT, s_uT.n) if s_uT.n else None
                        P.op("sync", lambda e, ti=ti, slot=slot:
                             e.dma_start_transpose(
                                 uT[:, ti, :, :],
                                 u_tm[:, slot, :].rearrange(
                                     "p (c f) -> p c f", f=128)),
                             waits=[(s_utm, v_utm), wprev, wchain],
                             inc=s_uT, dma=True)
                        uT_rd[slot] = s_uT.n
                    return uT_base

                head = {}
                pard_done = {0: None, 1: None}
                for gi, grp in enumerate(groups):
                    G = len(grp)
                    v_dmm_prev = s_dmg.n
                    nhs = {}
                    ngen = None
                    if gi not in head:
                        hstate = {}
                        for _ in head_gen(grp, hstate):
                            pass
                        uT_base = emit_utp(grp, hstate["pend"])
                    else:
                        uT_base = head[gi]
                    if gi + 1 < len(groups) \
                            and not (app == 1 and gi == 0):
                        ngen = head_gen(groups[gi + 1], nhs)

                    vqt_done = {}

                    vtt_tile = {}

                    def emit_vchain(ti, t):
                        tc = (slice(None), slice(t, t + 1))
                        rv(lambda e, ti=ti, tc=tc: e.tensor_reduce(
                            cols["mt"][tc], vmax[:, ti, :], AX.X, ALU.max),
                           waits=[(s_vtt, vtt_tile[ti])])
                        rv(lambda e, tc=tc: e.tensor_scalar(
                            cols["t0"][tc], cols["mt"][tc], 1e-12, None,
                            ALU.max))
                        rv(lambda e, tc=tc: e.reciprocal(
                            cols["t1"][tc], cols["t0"][tc]))
                        rv(lambda e, tc=tc: e.tensor_scalar(
                            cols["f"][tc], cols["t1"][tc], 127.0, None,
                            ALU.mult))
                        rv(lambda e, tc=tc: e.tensor_tensor(
                            cols["t0"][tc], cols["cu"][tc], cols["mt"][tc],
                            ALU.mult))
                        rv(lambda e, tc=tc: e.tensor_scalar(
                            cols["cd"][tc], cols["t0"][tc], gww[:, 2:3], None,
                            ALU.mult), inc=s_row2)
                        v_row2_t = s_row2.n
                        wvq_h = [(s_vqtp[0], vq_hrd[0]) if vq_hrd[0] else None,
                                 (s_vqtp[0], vq_hrd[1]) if vq_hrd[1] else None]
                        for q8 in range(8):
                            pt2 = s_t2.n % 2
                            P.op("scalar", lambda e, ti=ti, t=t, q8=q8,
                                 pt2=pt2:
                                 e.activation(
                                     t2[:, pt2, :],
                                     vbuf[:, ti, q8 * 512:(q8 + 1) * 512],
                                     AF.Identity,
                                     scale=cols["f"][:, t:t + 1]),
                                 waits=[(s_row2, v_row2_t), (smem, v_mem),
                                        (s_vq8, s_t2.n - 1)
                                        if s_t2.n >= 2 else None,
                                        wvq_h[q8 // 4]
                                        if q8 % 4 == 0 else None],
                                 inc=s_t2)
                            v_t2 = s_t2.n
                            P.op("vector", lambda e, q8=q8, pt2=pt2:
                                 e.tensor_scalar(
                                     vq[:, 0, q8 * 512:(q8 + 1) * 512],
                                     t2[:, pt2, :], MAGIC, MAGIC,
                                     ALU.add, ALU.subtract),
                                 waits=[(s_t2, v_t2)], inc=s_vq8)
                            if q8 == 3 or q8 == 7:
                                hh = q8 // 4
                                v_vq8 = s_vq8.n
                                wch = (s_vqtp[0], s_vqtp[0].n) \
                                    if s_vqtp[0].n else None
                                P.op("sync", lambda e, ti=ti, hh=hh:
                                     e.dma_start_transpose(
                                         vqT[:, ti, 16 * hh:16 * hh + 16, :],
                                         vq[:, 0, 2048 * hh:2048 * hh + 2048]
                                         .rearrange("p (c f) -> p c f",
                                                    f=128)),
                                     waits=[(s_vq8, v_vq8), wch],
                                     inc=s_vqtp[0], dma=True)
                                vq_hrd[hh] = s_vqtp[0].n
                        vqt_done[ti] = (s_vqtp[0], s_vqtp[0].n)

                    qg = (app == 1 and (gi == 0 or DBG_QUANT_ALL))
                    qdone = {}
                    qld = {}

                    def q_src(oc2, ci):
                        wi2, hh2 = ci // 2, ci % 2
                        wv = (wgT_q, wuT_q)[wi2]
                        scr = (scr_g, scr_u)[wi2]
                        return (wv[:, 4 * hh2:4 * hh2 + 4, oc2, :],
                                wgup[:, oc2 % 2, wi2,
                                     4 * hh2:4 * hh2 + 4, :],
                                scr[oc2][:, 4 * hh2:4 * hh2 + 4, :], wi2)

                    def q_drip_load(oc2, ci):
                        srcv, dst, scrv, wi2 = q_src(oc2, ci)
                        qld[(oc2, ci)] = (q_load(srcv), dst, scrv, wi2)

                    def q_drip_comp(oc2, ci, wclob2):
                        ld, dst, scrv, wi2 = qld.pop((oc2, ci))
                        qdone[oc2] = q_compute(ld, dst, scrv, wi2,
                                               extra_w=wclob2)
                    if qg:
                        for ci in range(4):
                            q_drip_load(0, ci)
                            q_drip_comp(0, ci, None)

                    for oc in range(8):
                        par = oc % 2
                        wclob = (s_gup, gup_hist[-2]) if len(gup_hist) >= 2 \
                            else None
                        if qg:
                            wq_wait = [(sqdve, qdone[oc])]
                        else:
                            P.op("gpsimd", lambda e, oc=oc, par=par:
                                 e.dma_start(
                                     wgup[:, par, 0, :, :], scr_g[oc]),
                                 waits=[wclob, (sscrw, sscrw.n)],
                                 inc=s_wqlp[par], dma=True)
                            P.op("gpsimd", lambda e, oc=oc, par=par:
                                 e.dma_start(
                                     wgup[:, par, 1, :, :], scr_u[oc]),
                                 inc=s_wqlp[par], dma=True)
                            wq_wait = [(s_wqlp[par], s_wqlp[par].n)]
                        for ti, t in enumerate(grp):
                            pp = s_gup.n % 3
                            wg_, wu_ = pacq(2 * pp), pacq(2 * pp + 1)
                            for icc in range(8):
                                wts = (wq_wait +
                                       [(s_uT, uT_base + 16 * (ti + 1)),
                                        wg_, wu_] if icc == 0 else [])
                                P.op("tensor", lambda e, icc=icc, ti=ti,
                                     par=par, pp=pp:
                                     e.matmul(ppool[:, 2 * pp, :],
                                              uT[:, ti, icc, :],
                                              wgup[:, par, 0, icc, :],
                                              start=(icc == 0), stop=(icc == 7)),
                                     waits=wts)
                                P.op("tensor", lambda e, icc=icc, ti=ti,
                                     par=par, pp=pp:
                                     e.matmul(ppool[:, 2 * pp + 1, :],
                                              uT[:, ti, icc, :],
                                              wgup[:, par, 1, icc, :],
                                              start=(icc == 0), stop=(icc == 7)),
                                     inc=(s_gup if icc == 7 else None))
                            v_gup = s_gup.n
                            sp_ = s_silu.n % 2
                            P.op("scalar", lambda e, t=t, pp=pp, sp_=sp_:
                                 e.activation(silub[:, sp_, :],
                                              ppool[:, 2 * pp, :], AF.Silu,
                                              scale=cols["cg"][:, t:t + 1]),
                                 waits=[(s_gup, v_gup),
                                        (s_vw, s_vw.n - 1)
                                        if s_silu.n >= 2 else None],
                                 inc=s_silu)
                            v_silu = s_silu.n
                            P.op("vector", lambda e, ti=ti, oc=oc, pp=pp,
                                 sp_=sp_:
                                 e.tensor_tensor(
                                     vbuf[:, ti, oc * 512:(oc + 1) * 512],
                                     silub[:, sp_, :], ppool[:, 2 * pp + 1, :],
                                     ALU.mult),
                                 waits=[(s_silu, v_silu),
                                        (s_dmg, v_dmm_prev)],
                                 inc=[s_vw, psl[2 * pp], psl[2 * pp + 1]])
                            P.op("vector", lambda e, ti=ti, oc=oc:
                                 e.reduce_max(
                                     vmax[:, ti, oc:oc + 1],
                                     vbuf[:, ti, oc * 512:(oc + 1) * 512],
                                     AX.X, apply_absolute_value=True),
                                 waits=[(s_vw, s_vw.n)], inc=s_vtt)
                            if oc == 7:
                                vtt_tile[ti] = s_vtt.n
                            if ngen is not None and oc >= 1:
                                for _ in range(1 if oc < 5 else 2):
                                    if next(ngen, "END") == "END":
                                        ngen = None
                                        break
                            if qg and oc < 7:
                                if ti < 4:
                                    q_drip_load(oc + 1, ti)
                                if 1 <= ti <= 4:
                                    q_drip_comp(oc + 1, ti - 1,
                                                (s_gup, gup_hist[-1])
                                                if gup_hist else None)
                        gup_hist.append(s_gup.n)


                    for ti, t in enumerate(grp):
                        emit_vchain(ti, t)

                    preload = {}
                    if ngen is not None:
                        for _ in ngen:
                            pass
                        ngen = None
                    if "pend" in nhs:
                        head[gi + 1] = emit_utp(groups[gi + 1], nhs["pend"])

                    dmg_tile = {}
                    dwn_tile = {}
                    for hc in range(2):
                        if hc == 0 and G > 4:
                            sched = [(0, list(range(0, 4))),
                                     (1, list(range(0, 4))),
                                     (0, list(range(4, G))),
                                     (1, list(range(4, G))),
                                     (2, list(range(G))),
                                     (3, list(range(G)))]
                        else:
                            sched = [(icg, list(range(G)))
                                     for icg in range(4)]
                        loaded = {}
                        done_cnt = {k: 0 for k in range(4)}
                        for icg, tis in sched:
                            if not tis:
                                continue
                            pard = (hc * 4 + icg) % 2
                            first_chunk = icg not in loaded
                            if first_chunk:
                                wdclob = (s_dmg, pard_done[pard]) \
                                    if pard_done[pard] else None
                                if (hc, icg) in preload:
                                    loaded[icg] = preload[(hc, icg)]
                                elif app == 1 and (gi == 0
                                                   or DBG_QUANT_ALL):
                                    vg = None
                                    for hh in range(2):
                                        vg = quant_chunk(
                                            wdT_q[:, icg * 8 + 4 * hh:
                                                  icg * 8 + 4 * hh + 4,
                                                  hc * 512:(hc + 1) * 512],
                                            wdb[:, pard,
                                                4 * hh:4 * hh + 4, :],
                                            scr_d[hc, icg]
                                            [:, 4 * hh:4 * hh + 4, :],
                                            2, extra_w=wdclob)
                                    loaded[icg] = [(sqdve, vg)]
                                else:
                                    P.op("gpsimd", lambda e, hc=hc, icg=icg,
                                         pard=pard: e.dma_start(
                                             wdb[:, pard, :, :],
                                             scr_d[hc, icg]),
                                         waits=[wdclob, (sscrw, sscrw.n)],
                                         inc=s_wdp[pard], dma=True)
                                    loaded[icg] = \
                                        [(s_wdp[pard], s_wdp[pard].n)]
                            for ti in tis:
                                for icc in range(8):
                                    wts = []
                                    if icc == 0:
                                        if ti == tis[0] and first_chunk:
                                            wts += loaded[icg]
                                        if icg == 0 and hc == 0:
                                            wts += [vqt_done[ti], pacq(ti)]
                                        if icg == 0 and hc == 1:
                                            wts.append(
                                                (s_dwn, dwn_tile[ti]))
                                    st = (icg == 0 and icc == 0)
                                    sp2 = (icg == 3 and icc == 7)
                                    incs = [s_dmg] if icc == 7 else None
                                    P.op("tensor", lambda e, icc=icc, ti=ti,
                                         icg=icg, pard=pard, st=st, sp2=sp2:
                                         e.matmul(ppool[:, ti, :],
                                                  vqT[:, ti, icg * 8 + icc, :],
                                                  wdb[:, pard, icc, :],
                                                  start=st, stop=sp2,
                                                  skip_group_check=True),
                                         waits=wts, inc=incs)
                                if icg == 3:
                                    dmg_tile[ti] = s_dmg.n
                            done_cnt[icg] += len(tis)
                            if done_cnt[icg] == G:
                                pard_done[pard] = s_dmg.n
                        dmg_hist.append(s_dmg.n)
                        for ti, t in enumerate(grp):
                            last = (hc == 1)
                            if app == 1:
                                P.op("vector", lambda e, ti=ti, t=t, hc=hc:
                                     e.scalar_tensor_tensor(
                                         xs[:, t, hc * 512:(hc + 1) * 512],
                                         ppool[:, ti, :],
                                         cols["cd"][:, t:t + 1],
                                         xs[:, t, hc * 512:(hc + 1) * 512],
                                         ALU.mult, ALU.add),
                                     waits=[(s_dmg, dmg_tile[ti])],
                                     inc=([s_dwn, psl[ti]] if last
                                          else [s_dwn]))
                            else:
                                gidx = gi * GA + ti
                                slot = gidx % 3
                                strm = (gidx % 3) + 3 * hc
                                wout = (s_outp[strm],
                                        out_store.get((gidx - 3, hc), 0)) \
                                    if gidx >= 3 else None
                                P.op("vector", lambda e, ti=ti, t=t, hc=hc,
                                     slot=slot: e.scalar_tensor_tensor(
                                         bufs["outf"][:, slot,
                                                      hc * 512:(hc + 1) * 512],
                                         ppool[:, ti, :],
                                         cols["cd"][:, t:t + 1],
                                         xs[:, t, hc * 512:(hc + 1) * 512],
                                         ALU.mult, ALU.add),
                                     waits=[(s_dmg, dmg_tile[ti]), wout],
                                     inc=([s_dwn, psl[ti]] if last
                                          else [s_dwn]))
                                out_t = out.rearrange("(t p) h -> t p h",
                                                      p=128)
                                P.op("sync", lambda e, t=t, slot=slot, hc=hc:
                                     e.dma_start(
                                         out_t[t - 2][:, hc * 512:(hc + 1) * 512],
                                         bufs["outf"][:, slot,
                                                      hc * 512:(hc + 1) * 512]),
                                     waits=[(s_dwn, s_dwn.n)],
                                     inc=s_outp[strm], dma=True)
                                out_store[(gidx, hc)] = s_outp[strm].n
                            if hc == 0:
                                dwn_tile[ti] = s_dwn.n

            ffn_app(1, list(range(NT)), lambda t: [(sx, 16 * NT)], bufs)

            sbar = S("sbar")

            def barrier():
                base = sbar.n
                for e in Prog.ENGS:
                    P.op(e, lambda x, e=e: P.carrier(e, x), inc=sbar)
                for e in Prog.ENGS:
                    P.op(e, lambda x: x.nop(), waits=[(sbar, base + 5)])

            if debug_taps:
                barrier()
                dxm = taps_out["dbg_xmid"].rearrange("(t p) h -> t p h", p=128)
                sdbg = S("sdbg")
                for t in range(NT):
                    P.op("gpsimd", lambda e, t=t: e.dma_start(dxm[t], xs[:, t, :]),
                         inc=sdbg, dma=True)
                P.op("gpsimd", lambda e: e.nop(), waits=[(sdbg, 16 * NT)])
            barrier()

        with ExitStack() as bctx:
            zf = sb(bctx, "zf", [128, ZR, CH], BF16)
            winT = sb(bctx, "winT", [128, 8, CH], BF16)
            woutT = sb(bctx, "woutT", [128, 16, H], BF16)
            tabs = sb(bctx, "tabs", [128, 5, CH], BF16)
            Csb = sb(bctx, "Csb", [128, 4, 4, 128], BF16)
            iDsb = sb(bctx, "iDsb", [128, 4, 2, 128], BF16)
            bjnk = sb(bctx, "bjnk", [128, H], BF16)
            brows = sb(bctx, "brows", [128, CH], BF16)
            binb = brows[0:1, :]
            bab = brows[32:33, :]
            boutb = brows[64:65, 0:H]
            u2tm = sb(bctx, "u2tm", [128, 1, H], BF16)
            u2T = sb(bctx, "u2T", [128, 2, 8, 128], BF16)

            s_pm = S("s_pm")
            u2Tp = [S("u2Tp0"), S("u2Tp1")]
            sB = {k: S("b_" + k) for k in
                  ("cload", "wint", "woutt", "tapt", "tab", "tabev", "zrow",
                   "zsqrt", "u2", "u2t", "u2T", "zmm", "zev", "fwd", "fev",
                   "pch", "idft", "sig", "gat", "catp", "catev", "omm", "oev",
                   "stat")}

            P.op("gpsimd", lambda e: e.dma_start(Csb[:, :, :, :], Cd),
                 inc=sB["cload"], dma=True)
            P.op("gpsimd", lambda e: e.dma_start(iDsb[:, :, :, :], iDd),
                 inc=sB["cload"], dma=True)
            P.op("gpsimd", lambda e: e.dma_start(binb[:, :], b_in),
                 inc=sB["cload"], dma=True)
            P.op("gpsimd", lambda e: e.dma_start(bab[:, 0:H], b_a),
                 inc=sB["cload"], dma=True)
            P.op("gpsimd", lambda e: e.dma_start(bab[:, H:CH], b_b),
                 inc=sB["cload"], dma=True)
            P.op("gpsimd", lambda e: e.dma_start(boutb[:, :], b_out),
                 inc=sB["cload"], dma=True)
            v_cload = sB["cload"].n

            with ExitStack() as wctx:
                wstg = sb(wctx, "wstg", [128, 2, 4096], BF16)
                tapb = sb(wctx, "tapb", [128, 16, 256], BF16)
                tapT = sb(wctx, "tapT", [128, 2, CH], BF16)
                D2sb = sb(wctx, "D2sb", [128, 2, 5, 128], BF16)
                sldp = [S("b_sld0"), S("b_sld1")]
                sD2 = S("b_sD2")
                staps = S("b_staps")
                wintp = [S("b_wint0"), S("b_wint1")]
                wouttp = [S("b_woutt0"), S("b_woutt1")]
                P.op("gpsimd", lambda e: e.dma_start(D2sb[:, :, :, :], D2d),
                     inc=sD2, dma=True)
                wa_r = wa.rearrange("(g p) k -> p g k", p=128)
                wb_r = wb.rearrange("(g p) k -> p g k", p=128)
                P.op("gpsimd", lambda e: e.dma_start(tapb[:, 0:8, :], wa_r),
                     inc=staps, dma=True)
                P.op("gpsimd", lambda e: e.dma_start(tapb[:, 8:16, :], wb_r),
                     inc=staps, dma=True)
                v_tapl = staps.n
                for g in range(16):
                    P.op("sync", lambda e, g=g: e.dma_start_transpose(
                        tapT[:, :, g * 128:(g + 1) * 128],
                        tapb[:, g, :].rearrange("p (j f) -> p j f", f=128)),
                        waits=[(staps, v_tapl)], inc=sB["tapt"], dma=True)
                v_tapt = sB["tapt"].n
                for tb in range(5):
                    for fc in range(4):
                        slw = pacq(6)
                        for jc in range(2):
                            P.op("tensor", lambda e, tb=tb, fc=fc, jc=jc:
                                 e.matmul(ppool[:, 6, :],
                                          D2sb[:, jc, tb, :],
                                          tapT[:, jc, fc * 512:(fc + 1) * 512],
                                          start=(jc == 0), stop=(jc == 1)),
                                 waits=([(sB["tapt"], v_tapt),
                                         (sD2, 16), slw]
                                        if jc == 0 else []),
                                 inc=(sB["tab"] if jc == 1 else None))
                        v_tab = sB["tab"].n
                        P.op("vector", lambda e, tb=tb, fc=fc: e.tensor_copy(
                            tabs[:, tb, fc * 512:(fc + 1) * 512],
                            ppool[:, 6, :]),
                            waits=[(sB["tab"], v_tab)],
                            inc=[psl[6], sB["tabev"]])
                v_tabev = sB["tabev"].n
                w_in_r = w_in.rearrange("(g p) h -> p g h", p=128)
                w_out_r = w_out.rearrange("(g p) c -> p g c", p=128)
                wstg_v = [wstg[:, 0, :].rearrange("p (g h) -> p g h", h=H),
                          wstg[:, 1, :].rearrange("p (g h) -> p g h", h=H)]
                tpar_hist = {0: [], 1: []}
                win_cc = []
                for pc in range(8):
                    par = pc % 2
                    w = (wintp[par], tpar_hist[par][-1]) \
                        if tpar_hist[par] else None
                    P.op("gpsimd", lambda e, pc=pc, par=par: e.dma_start(
                        wstg_v[par][:, 0:2, :],
                        w_in_r[:, 2 * pc:2 * pc + 2, :]),
                        waits=[w], inc=sldp[par], dma=True)
                    v_ld = sldp[par].n
                    for g in range(2):
                        cc = 2 * pc + g
                        P.op("sync", lambda e, par=par, cc=cc, g=g:
                             e.dma_start_transpose(
                                 winT[:, :, cc * 128:(cc + 1) * 128],
                                 wstg[:, par, g * H:(g + 1) * H]
                                 .rearrange("p (i f) -> p i f", f=128)),
                             waits=[(sldp[par], v_ld)],
                             inc=wintp[par], dma=True)
                    tpar_hist[par].append(wintp[par].n)
                    if pc % 2 == 1:
                        win_cc.append((wintp[par], wintp[par].n))
                opar_hist = {0: [], 1: []}
                for pc in range(4):
                    par = pc % 2
                    w1 = (wintp[par], tpar_hist[par][-1])
                    w2 = (wouttp[par], opar_hist[par][-1]) \
                        if opar_hist[par] else None
                    P.op("gpsimd", lambda e, pc=pc, par=par: e.dma_start(
                        wstg[:, par, :].rearrange("p (g c) -> p g c", c=CH),
                        w_out_r[:, 2 * pc:2 * pc + 2, :]),
                        waits=[w1, w2], inc=sldp[par], dma=True)
                    v_ld = sldp[par].n
                    for g in range(2):
                        hc = 2 * pc + g
                        P.op("sync", lambda e, par=par, hc=hc, g=g:
                             e.dma_start_transpose(
                                 woutT[:, :, hc * 128:(hc + 1) * 128],
                                 wstg[:, par, g * CH:(g + 1) * CH]
                                 .rearrange("p (i f) -> p i f", f=128)),
                             waits=[(sldp[par], v_ld)],
                             inc=wouttp[par], dma=True)
                    opar_hist[par].append(wouttp[par].n)
                win_waits = [(wintp[0], wintp[0].n), (wintp[1], wintp[1].n)]
                wout_waits = [(wouttp[0], wouttp[0].n),
                              (wouttp[1], wouttp[1].n)]
                s_bdrain = S("s_bdrain")
                P.op("sync", lambda e: e.nop(),
                     waits=[(sB["tapt"], sB["tapt"].n),
                            (wintp[0], wintp[0].n), (wintp[1], wintp[1].n),
                            (wouttp[0], wouttp[0].n),
                            (wouttp[1], wouttp[1].n)], inc=s_bdrain)
                P.op("tensor", lambda e: e.nop(),
                     waits=[(sB["tabev"], sB["tabev"].n)], inc=s_bdrain)
                s_wprep = S("s_wprep")
                P.op("sync", lambda e: e.nop(),
                     waits=[(sB["tapt"], sB["tapt"].n),
                            (wintp[0], wintp[0].n), (wintp[1], wintp[1].n),
                            (wouttp[0], wouttp[0].n), (wouttp[1], wouttp[1].n),
                            (sldp[0], sldp[0].n), (sldp[1], sldp[1].n),
                            (sD2, sD2.n), (staps, staps.n)], inc=s_wprep)

            if True:
                with ExitStack() as pctx:
                    zh = sb(pctx, "zh", [128, 4, CH], BF16)
                    Pb = sb(pctx, "Pb", [128, 4, CH], BF16)
                    ptmp = sb(pctx, "ptmp", [128, CH], BF16)
                    sigb = sb(pctx, "sigb", [128, 2, 512], BF16)
                    gat = sb(pctx, "gat", [128, CH], BF16)
                    catT = sb(pctx, "catT", [128, 16, 128], BF16)

                    fwd_done_after_block = []
                    idft_done_after_block = []
                    zmm_hist = []
                    head_tp = {}
                    pch_hist = []
                    pairctr = [0]
                    u2tp_last = [None]

                    def tile_head(t):
                        pf = sB["u2"].n % 2
                        tslc = (slice(None), slice(t, t + 1))
                        P.op("scalar", lambda e, t=t, tslc=tslc:
                             e.activation(
                                 bjnk[:, :], xs[:, t, :], AF.Square,
                                 accum_out=cols["s2"][tslc]),
                             waits=[(sB["stat"], sB["stat"].n)],
                             inc=sB["stat"])
                        def rz(fn, waits=(), inc=None):
                            ws = list(waits) + [(sB["zrow"], sB["zrow"].n)]
                            ics = [sB["zrow"]] + ([] if inc is None else [inc])
                            P.op("vector", fn, waits=ws, inc=ics)
                        rz(lambda e, t=t, tslc=tslc: e.reduce_sum(
                            cols["s1"][tslc], xs[:, t, :], AX.X))
                        rz(lambda e, tslc=tslc: e.tensor_scalar(
                            cols["m"][tslc], cols["s1"][tslc], 1.0 / H, None,
                            ALU.mult))
                        rz(lambda e, tslc=tslc: e.tensor_scalar(
                            cols["ve"][tslc], cols["s2"][tslc], 1.0 / H, None,
                            ALU.mult), waits=[(sB["stat"], sB["stat"].n)])
                        rz(lambda e, tslc=tslc: e.tensor_tensor(
                            cols["t0"][tslc], cols["m"][tslc], cols["m"][tslc],
                            ALU.mult))
                        rz(lambda e, tslc=tslc: e.tensor_scalar(
                            cols["t0"][tslc], cols["t0"][tslc], -1.0, 1e-5,
                            ALU.mult, ALU.add))
                        rz(lambda e, tslc=tslc: e.tensor_tensor(
                            cols["ve"][tslc], cols["ve"][tslc], cols["t0"][tslc],
                            ALU.add))
                        rz(lambda e, tslc=tslc: e.reciprocal(
                            cols["rq"][tslc], cols["ve"][tslc]))
                        v_zr = sB["zrow"].n
                        P.op("scalar", lambda e, tslc=tslc: e.activation(
                            cols["r"][tslc], cols["rq"][tslc], AF.Sqrt),
                            waits=[(sB["zrow"], v_zr)], inc=sB["zsqrt"])
                        v_zs = sB["zsqrt"].n
                        rz(lambda e, tslc=tslc: e.tensor_tensor(
                            cols["t0"][tslc], cols["m"][tslc], cols["r"][tslc],
                            ALU.mult), waits=[(sB["zsqrt"], v_zs)])
                        rz(lambda e, tslc=tslc: e.tensor_scalar(
                            cols["nb"][tslc], cols["t0"][tslc], -1.0, None,
                            ALU.mult))
                        v_zr = sB["zrow"].n
                        P.op("scalar", lambda e, t=t: e.activation(
                            u2tm[:, 0, :], xs[:, t, :], AF.Identity,
                            bias=cols["nb"][tslc], scale=cols["r"][tslc]),
                            waits=[(sB["zrow"], v_zr), u2tp_last[0]],
                            inc=sB["u2"])
                        v_u2 = sB["u2"].n
                        P.op("sync", lambda e, pf=pf: e.dma_start_transpose(
                            u2T[:, pf, :, :],
                            u2tm[:, 0, :].rearrange("p (c f) -> p c f",
                                                    f=128)),
                            waits=[(sB["u2"], v_u2),
                                   (sB["zmm"], zmm_hist[-1])
                                   if len(zmm_hist) >= 1 else None],
                            inc=u2Tp[pf], dma=True)
                        u2tp_last[0] = (u2Tp[pf], u2Tp[pf].n)
                        head_tp[t] = (u2Tp[pf].n, pf)

                    def tile_zproj(t):
                        v_u2T, pf = head_tp[t]
                        for cc in range(4):
                            sidx = sB["zmm"].n % 2
                            zw = pacq(sidx)
                            for icc in range(8):
                                wts = []
                                if icc == 0:
                                    wts = [(u2Tp[pf], v_u2T),
                                           (sB["cload"], v_cload), zw,
                                           win_cc[cc]]
                                P.op("tensor", lambda e, icc=icc, cc=cc,
                                     sidx=sidx, pf=pf:
                                     e.matmul(ppool[:, sidx, :],
                                              u2T[:, pf, icc, :],
                                              winT[:, icc,
                                                   cc * 512:(cc + 1) * 512],
                                              start=(icc == 0), stop=False),
                                     waits=wts)
                            P.op("tensor", lambda e, cc=cc, sidx=sidx:
                                 e.matmul(ppool[:, sidx, :], ones1[0:1, :],
                                          binb[:, cc * 512:(cc + 1) * 512],
                                          start=False, stop=True),
                                 waits=[(smem, v_mem)], inc=sB["zmm"])
                            v_zmm = sB["zmm"].n
                            slot = t % ZR
                            if t < 2 and not DBG_NOMASK:
                                P.op("vector", lambda e, slot=slot, cc=cc,
                                     sidx=sidx: e.tensor_scalar(
                                         zf[:, slot, cc * 512:(cc + 1) * 512],
                                         ppool[:, sidx, :], zmk[:, :], None,
                                         ALU.mult),
                                     waits=[(sB["zmm"], v_zmm),
                                            (szmk, 16)],
                                     inc=[psl[sidx], sB["zev"]])
                            else:
                                P.op("scalar", lambda e, slot=slot, cc=cc,
                                     sidx=sidx: e.activation(
                                         zf[:, slot, cc * 512:(cc + 1) * 512],
                                         ppool[:, sidx, :], AF.Identity),
                                     waits=[(sB["zmm"], v_zmm)],
                                     inc=[psl[sidx], sB["zev"]])
                        zmm_hist.append(sB["zmm"].n)

                    def emit_block_gen(b):
                        bd = (s_bdrain, 2) if b == 0 else None
                        v_zev_need = 4 * (2 * b + 4)
                        pair_specs = [
                            ((0, 2), [(0, 0, 0, 1, 2, ALU.subtract),
                                      (2, 1, 4, 0, 2, ALU.add)]),
                            ((1, 3), [(1, 0, 1, 1, 3, ALU.subtract),
                                      (3, 1, 1, 0, 3, ALU.add)]),
                        ]
                        pdata = []
                        for pk, ((poA, poB), steps) in enumerate(pair_specs):
                            zbase = (pairctr[0] % 2) * 2
                            pairctr[0] += 1
                            wzh = (sB["pch"], pch_hist[-2 + pk]) \
                                if len(pch_hist) >= 2 else None
                            for zi, po in enumerate((poA, poB)):
                                for fc in range(4):
                                    sidx = 2 + sB["fwd"].n % 2
                                    fw = pacq(sidx)
                                    for icc in range(4):
                                        wts = []
                                        if icc == 0:
                                            wts = [(sB["zev"], v_zev_need),
                                                   (sB["cload"], v_cload), fw]
                                        P.op("tensor", lambda e, icc=icc,
                                             po=po, fc=fc, b=b, sidx=sidx:
                                             e.matmul(
                                                 ppool[:, sidx, :],
                                                 Csb[:, icc, po, :],
                                                 zf[:, (2 * b + icc) % ZR,
                                                    fc * 512:(fc + 1) * 512],
                                                 start=(icc == 0),
                                                 stop=(icc == 3)),
                                             waits=wts,
                                             inc=(sB["fwd"] if icc == 3
                                                  else None))
                                    v_fwd = sB["fwd"].n
                                    P.op("scalar", lambda e, zi=zi, fc=fc,
                                         sidx=sidx, zbase=zbase:
                                         e.activation(
                                             zh[:, zbase + zi,
                                                fc * 512:(fc + 1) * 512],
                                             ppool[:, sidx, :], AF.Identity),
                                         waits=[(sB["fwd"], v_fwd), bd,
                                                wzh if fc == 0 and zi == 0
                                                else None],
                                         inc=[psl[sidx], sB["fev"]])
                            pdata.append((zbase, steps, sB["fev"].n))
                        yield
                        v_idft_prev = idft_done_after_block[-1] \
                            if idft_done_after_block else 0
                        for zbase, steps, v_fev in pdata:
                            for (pidx, za, ta, zb2, tb2, opf) in steps:
                                P.op("vector", lambda e, pidx=pidx, za=za,
                                     ta=ta, zbase=zbase: e.tensor_tensor(
                                         Pb[:, pidx, :],
                                         zh[:, zbase + za, :],
                                         tabs[:, ta, :], ALU.mult),
                                     waits=[(sB["fev"], v_fev), bd,
                                            (sB["tabev"], v_tabev),
                                            (sB["idft"], v_idft_prev),
                                            (s_pm, s_pm.n)], inc=s_pm)
                                P.op("vector", lambda e, zb2=zb2, tb2=tb2,
                                     zbase=zbase:
                                     e.tensor_tensor(ptmp[:, :],
                                                     zh[:, zbase + zb2, :],
                                                     tabs[:, tb2, :],
                                                     ALU.mult),
                                     waits=[(s_pm, s_pm.n)], inc=s_pm)
                                P.op("vector", lambda e, pidx=pidx, opf=opf:
                                     e.tensor_tensor(Pb[:, pidx, :],
                                                     Pb[:, pidx, :],
                                                     ptmp[:, :], opf),
                                     waits=[(s_pm, s_pm.n)],
                                     inc=[s_pm, sB["pch"]])
                            pch_hist.append(sB["pch"].n)
                        v_pch = sB["pch"].n
                        yield

                        for po in range(2):
                            tgt = 2 + 2 * b + po
                            for pc in range(2):
                                wA, wB2 = pacq(4), pacq(5)
                                for half, (sidx, fc) in enumerate(
                                        ((4, pc), (5, pc + 2))):
                                    for icc in range(4):
                                        wts = []
                                        if icc == 0:
                                            wts = [(sB["pch"], v_pch),
                                                   wA if half == 0 else wB2]
                                        P.op("tensor", lambda e, icc=icc,
                                             po=po, fc=fc, sidx=sidx:
                                             e.matmul(
                                                 ppool[:, sidx, :],
                                                 iDsb[:, icc, po, :],
                                                 Pb[:, icc,
                                                    fc * 512:(fc + 1) * 512],
                                                 start=(icc == 0), stop=False),
                                             waits=wts)
                                    P.op("tensor", lambda e, fc=fc, sidx=sidx:
                                         e.matmul(ppool[:, sidx, :],
                                                  ones1[32:33, :],
                                                  bab[:, fc * 512:
                                                      (fc + 1) * 512],
                                                  start=False, stop=True),
                                         inc=sB["idft"])
                                v_idft = sB["idft"].n
                                P.op("scalar", lambda e: e.activation(
                                    sigb[:, 0, :], ppool[:, 4, :], AF.Sigmoid),
                                    waits=[(sB["idft"], v_idft), bd],
                                    inc=sB["sig"])
                                P.op("scalar", lambda e: e.activation(
                                    sigb[:, 1, :], ppool[:, 5, :], AF.Sigmoid),
                                    inc=sB["sig"])
                                v_sig = sB["sig"].n
                                P.op("vector", lambda e, pc=pc: e.tensor_tensor(
                                    gat[:, pc * 512:(pc + 1) * 512],
                                    ppool[:, 4, :], sigb[:, 1, :], ALU.mult),
                                    waits=[(sB["sig"], v_sig), bd,
                                           (sB["catev"], sB["catev"].n)
                                           if pc == 0 else None])
                                P.op("vector", lambda e, pc=pc: e.tensor_tensor(
                                    gat[:, (pc + 2) * 512:(pc + 3) * 512],
                                    ppool[:, 5, :], sigb[:, 0, :], ALU.mult),
                                    inc=[sB["gat"], psl[4], psl[5]])
                            v_gat = sB["gat"].n
                            P.op("sync", lambda e: e.dma_start_transpose(
                                catT[:, :, :],
                                gat[:, :].rearrange("p (c f) -> p c f",
                                                    f=128)),
                                waits=[(sB["gat"], v_gat), bd,
                                       (sB["omm"], sB["omm"].n)],
                                inc=sB["catev"], dma=True)
                            v_catev = sB["catev"].n
                            yield
                            for hc in range(2):
                                ow = pacq(6)
                                for ec in range(16):
                                    wts = []
                                    if ec == 0:
                                        wts = [(sB["catev"], v_catev),
                                               ow] + wout_waits
                                    P.op("tensor", lambda e, ec=ec, hc=hc:
                                         e.matmul(
                                             ppool[:, 6, :],
                                             catT[:, ec, :],
                                             woutT[:, ec,
                                                   hc * 512:(hc + 1) * 512],
                                             start=(ec == 0), stop=False),
                                         waits=wts)
                                P.op("tensor", lambda e, hc=hc: e.matmul(
                                    ppool[:, 6, :], ones1[64:65, :],
                                    boutb[:, hc * 512:(hc + 1) * 512],
                                    start=False, stop=True),
                                    inc=sB["omm"])
                                v_omm = sB["omm"].n
                                P.op("vector", lambda e, tgt=tgt, hc=hc:
                                     e.tensor_tensor(
                                         xs[:, tgt, hc * 512:(hc + 1) * 512],
                                         ppool[:, 6, :],
                                         xs[:, tgt, hc * 512:(hc + 1) * 512],
                                         ALU.add),
                                     waits=[(sB["omm"], v_omm)],
                                     inc=[psl[6], sB["oev"]])
                        idft_done_after_block.append(sB["idft"].n)

                    def do_zproj(t):
                        if t + 1 < NT and (t + 1) not in head_tp:
                            tile_head(t + 1)
                        tile_zproj(t)

                    tile_head(0)
                    for t in range(4):
                        do_zproj(t)
                    NB = (NT - 2) // 2
                    gens = {}
                    for b in range(NB):
                        gens[b] = emit_block_gen(b)
                        next(gens[b])
                        if b >= 1:
                            next(gens[b - 1])
                        if 2 * b + 4 < NT:
                            do_zproj(2 * b + 4)
                        if b >= 1:
                            next(gens[b - 1])
                        if 2 * b + 5 < NT:
                            do_zproj(2 * b + 5)
                        if b >= 1:
                            for _ in gens.pop(b - 1):
                                pass
                        next(gens[b])
                    next(gens[NB - 1])
                    for _ in gens.pop(NB - 1):
                        pass
                    if debug_taps:
                        barrier()
                        dz = taps_out["dbg_z"].rearrange("(t p) c -> t p c",
                                                         p=128)
                        sdbg2 = S("sdbg2")
                        for t in range(NT - ZR, NT):
                            P.op("gpsimd", lambda e, t=t: e.dma_start(
                                dz[t], zf[:, t % ZR, :]), inc=sdbg2, dma=True)
                        P.op("gpsimd", lambda e: e.nop(),
                             waits=[(sdbg2, 16 * ZR)])
                        dx3 = taps_out["dbg_x3"].rearrange(
                            "(t p) h -> t p h", p=128)
                        sdbg3 = S("sdbg3")
                        for t in range(NV):
                            P.op("gpsimd", lambda e, t=t: e.dma_start(
                                dx3[t], xs[:, t + 2, :]), inc=sdbg3, dma=True)
                        P.op("gpsimd", lambda e: e.nop(),
                             waits=[(sdbg3, 16 * NV)])
            barrier()

        with ExitStack() as cctx:
            bufs2 = dict(
                uT=sb(cctx, "uT2", [128, GA, 8, 128], BF16),
                u_f=sb(cctx, "u_f2", [128, 1, H], F32),
                u_tm=sb(cctx, "u_tm2", [128, GA, H], BF16),
                vbuf=sb(cctx, "vbuf2", [128, GA, INNER], BF16),
                t2=sb(cctx, "t22", [128, 2, 512], F32),
                vq=sb(cctx, "vq2", [128, 1, INNER], BF16),
                wgup=sb(cctx, "wgup2", [128, 2, 2, 8, 512], BF16),
                wdb=sb(cctx, "wdb2", [128, 2, 8, 512], BF16),
                silu=sb(cctx, "silu2", [128, 2, 512], F32),
                outf=sb(cctx, "outf", [128, 3, H], F32),
            )
            bufs2["vqT"] = nc.alloc_sbuf_tensor_at(
                "vqT2", [128, GA, 32, 128], BF16,
                offset=nc.lookup_mloc(bufs2["vbuf"]).addr)
            ffn_app(2, list(range(2, NT)), lambda t: [], bufs2)
            P.op("sync", lambda e: e.nop(),
                 waits=[(s, s.n) for s in s_outp])

        with nc.Block() as block:
            @block.sync
            def _(eng):
                P.emit("sync", eng)

            @block.gpsimd
            def _(eng):
                P.emit("gpsimd", eng)

            @block.tensor
            def _(eng):
                P.emit("tensor", eng)

            @block.vector
            def _(eng):
                P.emit("vector", eng)

            @block.scalar
            def _(eng):
                P.emit("scalar", eng)

    return nc


def kernel(L0_attn_x, L0_attn_w_in, L0_attn_b_in, L0_attn_w_a, L0_attn_b_a,
           L0_attn_w_b, L0_attn_b_b, L0_attn_w_out, L0_attn_b_out,
           L0_ffn_w_gate, L0_ffn_w_up, L0_ffn_w_down, _debug_taps=False,
           _cores=None):
    x = np.asarray(L0_attn_x, np.float32)
    B, T, Hd = x.shape
    xf = np.ascontiguousarray(x.reshape(B * T, Hd))

    Cd, iDd, D2d = _dft_consts()
    shared = dict(
        wgT=np.ascontiguousarray(np.asarray(L0_ffn_w_gate, np.float32).T),
        wuT=np.ascontiguousarray(np.asarray(L0_ffn_w_up, np.float32).T),
        wdT=np.ascontiguousarray(np.asarray(L0_ffn_w_down, np.float32).T),
        w_in=np.ascontiguousarray(np.asarray(L0_attn_w_in, np.float32)),
        w_out=np.ascontiguousarray(np.asarray(L0_attn_w_out, np.float32)),
        wa=np.ascontiguousarray(np.asarray(L0_attn_w_a, np.float32)[:, 0, :]),
        wb=np.ascontiguousarray(np.asarray(L0_attn_w_b, np.float32)[:, 0, :]),
        b_in=np.ascontiguousarray(np.asarray(L0_attn_b_in, np.float32)[None, :]),
        b_a=np.ascontiguousarray(np.asarray(L0_attn_b_a, np.float32)[None, :]),
        b_b=np.ascontiguousarray(np.asarray(L0_attn_b_b, np.float32)[None, :]),
        b_out=np.ascontiguousarray(np.asarray(L0_attn_b_out,
                                              np.float32)[None, :]),
        Cd=Cd, iDd=iDd, D2d=D2d,
        identd=np.eye(128, dtype=BFnp),
    )

    wgT_s = shared["wgT"].reshape(16, 128, 2048)
    wuT_s = shared["wuT"].reshape(16, 128, 2048)
    wdT_s = shared["wdT"].reshape(16, 128, 2048)

    nc = build_program(debug_taps=_debug_taps)
    cores = list(range(8)) if _cores is None else _cores
    in_maps = []
    for core in cores:
        tok0 = core * TPC
        seq_start = (core % 4 == 0)
        halo = (np.zeros((256, Hd), np.float32) if seq_start
                else xf[tok0 - 256: tok0])
        xhv = np.ascontiguousarray(
            np.concatenate([halo, xf[tok0:tok0 + TPC]], 0))
        zmv = np.full((128, 1), 0.0 if seq_start else 1.0, np.float32)
        wsclv = np.ascontiguousarray(np.stack(
            [wgT_s[2 * core:2 * core + 2],
             wuT_s[2 * core:2 * core + 2],
             wdT_s[2 * core:2 * core + 2]]))
        in_maps.append(dict(xh=xhv, zmask=zmv, wscl=wsclv, **shared))

    res = run_bass_kernel_spmd(nc, in_maps, core_ids=list(range(len(cores))))
    if _debug_taps:
        return res
    outs = [np.asarray(r["out"]) for r in res.results]
    return np.concatenate(outs, axis=0).reshape(B, T, Hd)


# version 35
# speedup vs baseline: 1.1274x; 1.0083x over previous
import numpy as np
import ml_dtypes
import concourse.bass as bass
import concourse.mybir as mybir
from concourse.bass_utils import run_bass_kernel_spmd

F32 = mybir.dt.float32
BF16 = mybir.dt.bfloat16
AF = mybir.ActivationFunctionType
ALU = mybir.AluOpType
AX = mybir.AxisListType
BFnp = ml_dtypes.bfloat16

MAGIC = float(np.float32(1.5 * 2**23))
DBG_QUANT_ALL = False
DBG_TRUNC = None
DBG_RACE = False
DBG_NOMASK = False
H = 1024
INNER = 4096
CH = 2048
NT = 18
NV = 16
TPC = 2048
NBLK = 8
GA = 6
ZR = 4


class Sem:
    def __init__(self, nc, name):
        self.h = nc.alloc_semaphore(name)
        self.n = 0


class Prog:
    ENGS = ("sync", "gpsimd", "tensor", "vector", "scalar")

    def __init__(self, nc):
        self.nc = nc
        self.ops = {e: [] for e in self.ENGS}
        self.sems = {}

    def sem(self, name):
        if name not in self.sems:
            self.sems[name] = Sem(self.nc, name)
        return self.sems[name]

    def op(self, eng, fn, waits=(), inc=None, dma=False):
        amt = 16 if dma else 1
        incs = []
        self.gidx = getattr(self, 'gidx', 0) + 1
        ws = [(s, v) for w in waits if w is not None for (s, v) in [w] if v > 0]
        for s, v in ws:
            assert v <= s.n, (f"wait on {s.h.name} for {v} but only {s.n} "
                              f"produced so far (eng {eng})")
        if inc is not None:
            for it in (inc if isinstance(inc, list) else [inc]):
                if isinstance(it, tuple):
                    s, a = it
                else:
                    s, a = it, amt
                s.n += a
                incs.append((s, a, s.n))
        self.ops[eng].append((fn, ws, incs, self.gidx))
        return incs[0][2] if incs else None

    def carrier(self, eng_name, eng):
        if eng_name in ("vector", "scalar", "gpsimd"):
            ent = self.carriers[eng_name]
            buf, k, cap = ent
            assert k < cap, f"carrier slots exhausted for {eng_name}"
            ent[1] = k + 1
            p, c = (k % 4) * 32, k // 4
            if eng_name == "scalar":
                return eng.memzero(buf[p:p + 1, c:c + 1])
            return eng.memset(buf[p:p + 1, c:c + 1], 0.0)
        return eng.nop()

    def emit(self, eng_name, eng):
        ops = self.ops[eng_name]
        if DBG_TRUNC is not None:
            if isinstance(DBG_TRUNC, float):
                ops = ops[:max(1, int(len(ops) * DBG_TRUNC))]
            else:
                ops = [o for o in ops if o[3] <= DBG_TRUNC]
        for fn, ws, incs, _gi in ops:
            for s, v in ws:
                eng.wait_ge(s.h, v)
            ins = fn(eng)
            if incs:
                ins.then_inc(incs[0][0].h, incs[0][1])
                for s, a, _ in incs[1:]:
                    self.carrier(eng_name, eng).then_inc(s.h, a)


def _dft_consts():
    N, K, OUT = 512, 256, 256
    t = np.arange(N)
    C = np.zeros((N, N), np.float64)
    for fr in range(N):
        if fr <= 256:
            C[:, fr] = np.cos(2 * np.pi * fr * t / N)
        else:
            C[:, fr] = np.sin(2 * np.pi * (fr - 256) * t / N)
    tau = np.arange(OUT) + (N - OUT)
    iD = np.zeros((N, OUT), np.float64)
    for fr in range(N):
        if fr <= 256:
            c = 1.0 if fr in (0, 256) else 2.0
            iD[fr] = c / N * np.cos(2 * np.pi * fr * tau / N)
        else:
            iD[fr] = 2.0 / N * np.sin(2 * np.pi * (fr - 256) * tau / N)
    j = np.arange(K)
    jr = (K - 1 - j)[:, None]
    Wc = np.cos(2 * np.pi * np.arange(257)[None, :] * jr / N)
    Ws = np.sin(2 * np.pi * np.arange(256)[None, :] * jr / N)
    U0 = Wc[:, 0:128].copy()
    U1 = Wc[:, 128:256].copy()
    V0 = np.zeros((K, 128)); V0[:, 1:] = Ws[:, 1:128]
    V1 = Ws[:, 128:256].copy()
    Nt = np.zeros((K, 128)); Nt[:, 0] = Wc[:, 256]; Nt[:, 1:] = Wc[:, 1:128]
    D2 = np.stack([U0, U1, V0, V1, Nt], axis=1)
    Cd = np.ascontiguousarray(
        C.reshape(4, 128, 4, 128).transpose(1, 0, 2, 3)).astype(BFnp)
    iDd = np.ascontiguousarray(
        iD.reshape(4, 128, 2, 128).transpose(1, 0, 2, 3)).astype(BFnp)
    D2d = np.ascontiguousarray(
        D2.reshape(2, 128, 5, 128).transpose(1, 0, 2, 3)).astype(BFnp)
    return Cd, iDd, D2d


def build_program(debug_taps=False):
    nc = bass.Bass("TRN2", target_bir_lowering=False,
                   dynamic_dma_scratch_size=16384,
                   detect_race_conditions=DBG_RACE)

    def din(name, shp, dt=F32):
        return nc.dram_tensor(name, list(shp), dt, kind="ExternalInput").ap()

    xh = din("xh", [NT * 128, H])
    zmask = din("zmask", [128, 1])
    wgT = din("wgT", [H, INNER])
    wuT = din("wuT", [H, INNER])
    wdT = din("wdT", [INNER, H])
    w_in = din("w_in", [CH, H])
    w_out = din("w_out", [H, CH])
    wa = din("wa", [H, 256])
    wb = din("wb", [H, 256])
    b_in = din("b_in", [1, CH])
    b_a = din("b_a", [1, H])
    b_b = din("b_b", [1, H])
    b_out = din("b_out", [1, H])
    wscl = din("wscl", [3, 2, 128, 2048])
    Cd = din("Cd", [128, 4, 4, 128], BF16)
    iDd = din("iDd", [128, 4, 2, 128], BF16)
    D2d = din("D2d", [128, 2, 5, 128], BF16)
    identd = din("identd", [128, 128], BF16)

    out = nc.dram_tensor("out", [TPC, H], F32, kind="ExternalOutput").ap()
    taps_out = {}
    if debug_taps:
        for nm, shp in (("dbg_xmid", [NT * 128, H]), ("dbg_z", [NT * 128, CH]),
                        ("dbg_x3", [NV * 128, H]),):
            taps_out[nm] = nc.dram_tensor(nm, shp, F32,
                                          kind="ExternalOutput").ap()

    scr_g = nc.dram_tensor("scr_g", [8, 128, 8, 512], BF16, kind="Internal").ap()
    scr_u = nc.dram_tensor("scr_u", [8, 128, 8, 512], BF16, kind="Internal").ap()
    scr_d = nc.dram_tensor("scr_d", [2, 4, 128, 8, 512], BF16,
                           kind="Internal").ap()
    cc_in = nc.dram_tensor("cc_in", [128, 3], F32, kind="Internal").ap()
    cc_out = nc.dram_tensor("cc_out", [128, 3], F32, kind="Internal",
                            addr_space="Shared").ap()

    P = Prog(nc)
    S = P.sem

    from contextlib import ExitStack
    with ExitStack() as top:
        def sb(ctx, name, shp, dt):
            return ctx.enter_context(nc.sbuf_tensor(name, shp, dt))

        xs = sb(top, "xs", [128, NT, H], BF16)
        dumv = sb(top, "dumv", [128, 256], F32)
        dums = sb(top, "dums", [128, 128], F32)
        dumg = sb(top, "dumg", [128, 128], F32)
        P.carriers = {"vector": [dumv, 0, 1024], "scalar": [dums, 0, 512],
                      "gpsimd": [dumg, 0, 512]}
        ident = sb(top, "ident", [128, 128], BF16)
        ones1 = sb(top, "ones1", [128, 128], BF16)
        ones128f = sb(top, "ones128f", [128, 128], F32)
        magicrow = sb(top, "magicrow", [128, 1], F32)
        hp = sb(top, "hp", [128, 1], F32)
        hn = sb(top, "hn", [128, 1], F32)
        zmk = sb(top, "zmk", [128, 1], F32)
        wsum = sb(top, "wsum", [128, 3], F32)
        acol = sb(top, "acol", [128, 3, 1], F32)
        nwcol = sb(top, "nwcol", [128, 3], F32)
        swv = sb(top, "swv", [128, 3], F32)
        gww = sb(top, "gww", [128, 3], F32)
        cols = {}
        for nm in ("s1", "s2", "mx", "mn", "m", "ve", "rq", "r", "mab", "sp",
                   "nb", "q", "cg", "cu", "mt", "f", "cd", "t0", "t1"):
            cols[nm] = sb(top, "col_" + nm, [128, NT], F32)
        vmax = sb(top, "vmax", [128, GA, 8], F32)
        ppool = top.enter_context(nc.psum_tensor("ppool", [128, 7, 512], F32))
        ptr = top.enter_context(nc.psum_tensor("ptr", [128, 8, 128], BF16))

        psl = [S(f"psl{i}") for i in range(7)]
        puse = [0] * 7
        ptrsl = S("pslT")
        ptruse = [0]

        def pacq(i):
            w = (psl[i], puse[i]) if puse[i] else None
            puse[i] += 1
            return w

        def ptracq():
            w = (ptrsl, ptruse[0]) if ptruse[0] else None
            ptruse[0] += 1
            return w

        sx = S("sx")
        sinit = S("sinit")
        smem = S("smem")
        szmk = S("szmk")
        sident = S("sident")
        P.op("gpsimd", lambda g: g.memset(ones1[:, :], 1.0))
        P.op("gpsimd", lambda g: g.memset(ones128f[:, :], 1.0))
        P.op("gpsimd", lambda g: g.memset(magicrow[:, :], MAGIC))
        P.op("gpsimd", lambda g: g.memset(hp[:, :], 0.5))
        P.op("gpsimd", lambda g: g.memset(hn[:, :], -0.5), inc=smem)
        v_mem = smem.n
        P.op("gpsimd", lambda g: g.dma_start(zmk[:, :], zmask), inc=szmk, dma=True)
        P.op("gpsimd", lambda g: g.dma_start(ident[:, :], identd), inc=sident,
             dma=True)
        v_init = sident.n
        xh_t = xh.rearrange("(t p) h -> t p h", p=128)
        for t in range(NT):
            P.op("gpsimd", lambda g, t=t: g.dma_start(xs[:, t, :], xh_t[t]),
                 inc=sx, dma=True)

        with ExitStack() as actx:
            wfp = sb(actx, "wfp", [128, 2, 4, 512], F32)
            wsg = sb(actx, "wsg", [128, 2, 4, 512], BF16)
            bufs = dict(
                uT=sb(actx, "uT", [128, GA, 8, 128], BF16),
                u_f=sb(actx, "u_f", [128, 1, H], F32),
                u_tm=sb(actx, "u_tm", [128, GA, H], BF16),
                vbuf=sb(actx, "vbuf", [128, GA, INNER], BF16),
                t2=sb(actx, "t2", [128, 2, 512], F32),
                vq=sb(actx, "vq", [128, 1, INNER], BF16),
                wgup=sb(actx, "wgup", [128, 2, 2, 8, 512], BF16),
                wdb=sb(actx, "wdb", [128, 2, 8, 512], BF16),
                silu=sb(actx, "silu", [128, 2, 512], F32),
                outf=None,
            )
            bufs["vqT"] = nc.alloc_sbuf_tensor_at(
                "vqT", [128, GA, 32, 128], BF16,
                offset=nc.lookup_mloc(bufs["vbuf"]).addr)

            swfp = [S("swf0"), S("swf1")]
            sabs = S("sabs")
            sacc = S("sacc")
            sscale = S("sscale")
            s_ccl = S("s_ccl")
            s_ccd = S("s_ccd")
            kglob = 0
            for wi in range(3):
                for k in range(2):
                    par = kglob % 2
                    w = (sabs, kglob - 1) if kglob >= 2 else None
                    P.op("sync", lambda e, wi=wi, k=k, par=par: e.dma_start(
                        wfp[:, par, :, :].rearrange("p c o -> p (c o)"),
                        wscl[wi, k]),
                        waits=[w], inc=swfp[par], dma=True)
                    kglob += 1
                    P.op("scalar", lambda e, wi=wi, par=par: e.activation(
                        wsg[:, 0, :, :], wfp[:, par, :, :], AF.Abs,
                        accum_out=acol[:, wi, :]),
                        waits=[(swfp[par], swfp[par].n), (sabs, sabs.n),
                               (sacc, sacc.n)],
                        inc=sabs)
                    if k == 0:
                        P.op("vector", lambda e, wi=wi: e.tensor_copy(
                            wsum[:, wi:wi + 1], acol[:, wi, :]),
                            waits=[(sabs, kglob), (sacc, sacc.n)], inc=sacc)
                    else:
                        P.op("vector", lambda e, wi=wi: e.tensor_add(
                            wsum[:, wi:wi + 1], wsum[:, wi:wi + 1],
                            acol[:, wi, :]),
                            waits=[(sabs, kglob), (sacc, sacc.n)], inc=sacc)
            NSC = kglob
            P.op("gpsimd", lambda e: e.dma_start(cc_in, wsum[:, :]),
                 waits=[(sacc, NSC)], inc=s_ccl, dma=True)
            P.op("gpsimd", lambda e: e.collective_compute(
                "AllReduce", ALU.add,
                replica_groups=[[0, 1, 2, 3, 4, 5, 6, 7]],
                ins=[cc_in], outs=[cc_out]),
                waits=[(s_ccl, 16)], inc=s_ccd)
            P.op("gpsimd", lambda e: e.dma_start(wsum[:, :], cc_out),
                 waits=[(s_ccd, 1)], inc=s_ccl, dma=True)
            sred = S("sred")
            wsl = pacq(6)
            P.op("tensor", lambda e: e.matmul(ppool[:, 6, 0:3], ones128f[:, :],
                                              wsum[:, :], start=True, stop=True),
                 waits=[(s_ccl, 32), wsl, (smem, v_mem)], inc=sred)
            P.op("vector", lambda e: e.tensor_copy(nwcol[:, :], ppool[:, 6, 0:3]),
                 waits=[(sred, 1)], inc=[psl[6], sscale])
            P.op("vector", lambda e: e.tensor_scalar(
                nwcol[:, :], nwcol[:, :], 1.0 / (H * INNER), 1e-5,
                ALU.mult, ALU.max), waits=[(sscale, sscale.n)], inc=sscale)
            P.op("vector", lambda e: e.reciprocal(swv[:, :], nwcol[:, :]),
                 waits=[(sscale, sscale.n)], inc=sscale)
            P.op("vector", lambda e: e.tensor_scalar(
                gww[:, 0:2], nwcol[:, 0:2], 1.0 / 254.0, None, ALU.mult),
                waits=[(sscale, sscale.n)], inc=sscale)
            P.op("vector", lambda e: e.tensor_scalar(
                gww[:, 2:3], nwcol[:, 2:3], 0.25 / 127.0, None, ALU.mult),
                waits=[(sscale, sscale.n)], inc=sscale)
            v_scale = sscale.n

            sqfp = [S("sqf0"), S("sqf1")]
            ssgn = S("ssgn")
            sqdve = S("sqdve")
            sscrw = S("sscrw")
            qk = [0]

            sq_hist = []

            def q_load(src_view):
                par = qk[0] % 2
                w = (ssgn, 2 * (qk[0] - 1)) if qk[0] >= 2 else (sabs, 6)
                P.op("sync", lambda e, par=par: e.dma_start(
                    wfp[:, par, :, :], src_view),
                    waits=[w], inc=sqfp[par], dma=True)
                qk[0] += 1
                return par, sqfp[par].n

            def q_compute(ld, dst_sb, dst_scr, wi, extra_w=None):
                par, vqf = ld
                wsg_w = (sqdve, sq_hist[-2]) if len(sq_hist) >= 2 else None
                P.op("scalar", lambda e, par=par, wi=wi: e.activation(
                    wsg[:, par, :, :], wfp[:, par, :, :], AF.Sign,
                    bias=hn[:, :], scale=swv[:, wi:wi + 1]),
                    waits=[(sqfp[par], vqf), (sscale, v_scale), (smem, v_mem),
                           wsg_w, (sabs, 6)],
                    inc=ssgn)
                P.op("scalar", lambda e, par=par, wi=wi, dst_sb=dst_sb:
                     e.activation(dst_sb, wfp[:, par, :, :], AF.Sign,
                                  bias=hp[:, :], scale=swv[:, wi:wi + 1]),
                     waits=[extra_w, (sscrw, sscrw.n)], inc=ssgn)
                vs = ssgn.n
                P.op("vector", lambda e, dst_sb=dst_sb, par=par:
                     e.tensor_add(dst_sb, dst_sb, wsg[:, par, :, :]),
                     waits=[(ssgn, vs)], inc=sqdve)
                vq_ = sqdve.n
                sq_hist.append(vq_)
                P.op("sync", lambda e, dst_scr=dst_scr, dst_sb=dst_sb:
                     e.dma_start(dst_scr, dst_sb),
                     waits=[(sqdve, vq_)], inc=sscrw, dma=True)
                return vq_

            def quant_chunk(src_view, dst_sb, dst_scr, wi, extra_w=None):
                return q_compute(q_load(src_view), dst_sb, dst_scr, wi,
                                 extra_w)

            s_sq = S("s_sq")
            s_sqs = S("s_sqs")
            s_row = S("s_row")
            s_sqrt = S("s_sqrt")
            s_ufw = S("s_ufw")
            s_utm = S("s_utm")
            s_utp = S("s_utp")
            s_uT = S("s_uT")
            s_wqlp = [S("s_wql0"), S("s_wql1")]
            s_wulp = [S("s_wul0"), S("s_wul1")]
            s_wdhp = [S("s_wdh0"), S("s_wdh1")]
            s_gup = S("s_gup")
            s_silu = S("s_silu")
            s_vtt = S("s_vtt")
            s_vw = S("s_vw")
            s_row2 = S("s_row2")
            s_t2 = S("s_t2")
            s_vq8 = S("s_vq8")
            s_vqtp = [S("s_vqt0"), S("s_vqt1")]
            s_wdp = [S("s_wd0"), S("s_wd1")]
            s_dwn = S("s_dwn")
            s_out = S("s_out")

            wgT_q = wgT.rearrange("(a p) (c o) -> p a c o", p=128, o=512)
            wuT_q = wuT.rearrange("(a p) (c o) -> p a c o", p=128, o=512)
            wdT_q = wdT.rearrange("(a p) h -> p a h", p=128)

            s_dmg = S("s_dmg")
            uT_rd = {i: None for i in range(GA)}
            vq_hrd = {0: None, 1: None}
            gup_hist = []
            out_store = {}
            s_outp = [S(f"s_outp{i}") for i in range(6)]
            dmg_hist = []
            out_hist = []

            def ffn_app(app, tiles, xwait, bufs):
                uT, u_f, u_tm = bufs["uT"], bufs["u_f"], bufs["u_tm"]
                vbuf, t2, vq = bufs["vbuf"], bufs["t2"], bufs["vq"]
                wgup, wdb, silub = bufs["wgup"], bufs["wdb"], bufs["silu"]
                vqT, outf = bufs["vqT"], bufs["outf"]
                groups = [tiles[i:i + GA] for i in range(0, len(tiles), GA)]

                def rv(fn, waits=(), inc=None):
                    ws = list(waits) + [(s_row, s_row.n)]
                    ics = [s_row] + ([] if inc is None else
                                     (inc if isinstance(inc, list)
                                      else [inc]))
                    P.op("vector", fn, waits=ws, inc=ics)

                def stats_tile(t):
                    wu = (s_utm, s_utm.n) if s_utm.n else None
                    P.op("scalar", lambda e, t=t: e.activation(
                        u_f[:, 0, :], xs[:, t, :], AF.Square,
                        accum_out=cols["s2"][:, t:t + 1]),
                        waits=list(xwait(t)) + [wu, (s_sqs, s_sqs.n)],
                        inc=s_sqs)
                    yield
                    P.op("scalar", lambda e, t=t: e.activation(
                        u_f[:, 0, :], xs[:, t, :], AF.Identity,
                        accum_out=cols["s1"][:, t:t + 1]),
                        waits=list(xwait(t)) + [(s_sqs, s_sqs.n)],
                        inc=s_sqs)
                    yield
                    P.op("vector", lambda e, t=t: e.reduce_max(
                        cols["mx"][:, t:t + 1], xs[:, t, :], AX.X),
                        waits=list(xwait(t)))
                    yield
                    P.op("vector", lambda e, t=t: e.tensor_reduce(
                        cols["mn"][:, t:t + 1], xs[:, t, :], AX.X,
                        ALU.min), inc=s_sq)
                    yield

                def rowmath_gen(grp, v_sq, v_sqs):
                    G = len(grp)
                    sl = (slice(None), slice(grp[0], grp[0] + G))
                    rv(lambda e, sl=sl: e.tensor_scalar(cols["m"][sl], cols["s1"][sl],
                                                 1.0 / H, None, ALU.mult),
                       waits=[(s_sq, v_sq), (s_sqs, v_sqs)])
                    yield
                    rv(lambda e, sl=sl: e.tensor_scalar(cols["ve"][sl], cols["s2"][sl],
                                                 1.0 / H, None, ALU.mult))
                    yield
                    rv(lambda e, sl=sl: e.tensor_tensor(cols["t0"][sl], cols["m"][sl],
                                                 cols["m"][sl], ALU.mult))
                    yield
                    rv(lambda e, sl=sl: e.tensor_scalar(cols["t0"][sl], cols["t0"][sl],
                                                 -1.0, 1e-5, ALU.mult, ALU.add))
                    yield
                    rv(lambda e, sl=sl: e.tensor_tensor(cols["ve"][sl], cols["ve"][sl],
                                                 cols["t0"][sl], ALU.add))
                    yield
                    rv(lambda e, sl=sl: e.reciprocal(cols["rq"][sl], cols["ve"][sl]))
                    yield
                    v_row = s_row.n
                    P.op("scalar", lambda e, sl=sl: e.activation(
                        cols["r"][sl], cols["rq"][sl], AF.Sqrt),
                        waits=[(s_row, v_row)], inc=s_sqrt)
                    yield
                    v_sqrt = s_sqrt.n
                    rv(lambda e, sl=sl: e.tensor_tensor(cols["t0"][sl], cols["mx"][sl],
                                                 cols["m"][sl], ALU.subtract),
                       waits=[(s_sqrt, v_sqrt)])
                    yield
                    rv(lambda e, sl=sl: e.tensor_tensor(cols["t1"][sl], cols["m"][sl],
                                                 cols["mn"][sl], ALU.subtract))
                    yield
                    rv(lambda e, sl=sl: e.tensor_tensor(cols["mab"][sl], cols["t0"][sl],
                                                 cols["t1"][sl], ALU.max))
                    yield
                    rv(lambda e, sl=sl: e.tensor_scalar(cols["t0"][sl], cols["mab"][sl],
                                                 1e-12, None, ALU.max))
                    yield
                    rv(lambda e, sl=sl: e.reciprocal(cols["t1"][sl], cols["t0"][sl]))
                    yield
                    rv(lambda e, sl=sl: e.tensor_scalar(cols["sp"][sl], cols["t1"][sl],
                                                 127.0, None, ALU.mult))
                    yield
                    rv(lambda e, sl=sl: e.tensor_tensor(cols["t0"][sl], cols["m"][sl],
                                                 cols["sp"][sl], ALU.mult))
                    yield
                    rv(lambda e, sl=sl: e.tensor_scalar(cols["nb"][sl], cols["t0"][sl],
                                                 -1.0, None, ALU.mult))
                    yield
                    rv(lambda e, sl=sl: e.tensor_tensor(cols["q"][sl], cols["r"][sl],
                                                 cols["mab"][sl], ALU.mult))
                    yield
                    rv(lambda e, sl=sl: e.tensor_scalar(cols["cg"][sl], cols["q"][sl],
                                                 gww[:, 0:1], None, ALU.mult))
                    yield
                    rv(lambda e, sl=sl: e.tensor_scalar(cols["cu"][sl], cols["q"][sl],
                                                 gww[:, 1:2], None, ALU.mult))
                    yield

                def head_gen(grp, out):
                    for ti, t in enumerate(grp):
                        yield from stats_tile(t)
                    yield from rowmath_gen(grp, s_sq.n, s_sqs.n)
                    v_row = s_row.n
                    pend = []
                    for ti, t in enumerate(grp):
                        P.op("scalar", lambda e, t=t: e.activation(
                            u_f[:, 0, :], xs[:, t, :], AF.Identity,
                            bias=cols["nb"][:, t:t + 1],
                            scale=cols["sp"][:, t:t + 1]),
                            waits=[(s_row, v_row),
                                   (s_utm, s_utm.n)
                                   if s_utm.n else None],
                            inc=s_ufw)
                        yield
                        v_ufw = s_ufw.n
                        wtr = (s_uT, uT_rd[ti]) if uT_rd[ti] else None
                        P.op("vector", lambda e, ti=ti:
                             e.tensor_scalar(
                                 u_tm[:, ti, :], u_f[:, 0, :], MAGIC,
                                 MAGIC, ALU.add, ALU.subtract),
                             waits=[(s_ufw, v_ufw), wtr], inc=s_utm)
                        pend.append((ti, s_utm.n))
                        yield
                    out["pend"] = pend

                def emit_utp(grp, pend):
                    uT_base = s_uT.n
                    for ti, (slot, v_utm) in enumerate(pend):
                        wprev = (s_gup, gup_hist[-1]) if gup_hist else None
                        wchain = (s_uT, s_uT.n) if s_uT.n else None
                        P.op("sync", lambda e, ti=ti, slot=slot:
                             e.dma_start_transpose(
                                 uT[:, ti, :, :],
                                 u_tm[:, slot, :].rearrange(
                                     "p (c f) -> p c f", f=128)),
                             waits=[(s_utm, v_utm), wprev, wchain],
                             inc=s_uT, dma=True)
                        uT_rd[slot] = s_uT.n
                    return uT_base

                head = {}
                pard_done = {0: None, 1: None}
                for gi, grp in enumerate(groups):
                    G = len(grp)
                    v_dmm_prev = s_dmg.n
                    nhs = {}
                    ngen = None
                    if gi not in head:
                        hstate = {}
                        for _ in head_gen(grp, hstate):
                            pass
                        uT_base = emit_utp(grp, hstate["pend"])
                    else:
                        uT_base = head[gi]
                    if gi + 1 < len(groups) \
                            and not (app == 1 and gi == 0):
                        ngen = head_gen(groups[gi + 1], nhs)

                    vqt_done = {}

                    vtt_tile = {}

                    def emit_vchain(ti, t):
                        tc = (slice(None), slice(t, t + 1))
                        rv(lambda e, ti=ti, tc=tc: e.tensor_reduce(
                            cols["mt"][tc], vmax[:, ti, :], AX.X, ALU.max),
                           waits=[(s_vtt, vtt_tile[ti])])
                        rv(lambda e, tc=tc: e.tensor_scalar(
                            cols["t0"][tc], cols["mt"][tc], 1e-12, None,
                            ALU.max))
                        rv(lambda e, tc=tc: e.reciprocal(
                            cols["t1"][tc], cols["t0"][tc]))
                        rv(lambda e, tc=tc: e.tensor_scalar(
                            cols["f"][tc], cols["t1"][tc], 127.0, None,
                            ALU.mult))
                        rv(lambda e, tc=tc: e.tensor_tensor(
                            cols["t0"][tc], cols["cu"][tc], cols["mt"][tc],
                            ALU.mult))
                        rv(lambda e, tc=tc: e.tensor_scalar(
                            cols["cd"][tc], cols["t0"][tc], gww[:, 2:3], None,
                            ALU.mult), inc=s_row2)
                        v_row2_t = s_row2.n
                        wvq_h = [(s_vqtp[0], vq_hrd[0]) if vq_hrd[0] else None,
                                 (s_vqtp[0], vq_hrd[1]) if vq_hrd[1] else None]
                        for q8 in range(8):
                            pt2 = s_t2.n % 2
                            P.op("scalar", lambda e, ti=ti, t=t, q8=q8,
                                 pt2=pt2:
                                 e.activation(
                                     t2[:, pt2, :],
                                     vbuf[:, ti, q8 * 512:(q8 + 1) * 512],
                                     AF.Identity,
                                     scale=cols["f"][:, t:t + 1]),
                                 waits=[(s_row2, v_row2_t), (smem, v_mem),
                                        (s_vq8, s_t2.n - 1)
                                        if s_t2.n >= 2 else None,
                                        wvq_h[q8 // 4]
                                        if q8 % 4 == 0 else None],
                                 inc=s_t2)
                            v_t2 = s_t2.n
                            P.op("vector", lambda e, q8=q8, pt2=pt2:
                                 e.tensor_scalar(
                                     vq[:, 0, q8 * 512:(q8 + 1) * 512],
                                     t2[:, pt2, :], MAGIC, MAGIC,
                                     ALU.add, ALU.subtract),
                                 waits=[(s_t2, v_t2)], inc=s_vq8)
                            if q8 == 3 or q8 == 7:
                                hh = q8 // 4
                                v_vq8 = s_vq8.n
                                wch = (s_vqtp[0], s_vqtp[0].n) \
                                    if s_vqtp[0].n else None
                                P.op("sync", lambda e, ti=ti, hh=hh:
                                     e.dma_start_transpose(
                                         vqT[:, ti, 16 * hh:16 * hh + 16, :],
                                         vq[:, 0, 2048 * hh:2048 * hh + 2048]
                                         .rearrange("p (c f) -> p c f",
                                                    f=128)),
                                     waits=[(s_vq8, v_vq8), wch],
                                     inc=s_vqtp[0], dma=True)
                                vq_hrd[hh] = s_vqtp[0].n
                        vqt_done[ti] = (s_vqtp[0], s_vqtp[0].n)

                    qg = (app == 1 and (gi == 0 or DBG_QUANT_ALL))
                    qdone = {}
                    qld = {}

                    def q_src(oc2, ci):
                        wi2, hh2 = ci // 2, ci % 2
                        wv = (wgT_q, wuT_q)[wi2]
                        scr = (scr_g, scr_u)[wi2]
                        return (wv[:, 4 * hh2:4 * hh2 + 4, oc2, :],
                                wgup[:, oc2 % 2, wi2,
                                     4 * hh2:4 * hh2 + 4, :],
                                scr[oc2][:, 4 * hh2:4 * hh2 + 4, :], wi2)

                    def q_drip_load(oc2, ci):
                        srcv, dst, scrv, wi2 = q_src(oc2, ci)
                        qld[(oc2, ci)] = (q_load(srcv), dst, scrv, wi2)

                    def q_drip_comp(oc2, ci, wclob2):
                        ld, dst, scrv, wi2 = qld.pop((oc2, ci))
                        qdone[oc2] = q_compute(ld, dst, scrv, wi2,
                                               extra_w=wclob2)
                    if qg:
                        for ci in range(4):
                            q_drip_load(0, ci)
                            q_drip_comp(0, ci, None)

                    for oc in range(8):
                        par = oc % 2
                        wclob = (s_gup, gup_hist[-2]) if len(gup_hist) >= 2 \
                            else None
                        if qg:
                            wq_wait = [(sqdve, qdone[oc])]
                        else:
                            P.op("gpsimd", lambda e, oc=oc, par=par:
                                 e.dma_start(
                                     wgup[:, par, 0, :, :], scr_g[oc]),
                                 waits=[wclob, (sscrw, sscrw.n)],
                                 inc=s_wqlp[par], dma=True)
                            P.op("gpsimd", lambda e, oc=oc, par=par:
                                 e.dma_start(
                                     wgup[:, par, 1, :, :], scr_u[oc]),
                                 inc=s_wqlp[par], dma=True)
                            wq_wait = [(s_wqlp[par], s_wqlp[par].n)]
                        for ti, t in enumerate(grp):
                            pp = s_gup.n % 3
                            wg_, wu_ = pacq(2 * pp), pacq(2 * pp + 1)
                            for icc in range(8):
                                wts = (wq_wait +
                                       [(s_uT, uT_base + 16 * (ti + 1)),
                                        wg_, wu_] if icc == 0 else [])
                                P.op("tensor", lambda e, icc=icc, ti=ti,
                                     par=par, pp=pp:
                                     e.matmul(ppool[:, 2 * pp, :],
                                              uT[:, ti, icc, :],
                                              wgup[:, par, 0, icc, :],
                                              start=(icc == 0), stop=(icc == 7)),
                                     waits=wts)
                                P.op("tensor", lambda e, icc=icc, ti=ti,
                                     par=par, pp=pp:
                                     e.matmul(ppool[:, 2 * pp + 1, :],
                                              uT[:, ti, icc, :],
                                              wgup[:, par, 1, icc, :],
                                              start=(icc == 0), stop=(icc == 7)),
                                     inc=(s_gup if icc == 7 else None))
                            v_gup = s_gup.n
                            sp_ = s_silu.n % 2
                            P.op("scalar", lambda e, t=t, pp=pp, sp_=sp_:
                                 e.activation(silub[:, sp_, :],
                                              ppool[:, 2 * pp, :], AF.Silu,
                                              scale=cols["cg"][:, t:t + 1]),
                                 waits=[(s_gup, v_gup),
                                        (s_vw, s_vw.n - 1)
                                        if s_silu.n >= 2 else None],
                                 inc=s_silu)
                            v_silu = s_silu.n
                            P.op("vector", lambda e, ti=ti, oc=oc, pp=pp,
                                 sp_=sp_:
                                 e.tensor_tensor(
                                     vbuf[:, ti, oc * 512:(oc + 1) * 512],
                                     silub[:, sp_, :], ppool[:, 2 * pp + 1, :],
                                     ALU.mult),
                                 waits=[(s_silu, v_silu),
                                        (s_dmg, v_dmm_prev)],
                                 inc=[s_vw, psl[2 * pp], psl[2 * pp + 1]])
                            P.op("vector", lambda e, ti=ti, oc=oc:
                                 e.reduce_max(
                                     vmax[:, ti, oc:oc + 1],
                                     vbuf[:, ti, oc * 512:(oc + 1) * 512],
                                     AX.X, apply_absolute_value=True),
                                 waits=[(s_vw, s_vw.n)], inc=s_vtt)
                            if oc == 7:
                                vtt_tile[ti] = s_vtt.n
                            if ngen is not None and oc >= 1:
                                for _ in range(1 if oc < 5 else 2):
                                    if next(ngen, "END") == "END":
                                        ngen = None
                                        break
                            if qg and oc < 7:
                                if ti < 4:
                                    q_drip_load(oc + 1, ti)
                                if 1 <= ti <= 4:
                                    q_drip_comp(oc + 1, ti - 1,
                                                (s_gup, gup_hist[-1])
                                                if gup_hist else None)
                        gup_hist.append(s_gup.n)


                    for ti, t in enumerate(grp):
                        emit_vchain(ti, t)

                    preload = {}
                    if ngen is not None:
                        for _ in ngen:
                            pass
                        ngen = None
                    if "pend" in nhs:
                        head[gi + 1] = emit_utp(groups[gi + 1], nhs["pend"])

                    dmg_tile = {}
                    dwn_tile = {}
                    qdone_d = {}
                    dq_pend = []

                    def dq_enq(hc2, icg2):
                        pard2 = (hc2 * 4 + icg2) % 2
                        for hh in range(2):
                            def ld(hc2=hc2, icg2=icg2, hh=hh):
                                qld[("d", hc2, icg2, hh)] = q_load(
                                    wdT_q[:, icg2 * 8 + 4 * hh:
                                          icg2 * 8 + 4 * hh + 4,
                                          hc2 * 512:(hc2 + 1) * 512])
                            def cp(hc2=hc2, icg2=icg2, hh=hh, pard2=pard2):
                                wclob2 = (s_dmg, pard_done[pard2]) \
                                    if pard_done[pard2] else None
                                vq2 = q_compute(
                                    qld.pop(("d", hc2, icg2, hh)),
                                    wdb[:, pard2, 4 * hh:4 * hh + 4, :],
                                    scr_d[hc2, icg2]
                                    [:, 4 * hh:4 * hh + 4, :],
                                    2, extra_w=wclob2)
                                if hh == 1:
                                    qdone_d[(hc2, icg2)] = vq2
                            dq_pend.append(ld)
                            dq_pend.append(cp)

                    if qg:
                        dq_enq(0, 0)
                        dq_enq(0, 1)
                        while dq_pend:
                            dq_pend.pop(0)()
                    dq_sched = {(0, 3): (0, 2), (0, 4): (0, 3),
                                (0, 5): (1, 0), (1, 0): (1, 1),
                                (1, 1): (1, 2), (1, 2): (1, 3)}
                    for hc in range(2):
                        if hc == 0 and G > 4:
                            sched = [(0, list(range(0, 4))),
                                     (1, list(range(0, 4))),
                                     (0, list(range(4, G))),
                                     (1, list(range(4, G))),
                                     (2, list(range(G))),
                                     (3, list(range(G)))]
                        else:
                            sched = [(icg, list(range(G)))
                                     for icg in range(4)]
                        loaded = {}
                        done_cnt = {k: 0 for k in range(4)}
                        for sj, (icg, tis) in enumerate(sched):
                            if not tis:
                                continue
                            if qg and (hc, sj) in dq_sched:
                                dq_enq(*dq_sched[(hc, sj)])
                            pard = (hc * 4 + icg) % 2
                            first_chunk = icg not in loaded
                            if first_chunk:
                                wdclob = (s_dmg, pard_done[pard]) \
                                    if pard_done[pard] else None
                                if (hc, icg) in preload:
                                    loaded[icg] = preload[(hc, icg)]
                                elif qg:
                                    while dq_pend and \
                                            (hc, icg) not in qdone_d:
                                        dq_pend.pop(0)()
                                    loaded[icg] = \
                                        [(sqdve, qdone_d[(hc, icg)])]
                                else:
                                    P.op("gpsimd", lambda e, hc=hc, icg=icg,
                                         pard=pard: e.dma_start(
                                             wdb[:, pard, :, :],
                                             scr_d[hc, icg]),
                                         waits=[wdclob, (sscrw, sscrw.n)],
                                         inc=s_wdp[pard], dma=True)
                                    loaded[icg] = \
                                        [(s_wdp[pard], s_wdp[pard].n)]
                            for ti in tis:
                                for icc in range(8):
                                    wts = []
                                    if icc == 0:
                                        if ti == tis[0] and first_chunk:
                                            wts += loaded[icg]
                                        if icg == 0 and hc == 0:
                                            wts += [vqt_done[ti], pacq(ti)]
                                        if icg == 0 and hc == 1:
                                            wts.append(
                                                (s_dwn, dwn_tile[ti]))
                                    st = (icg == 0 and icc == 0)
                                    sp2 = (icg == 3 and icc == 7)
                                    incs = [s_dmg] if icc == 7 else None
                                    P.op("tensor", lambda e, icc=icc, ti=ti,
                                         icg=icg, pard=pard, st=st, sp2=sp2:
                                         e.matmul(ppool[:, ti, :],
                                                  vqT[:, ti, icg * 8 + icc, :],
                                                  wdb[:, pard, icc, :],
                                                  start=st, stop=sp2,
                                                  skip_group_check=True),
                                         waits=wts, inc=incs)
                                if icg == 3:
                                    dmg_tile[ti] = s_dmg.n
                                if qg and dq_pend:
                                    dq_pend.pop(0)()
                            done_cnt[icg] += len(tis)
                            if done_cnt[icg] == G:
                                pard_done[pard] = s_dmg.n
                        dmg_hist.append(s_dmg.n)
                        for ti, t in enumerate(grp):
                            last = (hc == 1)
                            if app == 1:
                                P.op("vector", lambda e, ti=ti, t=t, hc=hc:
                                     e.scalar_tensor_tensor(
                                         xs[:, t, hc * 512:(hc + 1) * 512],
                                         ppool[:, ti, :],
                                         cols["cd"][:, t:t + 1],
                                         xs[:, t, hc * 512:(hc + 1) * 512],
                                         ALU.mult, ALU.add),
                                     waits=[(s_dmg, dmg_tile[ti])],
                                     inc=([s_dwn, psl[ti]] if last
                                          else [s_dwn]))
                            else:
                                gidx = gi * GA + ti
                                slot = gidx % 3
                                strm = (gidx % 3) + 3 * hc
                                wout = (s_outp[strm],
                                        out_store.get((gidx - 3, hc), 0)) \
                                    if gidx >= 3 else None
                                P.op("vector", lambda e, ti=ti, t=t, hc=hc,
                                     slot=slot: e.scalar_tensor_tensor(
                                         bufs["outf"][:, slot,
                                                      hc * 512:(hc + 1) * 512],
                                         ppool[:, ti, :],
                                         cols["cd"][:, t:t + 1],
                                         xs[:, t, hc * 512:(hc + 1) * 512],
                                         ALU.mult, ALU.add),
                                     waits=[(s_dmg, dmg_tile[ti]), wout],
                                     inc=([s_dwn, psl[ti]] if last
                                          else [s_dwn]))
                                out_t = out.rearrange("(t p) h -> t p h",
                                                      p=128)
                                P.op("sync", lambda e, t=t, slot=slot, hc=hc:
                                     e.dma_start(
                                         out_t[t - 2][:, hc * 512:(hc + 1) * 512],
                                         bufs["outf"][:, slot,
                                                      hc * 512:(hc + 1) * 512]),
                                     waits=[(s_dwn, s_dwn.n)],
                                     inc=s_outp[strm], dma=True)
                                out_store[(gidx, hc)] = s_outp[strm].n
                            if hc == 0:
                                dwn_tile[ti] = s_dwn.n

            ffn_app(1, list(range(NT)), lambda t: [(sx, 16 * NT)], bufs)

            sbar = S("sbar")

            def barrier():
                base = sbar.n
                for e in Prog.ENGS:
                    P.op(e, lambda x, e=e: P.carrier(e, x), inc=sbar)
                for e in Prog.ENGS:
                    P.op(e, lambda x: x.nop(), waits=[(sbar, base + 5)])

            if debug_taps:
                barrier()
                dxm = taps_out["dbg_xmid"].rearrange("(t p) h -> t p h", p=128)
                sdbg = S("sdbg")
                for t in range(NT):
                    P.op("gpsimd", lambda e, t=t: e.dma_start(dxm[t], xs[:, t, :]),
                         inc=sdbg, dma=True)
                P.op("gpsimd", lambda e: e.nop(), waits=[(sdbg, 16 * NT)])
            barrier()

        with ExitStack() as bctx:
            zf = sb(bctx, "zf", [128, ZR, CH], BF16)
            winT = sb(bctx, "winT", [128, 8, CH], BF16)
            woutT = sb(bctx, "woutT", [128, 16, H], BF16)
            tabs = sb(bctx, "tabs", [128, 5, CH], BF16)
            Csb = sb(bctx, "Csb", [128, 4, 4, 128], BF16)
            iDsb = sb(bctx, "iDsb", [128, 4, 2, 128], BF16)
            bjnk = sb(bctx, "bjnk", [128, H], BF16)
            brows = sb(bctx, "brows", [128, CH], BF16)
            binb = brows[0:1, :]
            bab = brows[32:33, :]
            boutb = brows[64:65, 0:H]
            u2tm = sb(bctx, "u2tm", [128, 1, H], BF16)
            u2T = sb(bctx, "u2T", [128, 2, 8, 128], BF16)

            s_pm = S("s_pm")
            u2Tp = [S("u2Tp0"), S("u2Tp1")]
            sB = {k: S("b_" + k) for k in
                  ("cload", "wint", "woutt", "tapt", "tab", "tabev", "zrow",
                   "zsqrt", "u2", "u2t", "u2T", "zmm", "zev", "fwd", "fev",
                   "pch", "idft", "sig", "gat", "catp", "catev", "omm", "oev",
                   "stat")}

            P.op("gpsimd", lambda e: e.dma_start(Csb[:, :, :, :], Cd),
                 inc=sB["cload"], dma=True)
            P.op("gpsimd", lambda e: e.dma_start(iDsb[:, :, :, :], iDd),
                 inc=sB["cload"], dma=True)
            P.op("gpsimd", lambda e: e.dma_start(binb[:, :], b_in),
                 inc=sB["cload"], dma=True)
            P.op("gpsimd", lambda e: e.dma_start(bab[:, 0:H], b_a),
                 inc=sB["cload"], dma=True)
            P.op("gpsimd", lambda e: e.dma_start(bab[:, H:CH], b_b),
                 inc=sB["cload"], dma=True)
            P.op("gpsimd", lambda e: e.dma_start(boutb[:, :], b_out),
                 inc=sB["cload"], dma=True)
            v_cload = sB["cload"].n

            with ExitStack() as wctx:
                wstg = sb(wctx, "wstg", [128, 2, 4096], BF16)
                tapb = sb(wctx, "tapb", [128, 16, 256], BF16)
                tapT = sb(wctx, "tapT", [128, 2, CH], BF16)
                D2sb = sb(wctx, "D2sb", [128, 2, 5, 128], BF16)
                sldp = [S("b_sld0"), S("b_sld1")]
                sD2 = S("b_sD2")
                staps = S("b_staps")
                wintp = [S("b_wint0"), S("b_wint1")]
                wouttp = [S("b_woutt0"), S("b_woutt1")]
                P.op("gpsimd", lambda e: e.dma_start(D2sb[:, :, :, :], D2d),
                     inc=sD2, dma=True)
                wa_r = wa.rearrange("(g p) k -> p g k", p=128)
                wb_r = wb.rearrange("(g p) k -> p g k", p=128)
                P.op("gpsimd", lambda e: e.dma_start(tapb[:, 0:8, :], wa_r),
                     inc=staps, dma=True)
                P.op("gpsimd", lambda e: e.dma_start(tapb[:, 8:16, :], wb_r),
                     inc=staps, dma=True)
                v_tapl = staps.n
                for g in range(16):
                    P.op("sync", lambda e, g=g: e.dma_start_transpose(
                        tapT[:, :, g * 128:(g + 1) * 128],
                        tapb[:, g, :].rearrange("p (j f) -> p j f", f=128)),
                        waits=[(staps, v_tapl)], inc=sB["tapt"], dma=True)
                v_tapt = sB["tapt"].n
                for tb in range(5):
                    for fc in range(4):
                        slw = pacq(6)
                        for jc in range(2):
                            P.op("tensor", lambda e, tb=tb, fc=fc, jc=jc:
                                 e.matmul(ppool[:, 6, :],
                                          D2sb[:, jc, tb, :],
                                          tapT[:, jc, fc * 512:(fc + 1) * 512],
                                          start=(jc == 0), stop=(jc == 1)),
                                 waits=([(sB["tapt"], v_tapt),
                                         (sD2, 16), slw]
                                        if jc == 0 else []),
                                 inc=(sB["tab"] if jc == 1 else None))
                        v_tab = sB["tab"].n
                        P.op("vector", lambda e, tb=tb, fc=fc: e.tensor_copy(
                            tabs[:, tb, fc * 512:(fc + 1) * 512],
                            ppool[:, 6, :]),
                            waits=[(sB["tab"], v_tab)],
                            inc=[psl[6], sB["tabev"]])
                v_tabev = sB["tabev"].n
                w_in_r = w_in.rearrange("(g p) h -> p g h", p=128)
                w_out_r = w_out.rearrange("(g p) c -> p g c", p=128)
                wstg_v = [wstg[:, 0, :].rearrange("p (g h) -> p g h", h=H),
                          wstg[:, 1, :].rearrange("p (g h) -> p g h", h=H)]
                tpar_hist = {0: [], 1: []}
                win_cc = []
                for pc in range(8):
                    par = pc % 2
                    w = (wintp[par], tpar_hist[par][-1]) \
                        if tpar_hist[par] else None
                    P.op("gpsimd", lambda e, pc=pc, par=par: e.dma_start(
                        wstg_v[par][:, 0:2, :],
                        w_in_r[:, 2 * pc:2 * pc + 2, :]),
                        waits=[w], inc=sldp[par], dma=True)
                    v_ld = sldp[par].n
                    for g in range(2):
                        cc = 2 * pc + g
                        P.op("sync", lambda e, par=par, cc=cc, g=g:
                             e.dma_start_transpose(
                                 winT[:, :, cc * 128:(cc + 1) * 128],
                                 wstg[:, par, g * H:(g + 1) * H]
                                 .rearrange("p (i f) -> p i f", f=128)),
                             waits=[(sldp[par], v_ld)],
                             inc=wintp[par], dma=True)
                    tpar_hist[par].append(wintp[par].n)
                    if pc % 2 == 1:
                        win_cc.append((wintp[par], wintp[par].n))
                opar_hist = {0: [], 1: []}
                for pc in range(4):
                    par = pc % 2
                    w1 = (wintp[par], tpar_hist[par][-1])
                    w2 = (wouttp[par], opar_hist[par][-1]) \
                        if opar_hist[par] else None
                    P.op("gpsimd", lambda e, pc=pc, par=par: e.dma_start(
                        wstg[:, par, :].rearrange("p (g c) -> p g c", c=CH),
                        w_out_r[:, 2 * pc:2 * pc + 2, :]),
                        waits=[w1, w2], inc=sldp[par], dma=True)
                    v_ld = sldp[par].n
                    for g in range(2):
                        hc = 2 * pc + g
                        P.op("sync", lambda e, par=par, hc=hc, g=g:
                             e.dma_start_transpose(
                                 woutT[:, :, hc * 128:(hc + 1) * 128],
                                 wstg[:, par, g * CH:(g + 1) * CH]
                                 .rearrange("p (i f) -> p i f", f=128)),
                             waits=[(sldp[par], v_ld)],
                             inc=wouttp[par], dma=True)
                    opar_hist[par].append(wouttp[par].n)
                win_waits = [(wintp[0], wintp[0].n), (wintp[1], wintp[1].n)]
                wout_waits = [(wouttp[0], wouttp[0].n),
                              (wouttp[1], wouttp[1].n)]
                s_bdrain = S("s_bdrain")
                P.op("sync", lambda e: e.nop(),
                     waits=[(sB["tapt"], sB["tapt"].n),
                            (wintp[0], wintp[0].n), (wintp[1], wintp[1].n),
                            (wouttp[0], wouttp[0].n),
                            (wouttp[1], wouttp[1].n)], inc=s_bdrain)
                P.op("tensor", lambda e: e.nop(),
                     waits=[(sB["tabev"], sB["tabev"].n)], inc=s_bdrain)
                s_wprep = S("s_wprep")
                P.op("sync", lambda e: e.nop(),
                     waits=[(sB["tapt"], sB["tapt"].n),
                            (wintp[0], wintp[0].n), (wintp[1], wintp[1].n),
                            (wouttp[0], wouttp[0].n), (wouttp[1], wouttp[1].n),
                            (sldp[0], sldp[0].n), (sldp[1], sldp[1].n),
                            (sD2, sD2.n), (staps, staps.n)], inc=s_wprep)

            if True:
                with ExitStack() as pctx:
                    zh = sb(pctx, "zh", [128, 4, CH], BF16)
                    Pb = sb(pctx, "Pb", [128, 4, CH], BF16)
                    ptmp = sb(pctx, "ptmp", [128, CH], BF16)
                    sigb = sb(pctx, "sigb", [128, 2, 512], BF16)
                    gat = sb(pctx, "gat", [128, CH], BF16)
                    catT = sb(pctx, "catT", [128, 16, 128], BF16)

                    fwd_done_after_block = []
                    idft_done_after_block = []
                    zmm_hist = []
                    head_tp = {}
                    pch_hist = []
                    pairctr = [0]
                    u2tp_last = [None]

                    def tile_head(t):
                        pf = sB["u2"].n % 2
                        tslc = (slice(None), slice(t, t + 1))
                        P.op("scalar", lambda e, t=t, tslc=tslc:
                             e.activation(
                                 bjnk[:, :], xs[:, t, :], AF.Square,
                                 accum_out=cols["s2"][tslc]),
                             waits=[(sB["stat"], sB["stat"].n)],
                             inc=sB["stat"])
                        def rz(fn, waits=(), inc=None):
                            ws = list(waits) + [(sB["zrow"], sB["zrow"].n)]
                            ics = [sB["zrow"]] + ([] if inc is None else [inc])
                            P.op("vector", fn, waits=ws, inc=ics)
                        rz(lambda e, t=t, tslc=tslc: e.reduce_sum(
                            cols["s1"][tslc], xs[:, t, :], AX.X))
                        rz(lambda e, tslc=tslc: e.tensor_scalar(
                            cols["m"][tslc], cols["s1"][tslc], 1.0 / H, None,
                            ALU.mult))
                        rz(lambda e, tslc=tslc: e.tensor_scalar(
                            cols["ve"][tslc], cols["s2"][tslc], 1.0 / H, None,
                            ALU.mult), waits=[(sB["stat"], sB["stat"].n)])
                        rz(lambda e, tslc=tslc: e.tensor_tensor(
                            cols["t0"][tslc], cols["m"][tslc], cols["m"][tslc],
                            ALU.mult))
                        rz(lambda e, tslc=tslc: e.tensor_scalar(
                            cols["t0"][tslc], cols["t0"][tslc], -1.0, 1e-5,
                            ALU.mult, ALU.add))
                        rz(lambda e, tslc=tslc: e.tensor_tensor(
                            cols["ve"][tslc], cols["ve"][tslc], cols["t0"][tslc],
                            ALU.add))
                        rz(lambda e, tslc=tslc: e.reciprocal(
                            cols["rq"][tslc], cols["ve"][tslc]))
                        v_zr = sB["zrow"].n
                        P.op("scalar", lambda e, tslc=tslc: e.activation(
                            cols["r"][tslc], cols["rq"][tslc], AF.Sqrt),
                            waits=[(sB["zrow"], v_zr)], inc=sB["zsqrt"])
                        v_zs = sB["zsqrt"].n
                        rz(lambda e, tslc=tslc: e.tensor_tensor(
                            cols["t0"][tslc], cols["m"][tslc], cols["r"][tslc],
                            ALU.mult), waits=[(sB["zsqrt"], v_zs)])
                        rz(lambda e, tslc=tslc: e.tensor_scalar(
                            cols["nb"][tslc], cols["t0"][tslc], -1.0, None,
                            ALU.mult))
                        v_zr = sB["zrow"].n
                        P.op("scalar", lambda e, t=t: e.activation(
                            u2tm[:, 0, :], xs[:, t, :], AF.Identity,
                            bias=cols["nb"][tslc], scale=cols["r"][tslc]),
                            waits=[(sB["zrow"], v_zr), u2tp_last[0]],
                            inc=sB["u2"])
                        v_u2 = sB["u2"].n
                        P.op("sync", lambda e, pf=pf: e.dma_start_transpose(
                            u2T[:, pf, :, :],
                            u2tm[:, 0, :].rearrange("p (c f) -> p c f",
                                                    f=128)),
                            waits=[(sB["u2"], v_u2),
                                   (sB["zmm"], zmm_hist[-1])
                                   if len(zmm_hist) >= 1 else None],
                            inc=u2Tp[pf], dma=True)
                        u2tp_last[0] = (u2Tp[pf], u2Tp[pf].n)
                        head_tp[t] = (u2Tp[pf].n, pf)

                    def tile_zproj(t):
                        v_u2T, pf = head_tp[t]
                        for cc in range(4):
                            sidx = sB["zmm"].n % 2
                            zw = pacq(sidx)
                            for icc in range(8):
                                wts = []
                                if icc == 0:
                                    wts = [(u2Tp[pf], v_u2T),
                                           (sB["cload"], v_cload), zw,
                                           win_cc[cc]]
                                P.op("tensor", lambda e, icc=icc, cc=cc,
                                     sidx=sidx, pf=pf:
                                     e.matmul(ppool[:, sidx, :],
                                              u2T[:, pf, icc, :],
                                              winT[:, icc,
                                                   cc * 512:(cc + 1) * 512],
                                              start=(icc == 0), stop=False),
                                     waits=wts)
                            P.op("tensor", lambda e, cc=cc, sidx=sidx:
                                 e.matmul(ppool[:, sidx, :], ones1[0:1, :],
                                          binb[:, cc * 512:(cc + 1) * 512],
                                          start=False, stop=True),
                                 waits=[(smem, v_mem)], inc=sB["zmm"])
                            v_zmm = sB["zmm"].n
                            slot = t % ZR
                            if t < 2 and not DBG_NOMASK:
                                P.op("vector", lambda e, slot=slot, cc=cc,
                                     sidx=sidx: e.tensor_scalar(
                                         zf[:, slot, cc * 512:(cc + 1) * 512],
                                         ppool[:, sidx, :], zmk[:, :], None,
                                         ALU.mult),
                                     waits=[(sB["zmm"], v_zmm),
                                            (szmk, 16)],
                                     inc=[psl[sidx], sB["zev"]])
                            else:
                                P.op("scalar", lambda e, slot=slot, cc=cc,
                                     sidx=sidx: e.activation(
                                         zf[:, slot, cc * 512:(cc + 1) * 512],
                                         ppool[:, sidx, :], AF.Identity),
                                     waits=[(sB["zmm"], v_zmm)],
                                     inc=[psl[sidx], sB["zev"]])
                        zmm_hist.append(sB["zmm"].n)

                    def emit_block_gen(b):
                        bd = (s_bdrain, 2) if b == 0 else None
                        v_zev_need = 4 * (2 * b + 4)
                        pair_specs = [
                            ((0, 2), [(0, 0, 0, 1, 2, ALU.subtract),
                                      (2, 1, 4, 0, 2, ALU.add)]),
                            ((1, 3), [(1, 0, 1, 1, 3, ALU.subtract),
                                      (3, 1, 1, 0, 3, ALU.add)]),
                        ]
                        pdata = []
                        for pk, ((poA, poB), steps) in enumerate(pair_specs):
                            zbase = (pairctr[0] % 2) * 2
                            pairctr[0] += 1
                            wzh = (sB["pch"], pch_hist[-2 + pk]) \
                                if len(pch_hist) >= 2 else None
                            for zi, po in enumerate((poA, poB)):
                                for fc in range(4):
                                    sidx = 2 + sB["fwd"].n % 2
                                    fw = pacq(sidx)
                                    for icc in range(4):
                                        wts = []
                                        if icc == 0:
                                            wts = [(sB["zev"], v_zev_need),
                                                   (sB["cload"], v_cload), fw]
                                        P.op("tensor", lambda e, icc=icc,
                                             po=po, fc=fc, b=b, sidx=sidx:
                                             e.matmul(
                                                 ppool[:, sidx, :],
                                                 Csb[:, icc, po, :],
                                                 zf[:, (2 * b + icc) % ZR,
                                                    fc * 512:(fc + 1) * 512],
                                                 start=(icc == 0),
                                                 stop=(icc == 3)),
                                             waits=wts,
                                             inc=(sB["fwd"] if icc == 3
                                                  else None))
                                    v_fwd = sB["fwd"].n
                                    P.op("scalar", lambda e, zi=zi, fc=fc,
                                         sidx=sidx, zbase=zbase:
                                         e.activation(
                                             zh[:, zbase + zi,
                                                fc * 512:(fc + 1) * 512],
                                             ppool[:, sidx, :], AF.Identity),
                                         waits=[(sB["fwd"], v_fwd), bd,
                                                wzh if fc == 0 and zi == 0
                                                else None],
                                         inc=[psl[sidx], sB["fev"]])
                            pdata.append((zbase, steps, sB["fev"].n))
                        yield
                        v_idft_prev = idft_done_after_block[-1] \
                            if idft_done_after_block else 0
                        for zbase, steps, v_fev in pdata:
                            for (pidx, za, ta, zb2, tb2, opf) in steps:
                                P.op("vector", lambda e, pidx=pidx, za=za,
                                     ta=ta, zbase=zbase: e.tensor_tensor(
                                         Pb[:, pidx, :],
                                         zh[:, zbase + za, :],
                                         tabs[:, ta, :], ALU.mult),
                                     waits=[(sB["fev"], v_fev), bd,
                                            (sB["tabev"], v_tabev),
                                            (sB["idft"], v_idft_prev),
                                            (s_pm, s_pm.n)], inc=s_pm)
                                P.op("vector", lambda e, zb2=zb2, tb2=tb2,
                                     zbase=zbase:
                                     e.tensor_tensor(ptmp[:, :],
                                                     zh[:, zbase + zb2, :],
                                                     tabs[:, tb2, :],
                                                     ALU.mult),
                                     waits=[(s_pm, s_pm.n)], inc=s_pm)
                                P.op("vector", lambda e, pidx=pidx, opf=opf:
                                     e.tensor_tensor(Pb[:, pidx, :],
                                                     Pb[:, pidx, :],
                                                     ptmp[:, :], opf),
                                     waits=[(s_pm, s_pm.n)],
                                     inc=[s_pm, sB["pch"]])
                            pch_hist.append(sB["pch"].n)
                        v_pch = sB["pch"].n
                        yield

                        for po in range(2):
                            tgt = 2 + 2 * b + po
                            for pc in range(2):
                                wA, wB2 = pacq(4), pacq(5)
                                for half, (sidx, fc) in enumerate(
                                        ((4, pc), (5, pc + 2))):
                                    for icc in range(4):
                                        wts = []
                                        if icc == 0:
                                            wts = [(sB["pch"], v_pch),
                                                   wA if half == 0 else wB2]
                                        P.op("tensor", lambda e, icc=icc,
                                             po=po, fc=fc, sidx=sidx:
                                             e.matmul(
                                                 ppool[:, sidx, :],
                                                 iDsb[:, icc, po, :],
                                                 Pb[:, icc,
                                                    fc * 512:(fc + 1) * 512],
                                                 start=(icc == 0), stop=False),
                                             waits=wts)
                                    P.op("tensor", lambda e, fc=fc, sidx=sidx:
                                         e.matmul(ppool[:, sidx, :],
                                                  ones1[32:33, :],
                                                  bab[:, fc * 512:
                                                      (fc + 1) * 512],
                                                  start=False, stop=True),
                                         inc=sB["idft"])
                                v_idft = sB["idft"].n
                                P.op("scalar", lambda e: e.activation(
                                    sigb[:, 0, :], ppool[:, 4, :], AF.Sigmoid),
                                    waits=[(sB["idft"], v_idft), bd],
                                    inc=sB["sig"])
                                P.op("scalar", lambda e: e.activation(
                                    sigb[:, 1, :], ppool[:, 5, :], AF.Sigmoid),
                                    inc=sB["sig"])
                                v_sig = sB["sig"].n
                                P.op("vector", lambda e, pc=pc: e.tensor_tensor(
                                    gat[:, pc * 512:(pc + 1) * 512],
                                    ppool[:, 4, :], sigb[:, 1, :], ALU.mult),
                                    waits=[(sB["sig"], v_sig), bd,
                                           (sB["catev"], sB["catev"].n)
                                           if pc == 0 else None])
                                P.op("vector", lambda e, pc=pc: e.tensor_tensor(
                                    gat[:, (pc + 2) * 512:(pc + 3) * 512],
                                    ppool[:, 5, :], sigb[:, 0, :], ALU.mult),
                                    inc=[sB["gat"], psl[4], psl[5]])
                            v_gat = sB["gat"].n
                            P.op("sync", lambda e: e.dma_start_transpose(
                                catT[:, :, :],
                                gat[:, :].rearrange("p (c f) -> p c f",
                                                    f=128)),
                                waits=[(sB["gat"], v_gat), bd,
                                       (sB["omm"], sB["omm"].n)],
                                inc=sB["catev"], dma=True)
                            v_catev = sB["catev"].n
                            yield
                            for hc in range(2):
                                ow = pacq(6)
                                for ec in range(16):
                                    wts = []
                                    if ec == 0:
                                        wts = [(sB["catev"], v_catev),
                                               ow] + wout_waits
                                    P.op("tensor", lambda e, ec=ec, hc=hc:
                                         e.matmul(
                                             ppool[:, 6, :],
                                             catT[:, ec, :],
                                             woutT[:, ec,
                                                   hc * 512:(hc + 1) * 512],
                                             start=(ec == 0), stop=False),
                                         waits=wts)
                                P.op("tensor", lambda e, hc=hc: e.matmul(
                                    ppool[:, 6, :], ones1[64:65, :],
                                    boutb[:, hc * 512:(hc + 1) * 512],
                                    start=False, stop=True),
                                    inc=sB["omm"])
                                v_omm = sB["omm"].n
                                P.op("vector", lambda e, tgt=tgt, hc=hc:
                                     e.tensor_tensor(
                                         xs[:, tgt, hc * 512:(hc + 1) * 512],
                                         ppool[:, 6, :],
                                         xs[:, tgt, hc * 512:(hc + 1) * 512],
                                         ALU.add),
                                     waits=[(sB["omm"], v_omm)],
                                     inc=[psl[6], sB["oev"]])
                        idft_done_after_block.append(sB["idft"].n)

                    def do_zproj(t):
                        if t + 1 < NT and (t + 1) not in head_tp:
                            tile_head(t + 1)
                        tile_zproj(t)

                    tile_head(0)
                    for t in range(4):
                        do_zproj(t)
                    NB = (NT - 2) // 2
                    gens = {}
                    for b in range(NB):
                        gens[b] = emit_block_gen(b)
                        next(gens[b])
                        if b >= 1:
                            next(gens[b - 1])
                        if 2 * b + 4 < NT:
                            do_zproj(2 * b + 4)
                        if b >= 1:
                            next(gens[b - 1])
                        if 2 * b + 5 < NT:
                            do_zproj(2 * b + 5)
                        if b >= 1:
                            for _ in gens.pop(b - 1):
                                pass
                        next(gens[b])
                    next(gens[NB - 1])
                    for _ in gens.pop(NB - 1):
                        pass
                    if debug_taps:
                        barrier()
                        dz = taps_out["dbg_z"].rearrange("(t p) c -> t p c",
                                                         p=128)
                        sdbg2 = S("sdbg2")
                        for t in range(NT - ZR, NT):
                            P.op("gpsimd", lambda e, t=t: e.dma_start(
                                dz[t], zf[:, t % ZR, :]), inc=sdbg2, dma=True)
                        P.op("gpsimd", lambda e: e.nop(),
                             waits=[(sdbg2, 16 * ZR)])
                        dx3 = taps_out["dbg_x3"].rearrange(
                            "(t p) h -> t p h", p=128)
                        sdbg3 = S("sdbg3")
                        for t in range(NV):
                            P.op("gpsimd", lambda e, t=t: e.dma_start(
                                dx3[t], xs[:, t + 2, :]), inc=sdbg3, dma=True)
                        P.op("gpsimd", lambda e: e.nop(),
                             waits=[(sdbg3, 16 * NV)])
            barrier()

        with ExitStack() as cctx:
            bufs2 = dict(
                uT=sb(cctx, "uT2", [128, GA, 8, 128], BF16),
                u_f=sb(cctx, "u_f2", [128, 1, H], F32),
                u_tm=sb(cctx, "u_tm2", [128, GA, H], BF16),
                vbuf=sb(cctx, "vbuf2", [128, GA, INNER], BF16),
                t2=sb(cctx, "t22", [128, 2, 512], F32),
                vq=sb(cctx, "vq2", [128, 1, INNER], BF16),
                wgup=sb(cctx, "wgup2", [128, 2, 2, 8, 512], BF16),
                wdb=sb(cctx, "wdb2", [128, 2, 8, 512], BF16),
                silu=sb(cctx, "silu2", [128, 2, 512], F32),
                outf=sb(cctx, "outf", [128, 3, H], F32),
            )
            bufs2["vqT"] = nc.alloc_sbuf_tensor_at(
                "vqT2", [128, GA, 32, 128], BF16,
                offset=nc.lookup_mloc(bufs2["vbuf"]).addr)
            ffn_app(2, list(range(2, NT)), lambda t: [], bufs2)
            P.op("sync", lambda e: e.nop(),
                 waits=[(s, s.n) for s in s_outp])

        with nc.Block() as block:
            @block.sync
            def _(eng):
                P.emit("sync", eng)

            @block.gpsimd
            def _(eng):
                P.emit("gpsimd", eng)

            @block.tensor
            def _(eng):
                P.emit("tensor", eng)

            @block.vector
            def _(eng):
                P.emit("vector", eng)

            @block.scalar
            def _(eng):
                P.emit("scalar", eng)

    return nc


def kernel(L0_attn_x, L0_attn_w_in, L0_attn_b_in, L0_attn_w_a, L0_attn_b_a,
           L0_attn_w_b, L0_attn_b_b, L0_attn_w_out, L0_attn_b_out,
           L0_ffn_w_gate, L0_ffn_w_up, L0_ffn_w_down, _debug_taps=False,
           _cores=None):
    x = np.asarray(L0_attn_x, np.float32)
    B, T, Hd = x.shape
    xf = np.ascontiguousarray(x.reshape(B * T, Hd))

    Cd, iDd, D2d = _dft_consts()
    shared = dict(
        wgT=np.ascontiguousarray(np.asarray(L0_ffn_w_gate, np.float32).T),
        wuT=np.ascontiguousarray(np.asarray(L0_ffn_w_up, np.float32).T),
        wdT=np.ascontiguousarray(np.asarray(L0_ffn_w_down, np.float32).T),
        w_in=np.ascontiguousarray(np.asarray(L0_attn_w_in, np.float32)),
        w_out=np.ascontiguousarray(np.asarray(L0_attn_w_out, np.float32)),
        wa=np.ascontiguousarray(np.asarray(L0_attn_w_a, np.float32)[:, 0, :]),
        wb=np.ascontiguousarray(np.asarray(L0_attn_w_b, np.float32)[:, 0, :]),
        b_in=np.ascontiguousarray(np.asarray(L0_attn_b_in, np.float32)[None, :]),
        b_a=np.ascontiguousarray(np.asarray(L0_attn_b_a, np.float32)[None, :]),
        b_b=np.ascontiguousarray(np.asarray(L0_attn_b_b, np.float32)[None, :]),
        b_out=np.ascontiguousarray(np.asarray(L0_attn_b_out,
                                              np.float32)[None, :]),
        Cd=Cd, iDd=iDd, D2d=D2d,
        identd=np.eye(128, dtype=BFnp),
    )

    wgT_s = shared["wgT"].reshape(16, 128, 2048)
    wuT_s = shared["wuT"].reshape(16, 128, 2048)
    wdT_s = shared["wdT"].reshape(16, 128, 2048)

    nc = build_program(debug_taps=_debug_taps)
    cores = list(range(8)) if _cores is None else _cores
    in_maps = []
    for core in cores:
        tok0 = core * TPC
        seq_start = (core % 4 == 0)
        halo = (np.zeros((256, Hd), np.float32) if seq_start
                else xf[tok0 - 256: tok0])
        xhv = np.ascontiguousarray(
            np.concatenate([halo, xf[tok0:tok0 + TPC]], 0))
        zmv = np.full((128, 1), 0.0 if seq_start else 1.0, np.float32)
        wsclv = np.ascontiguousarray(np.stack(
            [wgT_s[2 * core:2 * core + 2],
             wuT_s[2 * core:2 * core + 2],
             wdT_s[2 * core:2 * core + 2]]))
        in_maps.append(dict(xh=xhv, zmask=zmv, wscl=wsclv, **shared))

    res = run_bass_kernel_spmd(nc, in_maps, core_ids=list(range(len(cores))))
    if _debug_taps:
        return res
    outs = [np.asarray(r["out"]) for r in res.results]
    return np.concatenate(outs, axis=0).reshape(B, T, Hd)
